# version 18
# baseline (speedup 1.0000x reference)
import numpy as np
import concourse.bass as bass
import concourse.mybir as mybir
from concourse.bass_utils import run_bass_kernel_spmd

F32 = mybir.dt.float32
BF16 = mybir.dt.bfloat16
I32 = mybir.dt.int32
Alu = mybir.AluOpType
Act = mybir.ActivationFunctionType
AX = mybir.AxisListType.X

B, S, D = 4, 4096, 1024
OWN = S // 2
NH, HD = 16, 64
DI = 4096
MAGIC = float(np.float32(1.5 * 2 ** 23))
KB = 1024


def build_nc():
    nc = bass.Bass("TRN2", target_bir_lowering=False)

    x0 = nc.dram_tensor("x0", [S, D], F32, kind="ExternalInput")
    ipT = nc.dram_tensor("ipT", [D, 3 * D], F32, kind="ExternalInput")
    ipb = nc.dram_tensor("ipb", [1, 3 * D], F32, kind="ExternalInput")
    owT = nc.dram_tensor("owT", [D, D], F32, kind="ExternalInput")
    obr = nc.dram_tensor("obr", [1, D], F32, kind="ExternalInput")
    w1Td = nc.dram_tensor("w1T", [D, DI], F32, kind="ExternalInput")
    b1v = nc.dram_tensor("b1v", [DI], F32, kind="ExternalInput")
    w2Td = nc.dram_tensor("w2T", [DI, D], F32, kind="ExternalInput")
    b2r = nc.dram_tensor("b2r", [1, D], F32, kind="ExternalInput")
    wfd = {"g": nc.dram_tensor("wgT", [D, DI], F32, kind="ExternalInput"),
           "u": nc.dram_tensor("wuT", [D, DI], F32, kind="ExternalInput"),
           "d": nc.dram_tensor("wdT", [DI, D], F32, kind="ExternalInput")}
    ddv = nc.dram_tensor("ddv", [D], F32, kind="ExternalInput")
    ln1v = nc.dram_tensor("ln1v", [D], F32, kind="ExternalInput")
    ln2v = nc.dram_tensor("ln2v", [D], F32, kind="ExternalInput")
    gwT8 = nc.dram_tensor("gwT8", [D, 8], F32, kind="ExternalInput")
    gb8d = nc.dram_tensor("gb8", [8, 1], F32, kind="ExternalInput")
    phid = nc.dram_tensor("phi", [8, 8], F32, kind="ExternalInput")
    id8d = nc.dram_tensor("id8d", [8, 8], F32, kind="ExternalInput")
    hcold = nc.dram_tensor("hcold", [128, 2], F32, kind="ExternalInput")
    x0ownd = nc.dram_tensor("x0ownd", [OWN, D], F32, kind="ExternalInput")
    maskUd = nc.dram_tensor("maskUd", [128, 64], BF16, kind="ExternalInput")
    id128d = nc.dram_tensor("id128d", [128, 128], BF16, kind="ExternalInput")

    y = nc.dram_tensor("y", [OWN, D], F32, kind="ExternalOutput")
    z1d = nc.dram_tensor("z1d", [S, D], BF16)
    qd = nc.dram_tensor("qd", [D, OWN], BF16)
    kd = nc.dram_tensor("kd", [D, S], BF16)
    xod = nc.dram_tensor("xod", [D, OWN], BF16)
    n2d = nc.dram_tensor("n2d", [D, OWN], BF16)
    x1d = nc.dram_tensor("x1d", [OWN, D], BF16)
    pdd = nc.dram_tensor("pdd", [OWN, DI], BF16)
    pqd = nc.dram_tensor("pqd", [OWN, DI], BF16)
    wqd = {"g": nc.dram_tensor("wgqd", [D, DI], BF16),
           "u": nc.dram_tensor("wuqd", [D, DI], BF16),
           "d": nc.dram_tensor("wdqd", [DI, D], BF16)}
    m64d = nc.dram_tensor("m64d", [8, 8], F32)

    def SBA(name, shape, dtype, off_kb):
        off = int(round(off_kb * KB / 32.0)) * 32
        return nc.alloc_sbuf_tensor_at(name, shape, dtype, offset=off)

    ones_b = SBA("ones_b", [128, 128], BF16, 24.0)
    ones_f = SBA("ones_f", [128, 128], F32, 24.25)
    mku = SBA("mku", [128, 64], BF16, 24.75)
    id128 = SBA("id128", [128, 128], BF16, 68.5)
    cst = SBA("cst", [128, 8], F32, 68.8)
    beta = SBA("beta", [128, 8], F32, 25.0)
    betm1 = SBA("betm1", [128, 8], F32, 25.05)
    betacol = SBA("betacol", [128, 1], F32, 25.975)
    ln1c = SBA("ln1c", [128, 8], F32, 25.1)
    ln2c = SBA("ln2c", [128, 8], F32, 25.15)
    b1c = SBA("b1c", [128, 32], F32, 25.2)
    ipbcT = SBA("ipbcT", [128, 24], F32, 25.35)
    bq8 = SBA("bq8", [128, 8], F32, 25.45)
    obc = SBA("obc", [128, 8], F32, 25.5)
    xs = SBA("xs", [128, 8], F32, 25.55)
    xsb = SBA("xsb", [128, 8], BF16, 25.6)
    hcol = SBA("hcol", [128, 2], F32, 25.65)
    sk1 = SBA("sk1", [8, 8], F32, 25.7)
    sk2 = SBA("sk2", [8, 8], F32, 25.75)
    skr = SBA("skr", [8, 1], F32, 25.8)
    id8s = SBA("id8s", [8, 8], F32, 25.85)
    g8 = SBA("g8", [8, 1], F32, 25.9)
    gb8s = SBA("gb8s", [8, 1], F32, 25.95)
    mpr = SBA("mpr", [1, 64], F32, 26.0)
    mpb = SBA("mpb", [128, 64], F32, 26.3)
    gw8b = SBA("gw8b", [128, 64], BF16, 26.6)
    gw8f = SBA("gw8f", [128, 64], F32, 26.75)
    wstat = SBA("wstat", [128, 48], F32, 27.05)
    wsc = SBA("wsc", [128, 8], F32, 27.3)
    zscin = SBA("zscin", [128, 16], F32, 27.35)
    invgc = SBA("invgc", [128, 16], F32, 27.45)
    invuc = SBA("invuc", [128, 16], F32, 27.55)
    pmxall = SBA("pmxall", [128, 128], F32, 27.65)
    pscm = SBA("pscm", [128, 16], F32, 28.15)
    psci = SBA("psci", [128, 16], F32, 28.25)
    ipbr_b = SBA("ipbr_b", [1, 3 * D], BF16, 28.4)
    obb2_b = SBA("obb2_b", [1, D], BF16, 34.4)
    tr1 = SBA("tr1", [1, D], F32, 36.4)
    tr2 = SBA("tr2", [1, D], F32, 40.4)
    r1row = SBA("r1row", [1, S], F32, 44.4)
    r2row = SBA("r2row", [1, OWN], F32, 60.4)

    R1, R2, R3 = 71.0, 135.0, 200.0
    x0s = [SBA(f"x0s{i}", [128, D], F32, R1 + 4 * i) for i in range(4)]
    z1s = [SBA(f"z1s{i}", [128, D], BF16, R1 + 24 + 2 * i) for i in range(4)]
    stm = SBA("stm", [128, 4], F32, R3 + 12)
    stq = SBA("stq", [128, 4], F32, R3 + 12.05)
    stv = SBA("stv", [128, 4], F32, R3 + 12.1)
    stq2 = SBA("stq2", [128, 4], F32, R3 + 12.15)
    sqs = [SBA(f"sqs{i}", [128, D], BF16, R1 + 16 + 2 * i) for i in range(4)]
    xdb = SBA("xdb", [128, 8 * S], BF16, R1)
    z1T2 = [SBA(f"z1T{i}", [128, S], BF16, R2 + 48 * i) for i in range(2)]
    sprime = SBA("sprime", [128, 2049], F32, R2 + 8)
    xdcs = SBA("xdcs", [128, S], F32, R2 + 16.2)
    sqc = SBA("sqc", [128, S], BF16, R2 + 32.2)
    xoc = SBA("xoc", [128, OWN], BF16, R2 + 40.2)
    msacc = SBA("msacc", [1, S], F32, R2 + 44.2)
    r1b = SBA("r1b", [128, S], F32, R3)
    n1o = SBA("n1o", [128, 8 * OWN], BF16, R2)
    vbuf = SBA("vbuf", [128, 32 * 1040], BF16, R2)
    wst = SBA("wst", [128, 8 * D], BF16, R3)
    qkev = [SBA(f"qkev{i}", [128, 512], BF16, R3 + 16 + i) for i in range(2)]
    obuf = SBA("obuf", [128, 8 * OWN], BF16, R1)
    pr = SBA("pr", [128, 4096], BF16, R1 + 32)
    kh = [SBA(f"kh{i}", [64, S], BF16, R1 + 40 + 8 * i) for i in range(2)]
    qh = [SBA(f"qh{i}", [64, 512], BF16, R1 + 56 + i) for i in range(2)]
    onsb = [SBA(f"onsb{i}", [64, 512], BF16, R1 + 58 + i) for i in range(2)]
    rlsb = SBA("rlsb", [128, 512], BF16, R1 + 60)
    rlbs = SBA("rlbs", [128, 512], F32, R1 + 62)
    wkc = SBA("wkc", [128, 2048], F32, R3 + 16 - 0.03125)
    owb = SBA("owb", [128, 8 * D], BF16, R3)
    bo = SBA("bo", [128, 16 * D], BF16, R1 + 32)
    s2 = SBA("s2", [128, 8 * OWN], BF16, R2)
    sq2s = SBA("sq2s", [128, 2048], BF16, R2 + 32)
    r2b = SBA("r2b", [128, OWN], F32, R2 + 36.2)
    xosr = [SBA(f"xosr{i}", [128, 2048], BF16, R3 + 12 + 4 * i) for i in range(2)]
    w1h = SBA("w1h", [128, 8 * 2048], BF16, R1)
    w2h = SBA("w2h", [128, 16 * D], BF16, R2)
    bigm = SBA("bigm", [128, 8 * D], BF16, R2 + 32)
    hth = SBA("hth", [128, 16 * 256], BF16, R2 + 48)
    n2s = [SBA(f"n2s{i}", [128, 2048], BF16, R2 + 56 + 4 * i) for i in range(2)]
    wqc2 = [SBA(f"wqc{i}", [128, 1024], F32, R3 + 4 * i) for i in range(2)]
    wqo = SBA("wqo", [128, 2048], BF16, R3 + 8)
    x0o = [SBA(f"x0o{i}", [128, D], F32, R1 + 4 * i) for i in range(2)]
    x1t = [SBA(f"x1t{i}", [128, D], F32, R1 + 8 + 4 * i) for i in range(2)]
    zt = [SBA(f"zt{i}", [128, D], F32, R1 + 16 + 4 * i) for i in range(2)]
    z2qs = [SBA(f"z2qs{i}", [128, D], BF16, R1 + 24 + 2 * i) for i in range(2)]
    z2qT = SBA("z2qT", [128, 8 * OWN], BF16, R2)
    wgs = [SBA(f"wgs{i}", [128, 8 * 512], BF16, R2 + 32 + 8 * i) for i in range(2)]
    wus = [SBA(f"wus{i}", [128, 8 * 512], BF16, R2 + 48 + 8 * i) for i in range(2)]
    gsb = SBA("gsb", [128, 512], F32, R3)
    psb = [SBA(f"psb{i}", [128, 512], F32, R3 + 2 + 2 * i) for i in range(2)]
    ptile2 = [SBA(f"ptile{i}", [128, DI], BF16, R3 + 6 + 8 * i)
              for i in range(2)]
    pwide = SBA("pwide", [128, DI], F32, R1)
    pqtile2 = [SBA(f"pqtile{i}", [128, DI], BF16, R1 + 16 + 8 * i)
               for i in range(2)]
    fsb = SBA("fsb", [128, 16 * D], F32, R1)
    pqt = SBA("pqt", [128, 8 * 2048], BF16, R2)
    wds = SBA("wds", [128, 8 * D], BF16, R2 + 32)
    x1r = [SBA(f"x1r{i}", [128, D], BF16, R2 + 2 * i) for i in range(4)]
    otl = [SBA(f"otl{i}", [128, D], F32, R2 + 8 + 4 * i) for i in range(4)]

    psA = nc.alloc_psum_tensor("psA", [128, 2048], F32)
    psC = nc.alloc_psum_tensor("psC", [128, 1024], F32)
    psD = nc.alloc_psum_tensor("psD", [128, 1024], F32)

    semnames = ["qA0", "qA1", "qB0", "qB1", "qC0", "qC1", "qD0", "qD1",
                "qE0", "qE1", "qF0", "qF1", "qW0", "qW1", "qX0", "qX1",
                "qY0", "qY1", "qZ0", "qZ1", "qV0", "qV1", "qP0", "cV", "cS", "cT", "cG"]
    sems, cnt = {}, {}
    for n in semnames:
        sems[n] = nc.semaphore(n).__enter__()
        cnt[n] = 0

    _cvals = [1e-5, 1e-6, 1.0 / 1024.0, 0.125]
    for _j, _v in enumerate(_cvals):
        nc.const_aps.register(F32, _v, cst[:, _j:_j + 1])

    blk = nc.Block()
    block = blk.__enter__()

    def inc(instr, name, v=None):
        if v is None:
            v = 16 if name.startswith("q") else 1
        cnt[name] += v
        instr.then_inc(sems[name], v)
        return cnt[name]

    def tok(name):
        return cnt[name]


    class _RecInstr:
        __slots__ = ("incs",)
        def __init__(self):
            self.incs = []
        def then_inc(self, sem, v):
            self.incs.append((sem, v))
            return self

    class _RecEngine:
        def __init__(self, name):
            self._name = name
            self._calls = []
        def __getattr__(self, m):
            def call(*a, **k):
                h = _RecInstr()
                self._calls.append((m, a, k, h))
                return h
            return call

    _PROX = {n: _RecEngine(n) for n in
             ("sync", "vector", "scalar", "tensor", "gpsimd")}

    def _replay(name):
        calls = _PROX[name]._calls
        _PROX[name]._calls = []
        def body(e):
            for m, a, k, h in calls:
                alp = k.get("_alp")
                if alp is not None:
                    k = {x: v for x, v in k.items() if x != "_alp"}
                    with nc.allow_low_precision(alp):
                        i = getattr(e, m)(*a, **k)
                else:
                    i = getattr(e, m)(*a, **k)
                for sem, v in h.incs:
                    i.then_inc(sem, v)
        return body

    def flush():
        for n, meth in (("sync", block.sync), ("tensor", block.tensor),
                        ("scalar", block.scalar), ("vector", block.vector),
                        ("gpsimd", block.gpsimd)):
            if _PROX[n]._calls:
                meth(_replay(n))

    def on_sync(f):
        f(_PROX["sync"])

    def on_vec(f):
        e = _PROX["vector"]
        e.wait_ge(sems["cV"], cnt["cV"])
        f(e)

    def on_scal(f):
        e = _PROX["scalar"]
        e.wait_ge(sems["cS"], cnt["cS"])
        f(e)

    def on_ten(f):
        f(_PROX["tensor"])

    def on_gp(f):
        f(_PROX["gpsimd"])

    def W(e, name, v):
        if v > 0:
            e.wait_ge(sems[name], v)

    def own_blend(e, dst_v, ev_v, od_v):
        e.tensor_scalar(dst_v, od_v, hcol[:, 0:1], None, Alu.mult)
        e.drain()
        e.scalar_tensor_tensor(dst_v, ev_v, hcol[:, 1:2], dst_v,
                               Alu.mult, Alu.add)

    def f(e):
        W(e, "qP0", cnt["qP0"])
        inc(e.dma_start(mku[:], maskUd[:, :]), "qP0")
        W(e, "qP0", cnt["qP0"])
        inc(e.dma_start(hcol[:], hcold[:, :]), "qP0")
        W(e, "qP0", cnt["qP0"])
        inc(e.dma_start(beta[:], bass.AP(ddv, 0, [[1, 128], [128, 8]]),
                        allow_slow_non_contiguous=True), "qP0")
        W(e, "qP0", cnt["qP0"])
        inc(e.dma_start(ln1c[:], bass.AP(ln1v, 0, [[1, 128], [128, 8]]),
                        allow_slow_non_contiguous=True), "qP0")
        W(e, "qP0", cnt["qP0"])
        inc(e.dma_start(ln2c[:], bass.AP(ln2v, 0, [[1, 128], [128, 8]]),
                        allow_slow_non_contiguous=True), "qP0")
        W(e, "qP0", cnt["qP0"])
        inc(e.dma_start(b1c[:], bass.AP(b1v, 0, [[1, 128], [128, 32]]),
                        allow_slow_non_contiguous=True), "qP0")
        W(e, "qP0", cnt["qP0"])
        inc(e.dma_start(ipbcT[:], bass.AP(ipb, 0, [[1, 128], [128, 24]]),
                        allow_slow_non_contiguous=True), "qP0")
        W(e, "qP0", cnt["qP0"])
        inc(e.dma_start(obc[:], bass.AP(obr, 0, [[1, 128], [128, 8]]),
                        allow_slow_non_contiguous=True), "qP0")
        W(e, "qP0", cnt["qP0"])
        inc(e.dma_start(sk1[:], phid[:, :]), "qP0")
        W(e, "qP0", cnt["qP0"])
        inc(e.dma_start(id8s[:], id8d[:, :]), "qP0")
        W(e, "qP0", cnt["qP0"])
        inc(e.dma_start(gb8s[:], gb8d[:, :]), "qP0")
        W(e, "qP0", cnt["qP0"])
        inc(e.dma_start(tr1[:], obr[:, :]), "qP0")
        W(e, "qP0", cnt["qP0"])
        inc(e.dma_start(tr2[:], b2r[:, :]), "qP0")
        W(e, "qP0", cnt["qP0"])
        inc(e.dma_start(gw8f[:],
                        bass.AP(gwT8, 0, [[8, 128], [1024, 8], [1, 8]]),
                        allow_slow_non_contiguous=True), "qP0")
        W(e, "qP0", cnt["qP0"])
        inc(e.dma_start(id128[:], id128d[:, :]), "qP0")
    on_sync(f)
    t_p0 = tok("qP0")

    def f(e):
        W(e, "qW0", cnt["qW0"])
        i = e.dma_start(ipbr_b[:], ipb[:, :])
        inc(i, "qW0")
    on_gp(f)
    t_ipbr = tok("qW0")

    def f(e):
        e.memset(ones_b[:], 1.0)
        e.memset(ones_f[:], 1.0)
        e.memset(cst[:, 0:1], 1e-5)
        e.memset(cst[:, 1:2], 1e-6)
        e.memset(cst[:, 2:3], 1.0 / 1024.0)
        e.memset(cst[:, 3:4], 0.125)
        e.memset(pmxall[:], 0.0)
        W(e, "qP0", t_p0)
        e.tensor_scalar(bq8[:], ipbcT[:, 0:8], 0.125, None, Alu.mult)
        e.tensor_copy(gw8b[:], gw8f[:])
        e.tensor_add(tr1[:], tr1[:], tr2[:])
        e.drain()
        i = e.tensor_copy(obb2_b[:], tr1[:])
        inc(i, "cV")
    on_vec(f)
    t_p0v = tok("cV")

    def f(e):
        W(e, "cV", t_p0v)
        e.activation(beta[:], beta[:], Act.Sigmoid)
        i = e.activation(sk1[:], sk1[:], Act.Exp)
        inc(i, "cS")
    on_scal(f)
    t_p0s = tok("cS")

    def f(e):
        W(e, "cS", t_p0s)
        i = e.tensor_scalar(betm1[:], beta[:], -1.0, None, Alu.add)
        inc(i, "cV")
    on_vec(f)

    cur = sk1
    for itn in range(31):
        def f(e, cur=cur, itn=itn):
            if itn < 30:
                e.reduce_sum(skr[:], cur[:], axis=AX)
                e.drain()
                e.reciprocal(skr[:], skr[:])
                e.drain()
                i = e.tensor_scalar(cur[:], cur[:], skr[:, 0:1], None,
                                    Alu.mult)
                inc(i, "cV")
            else:
                i = e.engine_nop()
                inc(i, "cV")
        on_vec(f)
        tv = tok("cV")

        def f(e, cur=cur, tv=tv):
            W(e, "cV", tv)
            i = e.transpose(psD[0:8, 0:8], cur[:], id8s[:])
            inc(i, "cT")
        on_ten(f)
        tt = tok("cT")
        nxt = sk2 if cur is sk1 else sk1

        def f(e, nxt=nxt, tt=tt):
            W(e, "cT", tt)
            i = e.tensor_copy(nxt[:], psD[0:8, 0:8])
            inc(i, "cV")
        on_vec(f)
        cur = nxt
    skHT = cur

    flush()

    t_z1done, t_z1wb = {}, {}
    for t in range(32):
        sl = t % 4
        qx = "qA0" if t % 2 == 0 else "qA1"
        qz = "qB0" if t % 2 == 0 else "qB1"

        def f(e, t=t, sl=sl, qx=qx):
            if t >= 4:
                W(e, "cV", t_z1done[t - 4])
            W(e, qx, cnt[qx])
            i = e.dma_start(x0s[sl][:], x0[128 * t:128 * (t + 1), :])
            inc(i, qx)
        on_sync(f)
        t_x0 = tok(qx)

        def f(e, sl=sl, t_x0=t_x0, qx=qx):
            W(e, qx, t_x0)
            i = e.activation(sqs[sl][:], x0s[sl][:], Act.Square,
                             accum_out=stq[:, sl:sl + 1])
            inc(i, "cS")
        on_scal(f)
        t_sq = tok("cS")

        def f(e, sl=sl, t_x0=t_x0, t_sq=t_sq, qx=qx):
            W(e, qx, t_x0)
            e.reduce_sum(stm[:, sl:sl + 1], x0s[sl][:], axis=AX)
            e.drain()
            e.tensor_scalar(stm[:, sl:sl + 1], stm[:, sl:sl + 1], 1.0 / D,
                            None, Alu.mult)
            e.drain()
            e.tensor_mul(stq2[:, sl:sl + 1], stm[:, sl:sl + 1],
                         stm[:, sl:sl + 1])
            e.drain()
            W(e, "cS", t_sq)
            i = e.scalar_tensor_tensor(stv[:, sl:sl + 1], stq[:, sl:sl + 1],
                                       1.0 / D, stq2[:, sl:sl + 1],
                                       Alu.mult, Alu.subtract)
            inc(i, "cV")
        on_vec(f)
        t_var = tok("cV")

        def f(e, sl=sl, t_var=t_var):
            W(e, "cV", t_var)
            i = e.activation(stv[:, sl:sl + 1], stv[:, sl:sl + 1], Act.Sqrt,
                             bias=1e-5)
            inc(i, "cS")
        on_scal(f)
        t_r = tok("cS")

        def f(e, t=t, sl=sl, t_r=t_r, qz=qz):
            W(e, "cS", t_r)
            if t >= 4:
                W(e, qz, t_z1wb[t - 4])
            e.reciprocal(stv[:, sl:sl + 1], stv[:, sl:sl + 1])
            e.drain()
            i = e.tensor_scalar(z1s[sl][:], x0s[sl][:], stm[:, sl:sl + 1],
                                stv[:, sl:sl + 1], Alu.subtract, Alu.mult)
            inc(i, "cV")
        on_vec(f)
        t_z1done[t] = tok("cV")

        def f(e, t=t, sl=sl, qz=qz, tz=t_z1done[t]):
            W(e, "cV", tz)
            W(e, qz, cnt[qz])
            i = e.dma_start(z1d[128 * t:128 * (t + 1), :], z1s[sl][:])
            inc(i, qz)
        on_gp(f)
        t_z1wb[t] = tok(qz)

    t_A_qA0, t_A_qA1 = tok("qA0"), tok("qA1")
    t_A_cS = tok("cS")
    t_A_qB0, t_A_qB1 = tok("qB0"), tok("qB1")

    flush()

    t_scan, t_sqd, t_xowb, t_sqmm_last = {}, {}, {}, {}
    for c in range(8):
        def f(e, c=c):
            if c >= 2:
                W(e, "cV", t_scan[c - 2])
            if c == 0:
                W(e, "qB0", t_z1wb[30])
                W(e, "qB1", t_z1wb[31])
            W(e, "qC0", cnt["qC0"])
            i = e.dma_start(z1T2[c % 2][:], z1d[:, 128 * c:128 * (c + 1)],
                            transpose=True)
            inc(i, "qC0")
        on_sync(f)
        t_z1T = tok("qC0")

        def f(e, c=c, t_z1T=t_z1T):
            W(e, "qC0", t_z1T)
            z1T = z1T2[c % 2]
            if c == 0:
                e.memset(sprime[:, 0:1], 0.0)
            else:
                W(e, "cS", t_sqd[c - 1])
            bcast = bass.AP(beta, c, [[8, 128], [0, 2048]])
            e.tensor_tensor_scan(sprime[:, 1:2049], bcast, z1T[:, 0:2048],
                                 0.0, Alu.mult, Alu.add)
            e.drain()
            e.scalar_tensor_tensor(xdcs[:, 0:2048], sprime[:, 0:2048],
                                   betm1[:, c:c + 1], z1T[:, 0:2048],
                                   Alu.mult, Alu.add)
            e.drain()
            e.tensor_copy(sprime[:, 0:1], sprime[:, 2048:2049])
            e.drain()
            e.tensor_tensor_scan(sprime[:, 1:2049], bcast, z1T[:, 2048:4096],
                                 sprime[:, 0:1], Alu.mult, Alu.add)
            e.drain()
            e.scalar_tensor_tensor(xdcs[:, 2048:4096], sprime[:, 0:2048],
                                   betm1[:, c:c + 1], z1T[:, 2048:4096],
                                   Alu.mult, Alu.add)
            e.drain()
            e.memset(sprime[:, 0:1], 0.0)
            e.drain()
            e.reduce_sum(xs[:, c:c + 1], xdcs[:], axis=AX)
            e.tensor_copy(xdb[:, S * c:S * (c + 1)], xdcs[:])
            if c >= 1:
                W(e, "qB0", t_xowb[c - 1])
            srcv = xdcs[:, :].rearrange("p (u t s) -> p u t s", t=2, s=64)
            dstv = xoc[:, :].rearrange("p (u s) -> p u s", s=64)
            own_blend(e, dstv, srcv[:, :, 0, :], srcv[:, :, 1, :])
            i = e.engine_nop()
            inc(i, "cV")
        on_vec(f)
        t_scan[c] = tok("cV")

        def f(e, c=c, tsc=t_scan[c]):
            W(e, "cV", tsc)
            if c >= 1:
                W(e, "cT", t_sqmm_last[c - 1])
            i = e.activation(sqc[:], xdcs[:], Act.Square)
            inc(i, "cS")
        on_scal(f)
        t_sqd[c] = tok("cS")

        def f(e, c=c, tsc=t_scan[c]):
            W(e, "cV", tsc)
            W(e, "qB0", cnt["qB0"])
            i = e.dma_start(xod[128 * c:128 * (c + 1), :], xoc[:])
            inc(i, "qB0")
        on_gp(f)
        t_xowb[c] = tok("qB0")

        for g in range(8):
            if g < 4:
                tgt = lambda g=g: psA[0:1, 512 * g:512 * (g + 1)]
            elif g < 6:
                tgt = lambda g=g: psC[0:1, 512 * (g - 4):512 * (g - 3)]
            else:
                tgt = lambda g=g: psD[0:1, 512 * (g - 6):512 * (g - 5)]

            def f(e, c=c, g=g, tsq=t_sqd[c], tgt=tgt):
                if g == 0:
                    W(e, "cS", tsq)
                i = e.matmul(tgt(), ones_b[:, 0:1],
                             sqc[:, 512 * g:512 * (g + 1)],
                             start=(c == 0), stop=(c == 7))
                inc(i, "cT")
            on_ten(f)
        t_sqmm_last[c] = tok("cT")

    def f(e, tm=t_sqmm_last[7]):
        W(e, "cT", tm)
        e.activation(r1row[0:1, 0:2048], psA[0:1, 0:2048], Act.Sqrt,
                     bias=1e-6, scale=1.0 / D)
        e.activation(r1row[0:1, 2048:3072], psC[0:1, 0:1024], Act.Sqrt,
                     bias=1e-6, scale=1.0 / D)
        i = e.activation(r1row[0:1, 3072:4096], psD[0:1, 0:1024], Act.Sqrt,
                         bias=1e-6, scale=1.0 / D)
        inc(i, "cS")
    on_scal(f)
    t_r1s = tok("cS")

    def f(e, t_r1s=t_r1s):
        W(e, "cS", t_r1s)
        i = e.reciprocal(r1row[:], r1row[:])
        inc(i, "cV")
    on_vec(f)
    t_r1 = tok("cV")

    t_r1c = {}
    for g in range(8):
        def f(e, g=g, t_r1=t_r1):
            if g == 0:
                W(e, "cV", t_r1)
            if g >= 4:
                W(e, "cV", t_r1c[g - 4])
            i = e.matmul(psA[:, 512 * (g % 4):512 * (g % 4) + 512],
                         ones_f[0:1, :], r1row[0:1, 512 * g:512 * (g + 1)],
                         start=True, stop=True)
            inc(i, "cT")
        on_ten(f)
        t_pe = tok("cT")

        def f(e, g=g, t_pe=t_pe):
            W(e, "cT", t_pe)
            if g == 0:
                W(e, "qA0", t_A_qA0)
                W(e, "qA1", t_A_qA1)
                W(e, "cS", t_A_cS)
                W(e, "qB0", t_A_qB0)
                W(e, "qB1", t_A_qB1)
            i = e.tensor_copy(r1b[:, 512 * g:512 * (g + 1)],
                              psA[:, 512 * (g % 4):512 * (g % 4) + 512])
            inc(i, "cV")
        on_vec(f)
        t_r1c[g] = tok("cV")
    t_r1b = tok("cV")

    def f(e):
        i = e.tensor_scalar(xsb[:], xs[:], 1.0 / S, None, Alu.mult)
        inc(i, "cV")
    on_vec(f)
    t_xsb = tok("cV")

    def f(e, t_xsb=t_xsb):
        W(e, "cV", t_xsb)
        W(e, "cS", t_r1s)
        for kc in range(8):
            i = e.matmul(psD[0:8, 0:1], gw8b[:, 8 * kc:8 * (kc + 1)],
                         xsb[:, kc:kc + 1], start=(kc == 0), stop=(kc == 7))
        inc(i, "cT")
    on_ten(f)
    t_gmm = tok("cT")

    def f(e, t_gmm=t_gmm):
        W(e, "cT", t_gmm)
        i = e.activation(g8[:], psD[0:8, 0:1], Act.Sigmoid,
                         bias=gb8s[:, 0:1])
        inc(i, "cS")
    on_scal(f)
    t_g8 = tok("cS")

    def f(e, t_g8=t_g8):
        W(e, "cS", t_g8)
        e.tensor_scalar(skHT[:], skHT[:], g8[:, 0:1], None, Alu.mult)
        e.drain()
        i = e.tensor_add(skHT[:], skHT[:], id8s[:])
        inc(i, "cV")
    on_vec(f)
    t_mpt = tok("cV")

    def f(e, t_mpt=t_mpt):
        W(e, "cV", t_mpt)
        W(e, "qB1", cnt["qB1"])
        i = e.dma_start(m64d[:, :], skHT[:])
        inc(i, "qB1")
    on_gp(f)
    t_m64w = tok("qB1")

    def f(e, t_m64w=t_m64w):
        W(e, "qB1", t_m64w)
        W(e, "qC1", cnt["qC1"])
        i = e.dma_start(mpr[:], bass.AP(m64d, 0, [[64, 1], [1, 64]]))
        inc(i, "qC1")
    on_sync(f)
    t_mprl = tok("qC1")

    def f(e, t_mprl=t_mprl):
        W(e, "qC1", t_mprl)
        i = e.matmul(psD[:, 0:64], ones_f[0:1, :], mpr[0:1, :],
                     start=True, stop=True)
        inc(i, "cT")
    on_ten(f)
    t_mpbT = tok("cT")

    def f(e, t_mpbT=t_mpbT):
        W(e, "cT", t_mpbT)
        i = e.tensor_copy(mpb[:], psD[:, 0:64])
        inc(i, "cV")
    on_vec(f)
    t_mpb = tok("cV")

    flush()

    for c in range(8):
        def f(e, c=c):
            e.scalar_tensor_tensor(xdb[:, S * c:S * (c + 1)],
                                   xdb[:, S * c:S * (c + 1)],
                                   ln1c[:, c:c + 1], r1b[:],
                                   Alu.mult, Alu.mult)
            e.drain()
            srcv = xdb[:, S * c:S * (c + 1)].rearrange(
                "p (u t s) -> p u t s", t=2, s=64)
            dstv = n1o[:, OWN * c:OWN * (c + 1)].rearrange(
                "p (u s) -> p u s", s=64)
            own_blend(e, dstv, srcv[:, :, 0, :], srcv[:, :, 1, :])
            i = e.engine_nop()
            inc(i, "cV")
        on_vec(f)
    t_n1all = tok("cV")

    flush()

    def f(e):
        W(e, "cV", t_n1all)
        for kc in range(8):
            W(e, "qW0", cnt["qW0"])
            inc(e.dma_start(wst[:, D * kc:D * (kc + 1)],
                            ipT[128 * kc:128 * (kc + 1), 0:D]), "qW0")
    on_gp(f)
    t_wq = tok("qW0")

    t_qkev, t_qkwb = {}, {}
    nq = 0
    for oc in range(8):
        for tg in range(4):
            bk = nq % 4

            def f(e, oc=oc, tg=tg, bk=bk, nq=nq):
                if nq == 0:
                    W(e, "qW0", t_wq)
                    W(e, "cV", t_n1all)
                if nq >= 4:
                    W(e, "cS", t_qkev[nq - 4])
                for kc in range(8):
                    i = e.matmul(
                        psA[:, 512 * bk:512 * (bk + 1)],
                        wst[:, D * kc + 128 * oc:D * kc + 128 * (oc + 1)],
                        n1o[:, OWN * kc + 512 * tg:OWN * kc + 512 * (tg + 1)],
                        start=(kc == 0), stop=(kc == 7))
                inc(i, "cT")
            on_ten(f)
            t_mm = tok("cT")

            def f(e, oc=oc, bk=bk, nq=nq, t_mm=t_mm):
                W(e, "cT", t_mm)
                if nq >= 2:
                    W(e, "qC0" if nq % 2 == 0 else "qC1", t_qkwb[nq - 2])
                i = e.activation(qkev[nq % 2][:],
                                 psA[:, 512 * bk:512 * (bk + 1)],
                                 Act.Identity, bias=bq8[:, oc:oc + 1],
                                 scale=0.125)
                inc(i, "cS")
            on_scal(f)
            t_qkev[nq] = tok("cS")

            def f(e, oc=oc, tg=tg, nq=nq, tev=t_qkev[nq]):
                W(e, "cS", tev)
                W(e, "qC0", cnt["qC0"])
                _s = "qC0" if nq % 2 == 0 else "qC1"
                W(e, _s, cnt[_s])
                i = e.dma_start(qd[128 * oc:128 * (oc + 1),
                                   512 * tg:512 * (tg + 1)], qkev[nq % 2][:])
                inc(i, "qC0" if nq % 2 == 0 else "qC1")
            on_sync(f)
            t_qkwb[nq] = tok("qC0" if nq % 2 == 0 else "qC1")
            nq += 1

    def f(e, tprev=tok("cT")):
        W(e, "cT", tprev)
        for kc in range(8):
            W(e, "qW1", cnt["qW1"])
            inc(e.dma_start(wst[:, D * kc:D * (kc + 1)],
                            ipT[128 * kc:128 * (kc + 1), D:2 * D]), "qW1")
    on_gp(f)
    t_wk = tok("qW1")

    for oc in range(8):
        for tg in range(8):
            bk = nq % 4

            def f(e, oc=oc, tg=tg, bk=bk, nq=nq):
                if oc == 0 and tg == 0:
                    W(e, "qW1", t_wk)
                W(e, "cS", t_qkev[nq - 4])
                for kc in range(8):
                    i = e.matmul(
                        psA[:, 512 * bk:512 * (bk + 1)],
                        wst[:, D * kc + 128 * oc:D * kc + 128 * (oc + 1)],
                        xdb[:, S * kc + 512 * tg:S * kc + 512 * (tg + 1)],
                        start=(kc == 0), stop=(kc == 7))
                inc(i, "cT")
            on_ten(f)
            t_mm = tok("cT")

            def f(e, oc=oc, bk=bk, nq=nq, t_mm=t_mm):
                W(e, "cT", t_mm)
                W(e, "qC0" if nq % 2 == 0 else "qC1", t_qkwb[nq - 2])
                i = e.activation(qkev[nq % 2][:],
                                 psA[:, 512 * bk:512 * (bk + 1)],
                                 Act.Identity, bias=ipbcT[:, 8 + oc:9 + oc])
                inc(i, "cS")
            on_scal(f)
            t_qkev[nq] = tok("cS")

            def f(e, oc=oc, tg=tg, nq=nq, tev=t_qkev[nq]):
                W(e, "cS", tev)
                W(e, "qC0", cnt["qC0"])
                _s = "qC0" if nq % 2 == 0 else "qC1"
                W(e, _s, cnt[_s])
                i = e.dma_start(kd[128 * oc:128 * (oc + 1),
                                   512 * tg:512 * (tg + 1)], qkev[nq % 2][:])
                inc(i, "qC0" if nq % 2 == 0 else "qC1")
            on_sync(f)
            t_qkwb[nq] = tok("qC0" if nq % 2 == 0 else "qC1")
            nq += 1
    t_kwb0, t_kwb1 = tok("qC0"), tok("qC1")

    def f(e, tprev=tok("cT"), tsc=tok("cS")):
        W(e, "cT", tprev)
        W(e, "cS", tsc)
        W(e, "qB0", t_xowb[7])
        i = e.memset(vbuf[:, 64::65], 1.0)
        inc(i, "cV")
    on_vec(f)
    t_vones = tok("cV")

    def f(e, tprev=tok("cT")):
        W(e, "cT", tprev)
        for kc in range(8):
            W(e, "qW0", cnt["qW0"])
            inc(e.dma_start(wst[:, D * kc:D * (kc + 1)],
                            ipT[128 * kc:128 * (kc + 1), 2 * D:3 * D]),
                "qW0")
    on_gp(f)
    t_wv = tok("qW0")

    t_vev = {}
    nv = 0
    for tt in range(32):
        for vg in range(2):
            bk = nv % 4

            def f(e, tt=tt, vg=vg, bk=bk, nv=nv):
                if nv == 0:
                    W(e, "qW0", t_wv)
                    W(e, "cS", t_qkev[nq - 1])
                if nv >= 4:
                    W(e, "cV", t_vev[nv - 4])
                for kc in range(8):
                    e.matmul(
                        psA[:, 512 * bk:512 * (bk + 1)],
                        xdb[:, S * kc + 128 * tt:S * kc + 128 * (tt + 1)],
                        wst[:, D * kc + 512 * vg:D * kc + 512 * (vg + 1)],
                        start=(kc == 0), stop=False)
                i = e.matmul(
                    psA[:, 512 * bk:512 * (bk + 1)], ones_b[0:1, :],
                    ipbr_b[0:1, 2 * D + 512 * vg:2 * D + 512 * (vg + 1)],
                    start=False, stop=True)
                inc(i, "cT")
            on_ten(f)
            t_mm = tok("cT")

            def f(e, tt=tt, vg=vg, bk=bk, t_mm=t_mm, nv=nv):
                W(e, "cT", t_mm)
                if nv == 0:
                    W(e, "qB0", t_xowb[7])
                dst = vbuf[:, 1040 * tt + 520 * vg:1040 * tt + 520 * (vg + 1)]
                dst = dst.rearrange("p (hh c) -> p hh c", c=65)[:, :, 0:64]
                i = e.tensor_copy(dst, psA[:, 512 * bk:512 * (bk + 1)])
                inc(i, "cV")
            on_vec(f)
            t_vev[nv] = tok("cV")
            nv += 1
    t_vall = tok("cV")

    t_vmm_last = tok("cT")

    flush()

    ws_jobs = [(wi, wn, sub) for wi, wn in enumerate(["g", "u", "d"])
               for sub in range(16)]
    nws_box = [0]
    t_wsacc = {}

    def emit_ws_job():
        if nws_box[0] >= len(ws_jobs):
            return
        nws = nws_box[0]
        wi, wn, sub = ws_jobs[nws]

        def f(e, wn=wn, sub=sub, nws=nws):
            if nws == 0:
                W(e, "qC0", t_kwb0)
                W(e, "qC1", t_kwb1)
                W(e, "cS", t_qkev[95])
                W(e, "cV", t_vall)
            if nws >= 1:
                W(e, "cV", t_wsacc[nws - 1])
            if wn != "d":
                ch, hf = sub // 2, sub % 2
                W(e, "qF0", cnt["qF0"])
                inc(e.dma_start(wkc[:],
                                wfd[wn][128 * ch:128 * (ch + 1),
                                        2048 * hf:2048 * (hf + 1)]), "qF0")
            else:
                W(e, "qF0", cnt["qF0"])
                inc(e.dma_start(wkc[:, 0:1024],
                                wfd[wn][256 * sub:256 * sub + 128, :]),
                    "qF0")
                W(e, "qF0", cnt["qF0"])
                inc(e.dma_start(wkc[:, 1024:2048],
                                wfd[wn][256 * sub + 128:256 * (sub + 1), :]),
                    "qF0")
        on_sync(f)
        t_ld = tok("qF0")

        def f(e, wi=wi, sub=sub, t_ld=t_ld):
            W(e, "qF0", t_ld)
            col = 16 * wi + sub
            i = e.tensor_reduce(wstat[:, col:col + 1], wkc[:], AX,
                                Alu.add, apply_absolute_value=True)
            inc(i, "cV")
        on_vec(f)
        t_wsacc[nws] = tok("cV")
        nws_box[0] += 1

    owb_box = [0]
    t_owb = [0]

    def emit_owb():
        if owb_box[0]:
            return
        owb_box[0] = 1

        def f(e):
            W(e, "cV", t_wsacc[len(ws_jobs) - 1])
            for kc in range(8):
                W(e, "qW1", cnt["qW1"])
                inc(e.dma_start(owb[:, D * kc:D * (kc + 1)],
                                owT[128 * kc:128 * (kc + 1), :]), "qW1")
        on_gp(f)
        t_owb[0] = tok("qW1")

    t_khld, t_qld = {}, {}
    t_avL, t_expL, t_on, t_odma, t_G = {}, {}, {}, {}, {}
    s_tok = []
    exp_tok = []
    av_tok = []
    m_tok = {}
    it = 0
    kg = 0
    for hd_ in range(NH):
        khsl = hd_ % 2

        def f(e, hd_=hd_, khsl=khsl):
            if hd_ == 0:
                W(e, "qC0", t_kwb0)
                W(e, "qC1", t_kwb1)
                W(e, "cT", t_vmm_last)
            if hd_ >= 2:
                pit = 4 * (hd_ - 2) + 3
                W(e, "cT", t_avL[pit])
            W(e, "qD0", cnt["qD0"])
            i = e.dma_start(kh[khsl][:, 0:S],
                            kd[64 * hd_:64 * (hd_ + 1), 0:S])
            inc(i, "qD0")
        on_sync(f)
        t_khld[hd_] = tok("qD0")

        for i4 in range(4):
            nkb = 8 * i4 + 8
            G = nkb // 2
            sl = it % 2
            t_G[it] = G

            def f(e, i4=i4, hd_=hd_, sl=sl, it=it):
                if it == 0:
                    W(e, "qC0", t_kwb0)
                    W(e, "qC1", t_kwb1)
                    W(e, "cT", t_vmm_last)
                if it >= 2:
                    W(e, "cT", t_avL[it - 2])
                W(e, "qD1", cnt["qD1"])
                i = e.dma_start(qh[sl][:], qd[64 * hd_:64 * (hd_ + 1),
                                              512 * i4:512 * (i4 + 1)])
                inc(i, "qD1")
            on_sync(f)
            t_qld[it] = tok("qD1")

            for b0 in range(0, nkb, 4):
                kg_b = kg
                for kb in range(b0, b0 + 4):
                    sA = kg % 4
                    cs = 64 * max(0, kb - 8 * i4)

                    def f(e, i4=i4, hd_=hd_, it=it, kb=kb, sA=sA, cs=cs,
                          kg=kg, khsl=khsl):
                        if kb == 0:
                            W(e, "qD1", t_qld[it])
                            if i4 == 0:
                                W(e, "qD0", t_khld[hd_])
                        if kg >= 4 and kg % 2 == 0:
                            W(e, "cS", exp_tok[kg - 4])
                        elif kg == 0:
                            W(e, "cV", t_vall)
                        i = e.matmul(
                            psA[:, 512 * sA + cs:512 * (sA + 1)],
                            kh[khsl][:, 128 * kb:128 * (kb + 1)],
                            qh[it % 2][:, cs:512],
                            start=True, stop=True)
                        inc(i, "cT")
                    on_ten(f)
                    s_tok.append(tok("cT"))

                    if kg % 2 == 1:
                        def f(e, kg=kg):
                            W(e, "cT", s_tok[kg])
                            if kg >= 8:
                                W(e, "cT", av_tok[kg - 8])
                            sP = (kg - 1) % 8
                            sAp = (kg - 1) % 4
                            i = e.activation(
                                pr[:, 512 * sP:512 * sP + 1024],
                                psA[:, 512 * sAp:512 * sAp + 1024],
                                Act.Exp)
                            inc(i, "cS")
                        on_scal(f)
                        exp_tok.append(tok("cS"))
                        exp_tok.append(exp_tok[-1])

                        for dkb in (kb - 1, kb):
                            if dkb >= 8 * i4:
                                kgd = kg - (kb - dkb)
                                rc = 64 * (dkb - 8 * i4)

                                def f(e, kgd=kgd, rc=rc, kg=kg):
                                    W(e, "cS", exp_tok[kg])
                                    base = 512 * (kgd % 8) + rc
                                    i = e.tensor_mul(pr[:, base:base + 64],
                                                     pr[:, base:base + 64],
                                                     mku[:])
                                    inc(i, "cV")
                                on_vec(f)
                                m_tok[kgd] = tok("cV")
                    kg += 1

                for kb in range(b0, b0 + 4):
                    kgv = kg_b + (kb - b0)
                    cs = 64 * max(0, kb - 8 * i4)

                    def f(e, it=it, hd_=hd_, kb=kb, kgv=kgv, cs=cs,
                          nkb=nkb):
                        if kb % 2 == 0:
                            W(e, "cS", exp_tok[kgv])
                        if kgv in m_tok:
                            W(e, "cV", m_tok[kgv])
                        pse = psC if it % 2 == 0 else psD
                        if kb == 0 and it >= 2:
                            W(e, "cV", t_on[it - 2])
                        sP = kgv % 8
                        i = e.matmul(pse[0:65, cs:512],
                                     vbuf[:, 1040 * kb + 65 * hd_:
                                          1040 * kb + 65 * hd_ + 65],
                                     pr[:, 512 * sP + cs:512 * (sP + 1)],
                                     start=(kb == 0), stop=(kb == nkb - 1))
                        inc(i, "cT")
                    on_ten(f)
                    av_tok.append(tok("cT"))
            t_avL[it] = av_tok[-1]
            t_expL[it] = exp_tok[-1]

            def f(e, it=it, ta=t_avL[it]):
                W(e, "cT", ta)
                pse = psC if it % 2 == 0 else psD
                i = e.reciprocal(rlsb[64:65, :], pse[64:65, 0:512],
                                 _alp="bf16 rl operand for PE broadcast")
                inc(i, "cV")
            on_vec(f)
            t_rl = tok("cV")

            def f(e, t_rl=t_rl, it=it):
                W(e, "cV", t_rl)
                pse = psC if it % 2 == 0 else psD
                i = e.matmul(pse[:, 512:1024], ones_b[64:65, :],
                             rlsb[64:65, 0:512], start=True, stop=True)
                inc(i, "cT")
            on_ten(f)
            t_rlb = tok("cT")

            def f(e, it=it, sl=sl, t_rlb=t_rlb):
                W(e, "cT", t_rlb)
                pse = psC if it % 2 == 0 else psD
                if it >= 2:
                    W(e, "qE0" if sl == 0 else "qE1", t_odma[it - 2])
                e.tensor_copy(rlbs[:], pse[:, 512:1024])
                e.drain()
                i = e.tensor_mul(onsb[sl][:], pse[0:64, 0:512],
                                 rlbs[0:64, :])
                inc(i, "cV")
            on_vec(f)
            t_on[it] = tok("cV")

            def f(e, it=it, sl=sl, i4=i4, hd_=hd_, ton=t_on[it]):
                W(e, "cV", ton)
                pb = 64 * (hd_ % 2)
                cb = 2048 * (hd_ // 2) + 512 * i4
                qs = "qE0" if sl == 0 else "qE1"
                W(e, qs, cnt[qs])
                i = e.dma_start(obuf[pb:pb + 64, cb:cb + 512], onsb[sl][:])
                inc(i, qs)
            on_gp(f)
            t_odma[it] = tok("qE0" if sl == 0 else "qE1")
            emit_ws_job()
            if nws_box[0] >= len(ws_jobs):
                emit_owb()
            it += 1
    t_attn_d0, t_attn_d1 = tok("qE0"), tok("qE1")
    emit_owb()
    t_C2_cS = tok("cS")
    t_C2_qD0, t_C2_qD1 = tok("qD0"), tok("qD1")

    def f(e, tws=t_wsacc[len(ws_jobs) - 1]):
        W(e, "cV", tws)
        W(e, "cS", t_expL[63])
        i = e.matmul(psA[0:1, 0:48], ones_f[:, 0:1], wstat[:, 0:48],
                     start=True, stop=True)
        inc(i, "cT")
    on_ten(f)
    t_wsmm = tok("cT")

    def f(e, t_wsmm=t_wsmm):
        W(e, "cT", t_wsmm)
        for wi in range(3):
            e.reduce_sum(wstat[0:1, wi:wi + 1],
                         psA[0:1, 16 * wi:16 * (wi + 1)], axis=AX)
        e.drain()
        i = e.tensor_scalar(wstat[0:1, 0:3], wstat[0:1, 0:3],
                            1.0 / (D * DI), 1e-5, Alu.mult, Alu.max)
        inc(i, "cV")
    on_vec(f)
    t_wmean = tok("cV")

    def f(e, t_wmean=t_wmean):
        W(e, "cV", t_wmean)
        i = e.matmul(psA[:, 512:515], ones_f[0:1, :], wstat[0:1, 0:3],
                     start=True, stop=True)
        inc(i, "cT")
    on_ten(f)
    t_wbc = tok("cT")

    def f(e, t_wbc=t_wbc):
        W(e, "cT", t_wbc)
        e.tensor_copy(wsc[:, 3:6], psA[:, 512:515])
        i = e.reciprocal(wsc[:, 0:3], psA[:, 512:515])
        inc(i, "cV")
    on_vec(f)
    t_wsc = tok("cV")

    flush()

    t_av_last = t_avL[63]
    t_on_last = t_on[63]
    t_exp_last = t_expL[63]

    def f(e):
        W(e, "qB0", t_xowb[7])
        W(e, "cT", t_av_last)
        W(e, "cV", t_vall)
        for c in range(8):
            W(e, "qX0", cnt["qX0"])
            inc(e.dma_start(s2[:, OWN * c:OWN * (c + 1)],
                            xod[128 * c:128 * (c + 1), :]), "qX0")
    on_sync(f)
    t_xo_ld = tok("qX0")

    t_d1ev = {}
    nd1 = 0
    for oc in range(8):
        for tg in range(4):
            bk = nd1 % 4

            def f(e, oc=oc, tg=tg, bk=bk, nd1=nd1):
                if nd1 == 0:
                    W(e, "qW1", t_owb[0])
                    W(e, "qE0", t_attn_d0)
                    W(e, "qE1", t_attn_d1)
                    W(e, "cS", t_exp_last)
                    W(e, "cV", t_wsc)
                if nd1 >= 4:
                    W(e, "cV", t_d1ev[nd1 - 4])
                for kc in range(8):
                    i = e.matmul(
                        psA[:, 512 * bk:512 * (bk + 1)],
                        owb[:, D * kc + 128 * oc:D * kc + 128 * (oc + 1)],
                        obuf[:, OWN * kc + 512 * tg:OWN * kc + 512 * (tg + 1)],
                        start=(kc == 0), stop=(kc == 7))
                inc(i, "cT")
            on_ten(f)
            t_mm = tok("cT")

            def f(e, oc=oc, tg=tg, bk=bk, t_mm=t_mm, nd1=nd1):
                W(e, "cT", t_mm)
                if nd1 == 0:
                    W(e, "qX0", t_xo_ld)
                sl_ = s2[:, OWN * oc + 512 * tg:OWN * oc + 512 * (tg + 1)]
                i = e.scalar_tensor_tensor(sl_, psA[:, 512 * bk:512 * (bk + 1)],
                                           obc[:, oc:oc + 1], sl_,
                                           Alu.add, Alu.add)
                inc(i, "cV")
            on_vec(f)
            t_d1ev[nd1] = tok("cV")
            nd1 += 1
    t_s2 = tok("cV")
    t_s2mm_last = tok("cT")

    t_d1b = {}
    nb1 = 0
    for tt in range(16):
        for og in range(2):
            ps = psC if nb1 % 2 == 0 else psD

            def f(e, tt=tt, og=og, ps=ps, nb1=nb1):
                if nb1 == 0:
                    W(e, "cV", t_on_last)
                    W(e, "cV", t_wsc)
                if nb1 >= 2:
                    W(e, "cV", t_d1b[nb1 - 2])
                for kc in range(8):
                    e.matmul(
                        ps[:, 0:512],
                        obuf[:, OWN * kc + 128 * tt:OWN * kc + 128 * (tt + 1)],
                        owb[:, D * kc + 512 * og:D * kc + 512 * (og + 1)],
                        start=(kc == 0), stop=False)
                i = e.matmul(ps[:, 0:512], ones_b[0:1, :],
                             obb2_b[0:1, 512 * og:512 * (og + 1)],
                             start=False, stop=True)
                inc(i, "cT")
            on_ten(f)
            t_mm = tok("cT")

            def f(e, tt=tt, og=og, ps=ps, t_mm=t_mm, nb1=nb1):
                W(e, "cT", t_mm)
                if nb1 == 0:
                    W(e, "cS", t_C2_cS)
                    W(e, "qD0", t_C2_qD0)
                    W(e, "qD1", t_C2_qD1)
                i = e.tensor_copy(
                    bo[:, D * tt + 512 * og:D * tt + 512 * (og + 1)],
                    ps[:, 0:512])
                inc(i, "cV")
            on_vec(f)
            t_d1b[nb1] = tok("cV")
            nb1 += 1
    t_bo1 = tok("cV")
    t_bo1_mm = tok("cT")

    flush()

    t_sq2, t_ms2, t_ms2mm = {}, {}, {}
    nms = 0
    for oc in range(8):
        def f(e, oc=oc):
            if oc == 0:
                W(e, "cV", t_s2)
                W(e, "cT", t_av_last)
            else:
                W(e, "cT", t_ms2mm[oc - 1])
            i = e.activation(sq2s[:], s2[:, OWN * oc:OWN * (oc + 1)],
                             Act.Square)
            inc(i, "cS")
        on_scal(f)
        t_sq2[oc] = tok("cS")

        for g in range(4):
            def f(e, oc=oc, g=g, ts=t_sq2[oc], nms=nms):
                if g == 0:
                    W(e, "cS", ts)
                if nms == 0:
                    W(e, "cV", t_bo1)
                if nms >= 2:
                    W(e, "cV", t_ms2[nms - 2])
                i = e.matmul(psC[0:1, 512 * (nms % 2):512 * (nms % 2) + 512],
                             ones_b[:, 0:1], sq2s[:, 512 * g:512 * (g + 1)],
                             start=True, stop=True)
                inc(i, "cT")
            on_ten(f)
            t_pe = tok("cT")

            def f(e, oc=oc, g=g, t_pe=t_pe, nms=nms):
                W(e, "cT", t_pe)
                if oc == 0:
                    i = e.tensor_copy(r2row[0:1, 512 * g:512 * (g + 1)],
                                      psC[0:1, 512 * (nms % 2):512 * (nms % 2) + 512])
                else:
                    i = e.tensor_add(r2row[0:1, 512 * g:512 * (g + 1)],
                                     r2row[0:1, 512 * g:512 * (g + 1)],
                                     psC[0:1, 512 * (nms % 2):512 * (nms % 2) + 512])
                inc(i, "cV")
            on_vec(f)
            t_ms2[nms] = tok("cV")
            nms += 1
        t_ms2mm[oc] = tok("cT")

    def f(e, tm=t_ms2[nms - 1]):
        W(e, "cV", tm)
        i = e.activation(r2row[:], r2row[:], Act.Sqrt, bias=1e-6,
                         scale=1.0 / D)
        inc(i, "cS")
    on_scal(f)
    t_r2s = tok("cS")

    def f(e, t_r2s=t_r2s):
        W(e, "cS", t_r2s)
        i = e.reciprocal(r2row[:], r2row[:])
        inc(i, "cV")
    on_vec(f)
    t_r2 = tok("cV")

    t_r2c = {}
    for g in range(4):
        def f(e, g=g, t_r2=t_r2):
            if g == 0:
                W(e, "cV", t_r2)
                W(e, "cV", t_s2)
            i = e.matmul(psA[:, 512 * g:512 * (g + 1)], ones_f[0:1, :],
                         r2row[0:1, 512 * g:512 * (g + 1)],
                         start=True, stop=True)
            inc(i, "cT")
        on_ten(f)
        t_pe = tok("cT")

        def f(e, g=g, t_pe=t_pe):
            W(e, "cT", t_pe)
            i = e.tensor_copy(r2b[:, 512 * g:512 * (g + 1)],
                              psA[:, 512 * g:512 * (g + 1)])
            inc(i, "cV")
        on_vec(f)
        t_r2c[g] = tok("cV")

    t_n2wb = {}
    for oc in range(8):
        def f(e, oc=oc):
            if oc == 0:
                W(e, "cV", t_r2c[3])
            i = e.scalar_tensor_tensor(s2[:, OWN * oc:OWN * (oc + 1)],
                                       s2[:, OWN * oc:OWN * (oc + 1)],
                                       ln2c[:, oc:oc + 1], r2b[:],
                                       Alu.mult, Alu.mult)
            inc(i, "cV")
        on_vec(f)
        t_n2 = tok("cV")

        def f(e, oc=oc, t_n2=t_n2):
            W(e, "cV", t_n2)
            W(e, "qZ0", cnt["qZ0"])
            i = e.dma_start(n2d[128 * oc:128 * (oc + 1), :],
                            s2[:, OWN * oc:OWN * (oc + 1)])
            inc(i, "qZ0")
        on_gp(f)
        t_n2wb[oc] = tok("qZ0")
    t_n2all = tok("qZ0")

    t_D2_cS = tok("cS")

    def f(e, t_mpb=t_mpb):
        W(e, "cV", t_mpb)
        W(e, "cT", t_av_last)
        W(e, "cS", t_D2_cS)
        for n_ in range(8):
            for m_ in range(8):
                i = e.tensor_scalar(
                    bigm[:, D * n_ + 128 * m_:D * n_ + 128 * (m_ + 1)],
                    id128[:], mpb[:, 8 * n_ + m_:8 * n_ + m_ + 1],
                    None, Alu.mult)
        inc(i, "cV")
    on_vec(f)
    t_bigm = tok("cV")

    flush()

    wq_jobs = []
    for wi, wn in enumerate(["g", "u", "d"]):
        if wn != "d":
            for ch in range(8):
                for qf in range(4):
                    wq_jobs.append((wn, wi, ch, qf, 1024))
        else:
            for ch in range(32):
                wq_jobs.append((wn, wi, ch, 0, 1024))
    nqw_box = [0]
    t_wqv, t_wqwb, t_wqld = {}, {}, {}

    def _wq_load(j):
        if j >= len(wq_jobs) or j in t_wqld:
            return
        wn, wi, ch, qf, wd = wq_jobs[j]

        def f(e, wn=wn, ch=ch, qf=qf, wd=wd, j=j):
            if j == 0:
                W(e, "cT", t_bo1_mm)
                W(e, "qW0", t_wv)
                W(e, "qW1", t_owb[0])
            if j >= 2:
                W(e, "cV", t_wqv[j - 2])
            W(e, "qF1", cnt["qF1"])
            i = e.dma_start(wqc2[j % 2][:, 0:wd],
                            wfd[wn][128 * ch:128 * (ch + 1),
                                    1024 * qf:1024 * qf + wd])
            inc(i, "qF1")
        on_gp(f)
        t_wqld[j] = tok("qF1")

    def emit_wq_jobs(kjobs):
        for _ in range(kjobs):
            if nqw_box[0] >= len(wq_jobs):
                return
            nqw = nqw_box[0]
            if nqw == 0:
                _wq_load(0)
            _wq_load(nqw + 1)
            wn, wi, ch, qf, wd = wq_jobs[nqw]

            def f(e, wi=wi, wd=wd, nqw=nqw):
                W(e, "qF1", t_wqld[nqw])
                if nqw == 0:
                    W(e, "cV", t_wsc)
                if nqw >= 1:
                    W(e, "qZ0" if (nqw - 1) % 2 == 0 else "qZ1",
                      t_wqwb[nqw - 1])
                wqs = wqc2[nqw % 2]
                e.tensor_scalar(wqs[:, 0:wd], wqs[:, 0:wd],
                                wsc[:, wi:wi + 1], MAGIC, Alu.mult, Alu.add)
                e.drain()
                e.tensor_scalar(wqs[:, 0:wd], wqs[:, 0:wd],
                                MAGIC, -1.0, Alu.subtract, Alu.max)
                e.drain()
                i = e.tensor_scalar(wqo[:, 0:wd], wqs[:, 0:wd],
                                    1.0, None, Alu.min)
                inc(i, "cV")
            on_vec(f)
            t_wqv[nqw] = tok("cV")

            def f(e, wn=wn, ch=ch, qf=qf, wd=wd, tv=t_wqv[nqw], nqw=nqw):
                W(e, "cV", tv)
                qs = "qZ0" if nqw % 2 == 0 else "qZ1"
                if nqw == 0:
                    W(e, "qZ0", t_n2all)
                W(e, qs, cnt[qs])
                i = e.dma_start(wqd[wn][128 * ch:128 * (ch + 1),
                                        1024 * qf:1024 * qf + wd],
                                wqo[:, 0:wd])
                inc(i, qs)
            on_gp(f)
            t_wqwb[nqw] = tok("qZ0" if nqw % 2 == 0 else "qZ1")
            nqw_box[0] += 1

    t_w1mm_last = {0: 0, 1: 0}
    t_w2mm_last = {0: 0, 1: 0}
    t_gelu = {}
    t_boev = {}
    t_n2ld, t_xold = {}, {}
    t_w1ld, t_w2ld = {}, {}
    nbo = 0
    ngel = 0
    for half in range(2):
        def f(e, half=half):
            if half == 0:
                W(e, "cT", t_bo1_mm)
                W(e, "qZ0", t_n2all)
                W(e, "cV", t_bigm)
                W(e, "qE0", t_attn_d0)
                W(e, "qE1", t_attn_d1)
            else:
                W(e, "cT", t_w1mm_last[0])
                W(e, "cT", t_w2mm_last[0])
            for kc in range(8):
                W(e, "qW0", cnt["qW0"])
                inc(e.dma_start(w1h[:, 2048 * kc:2048 * (kc + 1)],
                                w1Td[128 * kc:128 * (kc + 1),
                                     2048 * half:2048 * (half + 1)]), "qW0")
            for j in range(16):
                W(e, "qW1", cnt["qW1"])
                inc(e.dma_start(w2h[:, D * j:D * (j + 1)],
                                w2Td[2048 * half + 128 * j:
                                     2048 * half + 128 * (j + 1), :]),
                    "qW1")
        on_gp(f)
        t_w1ld[half] = tok("qW0")
        t_w2ld[half] = tok("qW1")

        for g in range(8):
            gi = half * 8 + g
            sl = g % 2

            def f(e, half=half, g=g, gi=gi, sl=sl):
                if gi == 0:
                    W(e, "qZ0", t_n2all)
                    W(e, "cT", t_av_last)
                    W(e, "cV", t_wsacc[47])
                if gi >= 2:
                    W(e, "cT", t_w1mm_last_by_g[gi - 2])
                W(e, "qX1", cnt["qX1"])
                i = e.dma_start(
                    n2s[sl][:, :].rearrange("p (c j) -> p c j", j=256),
                    bass.AP(n2d, 256 * g,
                            [[OWN, 128], [128 * OWN, 8], [1, 256]]))
                inc(i, "qX1")
                if half == 1:
                    if gi == 8:
                        W(e, "qW0", t_wv)
                        W(e, "cS", t_qkev[95])
                    if gi >= 2:
                        W(e, "cT", t_mhc_last_by_g.get(gi - 2, 0))
                    W(e, "qX0", cnt["qX0"])
                    i = e.dma_start(
                        xosr[sl][:, :].rearrange("p (c j) -> p c j", j=256),
                        bass.AP(xod, 256 * g,
                                [[OWN, 128], [128 * OWN, 8], [1, 256]]))
                    inc(i, "qX0")
            if half == 0 and g == 0:
                t_w1mm_last_by_g = {}
                t_mhc_last_by_g = {}
            on_sync(f)
            t_n2ld[gi] = tok("qX1")
            t_xold[gi] = tok("qX0")

            for hc in range(16):
                bk = (gi * 16 + hc) % 4

                def f(e, half=half, g=g, gi=gi, sl=sl, hc=hc, bk=bk):
                    if hc == 0:
                        W(e, "qW0", t_w1ld[half])
                        W(e, "qX1", t_n2ld[gi])
                        if gi == 0:
                            W(e, "cV", t_r2c[3])
                    nglob = gi * 16 + hc
                    if nglob >= 4:
                        W(e, "cS", t_gelu[nglob - 4])
                    for kc in range(8):
                        i = e.matmul(
                            psA[:, 512 * bk:512 * bk + 256],
                            w1h[:, 2048 * kc + 128 * hc:
                                 2048 * kc + 128 * (hc + 1)],
                            n2s[sl][:, 256 * kc:256 * (kc + 1)],
                            start=(kc == 0), stop=(kc == 7))
                    inc(i, "cT")
                on_ten(f)
                t_w1mm = tok("cT")
                t_w1mm_last[half] = t_w1mm
                t_w1mm_last_by_g[gi] = t_w1mm

                def f(e, half=half, gi=gi, hc=hc, bk=bk, t_w1mm=t_w1mm,
                      ngel=ngel):
                    W(e, "cT", t_w1mm)
                    if ngel == 0:
                        W(e, "cV", t_bo1)
                    if ngel >= 16:
                        W(e, "cT", t_w2mm_by_g.get(gi - 1, 0))
                    i = e.activation(hth[:, 256 * hc:256 * (hc + 1)],
                                     psA[:, 512 * bk:512 * bk + 256],
                                     Act.Gelu,
                                     bias=b1c[:, 16 * half + hc:
                                              16 * half + hc + 1])
                    inc(i, "cS")
                if half == 0 and g == 0 and hc == 0:
                    t_w2mm_by_g = {}
                on_scal(f)
                t_gelu[gi * 16 + hc] = tok("cS")
                ngel += 1

            for tt2 in range(2):
                for og in range(2):
                    psel = nbo % 4
                    ps = (psC if psel < 2 else psD)
                    pcol = 512 * (psel % 2)

                    def f(e, half=half, g=g, gi=gi, sl=sl, tt2=tt2, og=og,
                          ps=ps, pcol=pcol, nbo=nbo):
                        if tt2 == 0 and og == 0:
                            W(e, "cS", t_gelu[gi * 16 + 15])
                            W(e, "qW1", t_w2ld[half])
                            if half == 1:
                                W(e, "qX0", t_xold[gi])
                            if gi == 0:
                                W(e, "cV", t_bo1)
                        if nbo >= 4:
                            W(e, "cV", t_boev[nbo - 4])
                        for kc2 in range(16):
                            i = e.matmul(
                                ps[:, pcol:pcol + 512],
                                hth[:, 256 * kc2 + 128 * tt2:
                                     256 * kc2 + 128 * (tt2 + 1)],
                                w2h[:, D * kc2 + 512 * og:
                                     D * kc2 + 512 * (og + 1)],
                                start=(kc2 == 0),
                                stop=(half == 0 and kc2 == 15))
                        if half == 1:
                            for kc3 in range(8):
                                i = e.matmul(
                                    ps[:, pcol:pcol + 512],
                                    xosr[sl][:, 256 * kc3 + 128 * tt2:
                                             256 * kc3 + 128 * (tt2 + 1)],
                                    bigm[:, D * kc3 + 512 * og:
                                         D * kc3 + 512 * (og + 1)],
                                    start=False, stop=(kc3 == 7))
                        inc(i, "cT")
                    on_ten(f)
                    t_w2mm = tok("cT")
                    t_w2mm_last[half] = t_w2mm
                    t_w2mm_by_g[gi] = t_w2mm
                    if half == 1:
                        t_mhc_last_by_g[gi] = t_w2mm

                    def f(e, half=half, g=g, tt2=tt2, og=og, ps=ps,
                          pcol=pcol, t_w2mm=t_w2mm):
                        W(e, "cT", t_w2mm)
                        tt = 2 * g + tt2
                        dst = bo[:, D * tt + 512 * og:D * tt + 512 * (og + 1)]
                        if half == 0:
                            i = e.tensor_add(dst, dst, ps[:, pcol:pcol + 512])
                        else:
                            i = e.tensor_add(dst, dst, ps[:, pcol:pcol + 512])
                        inc(i, "cV")
                    on_vec(f)
                    t_boev[nbo] = tok("cV")
                    nbo += 1
            emit_wq_jobs(6)
    t_d3 = tok("cV")
    t_d3mm = tok("cT")

    flush()

    t_x1wb, t_z2wb = {}, {}
    t_e1v = {}
    for tt in range(16):
        sl = tt % 2
        qx = "qX0" if sl == 0 else "qX1"

        def f(e, tt=tt, sl=sl, qx=qx):
            if tt == 0:
                W(e, "cV", t_d3)
                W(e, "cT", t_d3mm)
                W(e, "qW0", t_w1ld[1])
            if tt >= 2:
                W(e, "cV", t_e1v[tt - 2])
            W(e, qx, cnt[qx])
            i = e.dma_start(x0o[sl][:],
                            x0ownd[128 * tt:128 * (tt + 1), :])
            inc(i, qx)
        on_sync(f)
        t_x0ld = tok(qx)

        def f(e, tt=tt, sl=sl):
            if tt == 0:
                W(e, "cV", t_d3)
            bos = bo[:, D * tt:D * (tt + 1)]
            i = e.activation(zt[sl][:], bos, Act.Square,
                             accum_out=stq[:, sl:sl + 1])
            inc(i, "cS")
        on_scal(f)
        t_sq = tok("cS")

        def f(e, tt=tt, sl=sl, t_sq=t_sq, t_x0ld=t_x0ld, qx=qx):
            bos = bo[:, D * tt:D * (tt + 1)]
            e.reduce_sum(stm[:, sl:sl + 1], bos, axis=AX)
            e.drain()
            e.tensor_scalar(stm[:, sl:sl + 1], stm[:, sl:sl + 1], 1.0 / D,
                            None, Alu.mult)
            e.drain()
            e.tensor_mul(stq2[:, sl:sl + 1], stm[:, sl:sl + 1],
                         stm[:, sl:sl + 1])
            e.drain()
            W(e, "cS", t_sq)
            i = e.scalar_tensor_tensor(stv[:, sl:sl + 1], stq[:, sl:sl + 1],
                                       1.0 / D, stq2[:, sl:sl + 1],
                                       Alu.mult, Alu.subtract)
            inc(i, "cV")
        on_vec(f)
        t_var = tok("cV")

        def f(e, sl=sl, t_var=t_var):
            W(e, "cV", t_var)
            i = e.activation(stv[:, sl:sl + 1], stv[:, sl:sl + 1],
                             Act.Sqrt, bias=1e-5)
            inc(i, "cS")
        on_scal(f)
        t_r = tok("cS")

        def f(e, tt=tt, sl=sl, t_r=t_r, t_x0ld=t_x0ld, qx=qx):
            W(e, "cS", t_r)
            W(e, qx, t_x0ld)
            if tt >= 2:
                W(e, "qZ0" if sl == 0 else "qZ1", t_x1wb[tt - 2])
            e.reciprocal(stv[:, sl:sl + 1], stv[:, sl:sl + 1])
            e.drain()
            bos = bo[:, D * tt:D * (tt + 1)]
            e.tensor_scalar(x1t[sl][:], bos, stm[:, sl:sl + 1],
                            stv[:, sl:sl + 1], Alu.subtract, Alu.mult)
            e.drain()
            i = e.tensor_add(x1t[sl][:], x1t[sl][:], x0o[sl][:])
            inc(i, "cV")
        on_vec(f)
        t_x1 = tok("cV")

        def f(e, tt=tt, sl=sl, t_x1=t_x1):
            W(e, "cV", t_x1)
            qs = "qZ0" if sl == 0 else "qZ1"
            if tt == 0:
                W(e, "qZ0", t_wqwb[len(wq_jobs) - 2])
            if tt == 1:
                W(e, "qZ1", t_wqwb[len(wq_jobs) - 1])
            W(e, qs, cnt[qs])
            i = e.dma_start(x1d[128 * tt:128 * (tt + 1), :], x1t[sl][:])
            inc(i, qs)
        on_gp(f)
        t_x1wb[tt] = tok("qZ0" if sl == 0 else "qZ1")

        def f(e, tt=tt, sl=sl, t_x1=t_x1):
            W(e, "cV", t_x1)
            if tt >= 2:
                W(e, "cV", t_e1v[tt - 2])
            i = e.activation(zt[sl][:], x1t[sl][:], Act.Square,
                             accum_out=stq[:, sl:sl + 1])
            inc(i, "cS")
        on_scal(f)
        t_sqz = tok("cS")

        def f(e, sl=sl, t_sqz=t_sqz):
            e.reduce_sum(stm[:, sl:sl + 1], x1t[sl][:], axis=AX)
            e.drain()
            e.tensor_scalar(stm[:, sl:sl + 1], stm[:, sl:sl + 1], 1.0 / D,
                            None, Alu.mult)
            e.drain()
            e.tensor_mul(stq2[:, sl:sl + 1], stm[:, sl:sl + 1],
                         stm[:, sl:sl + 1])
            e.drain()
            W(e, "cS", t_sqz)
            i = e.scalar_tensor_tensor(stv[:, sl:sl + 1], stq[:, sl:sl + 1],
                                       1.0 / D, stq2[:, sl:sl + 1],
                                       Alu.mult, Alu.subtract)
            inc(i, "cV")
        on_vec(f)
        t_varz = tok("cV")

        def f(e, sl=sl, t_varz=t_varz):
            W(e, "cV", t_varz)
            i = e.activation(stv[:, sl:sl + 1], stv[:, sl:sl + 1],
                             Act.Sqrt, bias=1e-5)
            inc(i, "cS")
        on_scal(f)
        t_rz = tok("cS")

        def f(e, tt=tt, sl=sl, t_rz=t_rz):
            W(e, "cS", t_rz)
            if tt >= 2:
                W(e, "qV0" if sl == 0 else "qV1", t_z2wb[tt - 2])
            e.reciprocal(stv[:, sl:sl + 1], stv[:, sl:sl + 1])
            e.drain()
            i = e.tensor_scalar(zt[sl][:], x1t[sl][:], stm[:, sl:sl + 1],
                                stv[:, sl:sl + 1], Alu.subtract, Alu.mult)
            e.drain()
            i = e.tensor_reduce(stq[:, sl:sl + 1], zt[sl][:], AX,
                                Alu.max, apply_absolute_value=True)
            e.drain()
            e.tensor_scalar(stq[:, sl:sl + 1], stq[:, sl:sl + 1], 1e-5,
                            1.0 / 127.0, Alu.max, Alu.mult)
            e.drain()
            e.tensor_copy(zscin[:, tt:tt + 1], stq[:, sl:sl + 1])
            e.reciprocal(stq2[:, sl:sl + 1], stq[:, sl:sl + 1])
            e.drain()
            e.tensor_scalar(zt[sl][:], zt[sl][:], stq2[:, sl:sl + 1],
                            MAGIC, Alu.mult, Alu.add)
            e.drain()
            e.tensor_scalar(zt[sl][:], zt[sl][:], MAGIC, -128.0,
                            Alu.subtract, Alu.max)
            e.drain()
            i = e.tensor_scalar(z2qs[sl][:], zt[sl][:], 127.0, None,
                                Alu.min)
            inc(i, "cV")
        on_vec(f)
        t_e1v[tt] = tok("cV")

        def f(e, tt=tt, sl=sl, tv=t_e1v[tt]):
            W(e, "cV", tv)
            qs = "qV0" if sl == 0 else "qV1"
            W(e, qs, cnt[qs])
            i = e.dma_start(z1d[128 * tt:128 * (tt + 1), :], z2qs[sl][:])
            inc(i, qs)
        on_gp(f)
        t_z2wb[tt] = tok("qV0" if sl == 0 else "qV1")
    t_e1 = tok("cV")
    t_z2wb_all = (tok("qV0"), tok("qV1"))
    t_x1wb_all = (tok("qZ0"), tok("qZ1"))
    t_e1_cS = tok("cS")
    t_e1_qX0, t_e1_qX1 = tok("qX0"), tok("qX1")

    flush()

    def f(e):
        W(e, "qV0", t_z2wb_all[0])
        W(e, "qV1", t_z2wb_all[1])
        W(e, "cV", t_e1)
        W(e, "cT", t_d3mm)
        W(e, "qW1", t_w2ld[1])
        W(e, "qX1", t_n2ld[15])
        W(e, "cS", t_gelu[255])
        for c in range(8):
            W(e, "qX0", cnt["qX0"])
            inc(e.dma_start(z2qT[:, OWN * c:OWN * (c + 1)],
                            z1d[0:OWN, 128 * c:128 * (c + 1)],
                            transpose=True), "qX0")
    on_sync(f)
    t_z2qT = tok("qX0")

    def f(e):
        W(e, "cV", t_e1)
        W(e, "cV", t_wsc)
        e.tensor_scalar(invgc[:], zscin[:], wsc[:, 3:4], None, Alu.mult)
        i = e.tensor_scalar(invuc[:], zscin[:], wsc[:, 4:5], None, Alu.mult)
        inc(i, "cV")
    on_vec(f)
    t_invg = tok("cV")

    t_wgld, t_ev3, t_pwb = {}, {}, {}
    ne3 = 0
    for ig in range(8):
        sl = ig % 2

        def f(e, ig=ig, sl=sl):
            if ig == 0:
                W(e, "qZ0", t_wqwb[len(wq_jobs) - 2])
                W(e, "qZ1", t_wqwb[len(wq_jobs) - 1])
                W(e, "cT", t_d3mm)
                W(e, "cV", t_d3)
                W(e, "cS", t_gelu[255])
                W(e, "qX1", t_n2ld[15])
            if ig >= 2:
                W(e, "cT", t_e3mm_by_ig[ig - 2])
            W(e, "qW0", cnt["qW0"])
            i = e.dma_start(
                wgs[sl][:, :].rearrange("p (c j) -> p c j", j=512),
                bass.AP(wqd["g"], 512 * ig,
                        [[DI, 128], [128 * DI, 8], [1, 512]]))
            inc(i, "qW0")
            W(e, "qW1", cnt["qW1"])
            i = e.dma_start(
                wus[sl][:, :].rearrange("p (c j) -> p c j", j=512),
                bass.AP(wqd["u"], 512 * ig,
                        [[DI, 128], [128 * DI, 8], [1, 512]]))
            inc(i, "qW1")
        if ig == 0:
            t_e3mm_by_ig = {}
        on_gp(f)
        t_wgld[ig] = (tok("qW0"), tok("qW1"))

        for tt in range(16):
            pse = psC if ne3 % 2 == 0 else psD

            def f(e, ig=ig, sl=sl, tt=tt, ne3=ne3, pse=pse):
                if tt == 0:
                    W(e, "qW0", t_wgld[ig][0])
                    W(e, "qW1", t_wgld[ig][1])
                    if ig == 0:
                        W(e, "qX0", t_z2qT)
                        W(e, "cV", t_e1)
                if ne3 >= 2:
                    W(e, "cV", t_ev3[ne3 - 2])
                for kc in range(8):
                    e.matmul(pse[:, 0:512],
                             z2qT[:, OWN * kc + 128 * tt:
                                  OWN * kc + 128 * (tt + 1)],
                             wgs[sl][:, 512 * kc:512 * (kc + 1)],
                             start=(kc == 0), stop=(kc == 7))
                for kc in range(8):
                    i = e.matmul(pse[:, 512:1024],
                                 z2qT[:, OWN * kc + 128 * tt:
                                      OWN * kc + 128 * (tt + 1)],
                                 wus[sl][:, 512 * kc:512 * (kc + 1)],
                                 start=(kc == 0), stop=(kc == 7))
                inc(i, "cT")
            on_ten(f)
            t_mm = tok("cT")
            t_e3mm_by_ig[ig] = t_mm

            def f(e, tt=tt, t_mm=t_mm, pse=pse, ig=ig, ne3=ne3):
                W(e, "cT", t_mm)
                if ig == 0 and tt == 0:
                    W(e, "cV", t_invg)
                if ne3 >= 1:
                    W(e, "cV", t_ev3[ne3 - 1])
                i = e.activation(gsb[:], pse[:, 0:512], Act.Silu,
                                 scale=invgc[:, tt:tt + 1])
                inc(i, "cS")
            on_scal(f)
            t_silu = tok("cS")

            def f(e, ig=ig, tt=tt, ne3=ne3, t_silu=t_silu, pse=pse):
                W(e, "cS", t_silu)
                if ne3 >= 2:
                    W(e, "qZ0" if ne3 % 2 == 0 else "qZ1", t_pwb[ne3 - 2])
                sl2 = ne3 % 2
                e.scalar_tensor_tensor(psb[sl2][:], pse[:, 512:1024],
                                       invuc[:, tt:tt + 1], gsb[:],
                                       Alu.mult, Alu.mult)
                e.drain()
                col = 16 * ig + tt
                i = e.tensor_reduce(pmxall[:, col:col + 1], psb[sl2][:], AX,
                                    Alu.max, apply_absolute_value=True)
                inc(i, "cV")
            on_vec(f)
            t_ev3[ne3] = tok("cV")

            def f(e, ig=ig, tt=tt, ne3=ne3, tv=t_ev3[ne3]):
                W(e, "cV", tv)
                qs = "qZ0" if ne3 % 2 == 0 else "qZ1"
                if ne3 == 0:
                    W(e, "qZ0", t_x1wb_all[0])
                if ne3 == 1:
                    W(e, "qZ1", t_x1wb_all[1])
                W(e, qs, cnt[qs])
                i = e.dma_start(pdd[128 * tt:128 * (tt + 1),
                                    512 * ig:512 * (ig + 1)],
                                psb[ne3 % 2][:])
                inc(i, qs)
            on_gp(f)
            t_pwb[ne3] = tok("qZ0" if ne3 % 2 == 0 else "qZ1")
            ne3 += 1
    t_e3 = tok("cV")
    t_e3_cS = tok("cS")

    flush()

    def f(e, t_e3=t_e3):
        W(e, "cV", t_e3)
        vw = pmxall[:, :].rearrange("p (i t) -> p t i", t=16)
        e.tensor_reduce(pscm[:], vw, AX, Alu.max)
        e.drain()
        e.tensor_scalar(pscm[:], pscm[:], 1e-5, 1.0 / 127.0, Alu.max,
                        Alu.mult)
        e.drain()
        e.tensor_scalar(psci[:], pscm[:], wsc[:, 5:6], None, Alu.mult)
        e.drain()
        i = e.reciprocal(pscm[:], pscm[:])
        inc(i, "cV")
    on_vec(f)
    t_psc = tok("cV")

    t_pld, t_pqv, t_pqwb = {}, {}, {}
    for tt in range(16):
        def f(e, tt=tt):
            if tt == 0:
                W(e, "qZ0", t_pwb[126])
                W(e, "qZ1", t_pwb[127])
            if tt >= 2:
                W(e, "cV", t_pqv[tt - 2])
            W(e, "qX1", cnt["qX1"])
            i = e.dma_start(ptile2[tt % 2][:], pdd[128 * tt:128 * (tt + 1), :])
            inc(i, "qX1")
        on_sync(f)
        t_pld[tt] = tok("qX1")

        def f(e, tt=tt, tl=t_pld[tt]):
            W(e, "qX1", tl)
            if tt == 0:
                W(e, "cV", t_psc)
            if tt >= 2:
                W(e, "qZ1", t_pqwb[tt - 2])
            e.tensor_scalar(pwide[:], ptile2[tt % 2][:], pscm[:, tt:tt + 1],
                            MAGIC, Alu.mult, Alu.add)
            e.drain()
            e.tensor_scalar(pwide[:], pwide[:], MAGIC, -128.0,
                            Alu.subtract, Alu.max)
            e.drain()
            i = e.tensor_scalar(pqtile2[tt % 2][:], pwide[:], 127.0, None,
                                Alu.min)
            inc(i, "cV")
        on_vec(f)
        t_pqv[tt] = tok("cV")

        def f(e, tt=tt, tv=t_pqv[tt]):
            W(e, "cV", tv)
            W(e, "qZ1", cnt["qZ1"])
            i = e.dma_start(pqd[128 * tt:128 * (tt + 1), :], pqtile2[tt % 2][:])
            inc(i, "qZ1")
        on_gp(f)
        t_pqwb[tt] = tok("qZ1")
    t_pq_all = tok("qZ1")

    flush()

    t_pqt, t_wdld, t_fev = {}, {}, {}
    nf = 0
    for icg in range(4):
        def f(e, icg=icg):
            if icg == 0:
                W(e, "qZ1", t_pq_all)
                W(e, "cV", t_e3)
                W(e, "cT", t_e3mm_by_ig[7])
                W(e, "cS", t_e3_cS)
            if icg >= 1:
                W(e, "cT", t_e5mm_by_icg[icg - 1])
            for ic in range(8):
                W(e, "qX0", cnt["qX0"])
                inc(e.dma_start(pqt[:, 2048 * ic:2048 * (ic + 1)],
                                pqd[:, 128 * (8 * icg + ic):
                                    128 * (8 * icg + ic + 1)],
                                transpose=True), "qX0")
        if icg == 0:
            t_e5mm_by_icg = {}
        on_sync(f)
        t_pqt[icg] = tok("qX0")

        def f(e, icg=icg):
            if icg == 0:
                W(e, "cV", t_e3)
                W(e, "cT", t_e3mm_by_ig[7])
            if icg >= 1:
                W(e, "cT", t_e5mm_by_icg[icg - 1])
            W(e, "qW0", cnt["qW0"])
            i = e.dma_start(
                wds[:, :].rearrange("p (c j) -> p c j", j=D),
                bass.AP(wqd["d"], 128 * 8 * icg * D,
                        [[D, 128], [128 * D, 8], [1, D]]))
            inc(i, "qW0")
        on_gp(f)
        t_wdld[icg] = tok("qW0")

        for tt in range(16):
            psel = nf % 2
            ps = psC if psel == 0 else psD

            def f(e, icg=icg, tt=tt, ps=ps, nf=nf):
                if tt == 0:
                    W(e, "qX0", t_pqt[icg])
                    W(e, "qW0", t_wdld[icg])
                if nf >= 2:
                    W(e, "cV", t_fev[nf - 2])
                for ic in range(8):
                    for og in range(2):
                        i = e.matmul(
                            ps[:, 512 * og:512 * (og + 1)],
                            pqt[:, 2048 * ic + 128 * tt:
                                2048 * ic + 128 * (tt + 1)],
                            wds[:, D * ic + 512 * og:D * ic + 512 * (og + 1)],
                            start=(ic == 0), stop=(ic == 7))
                inc(i, "cT")
            on_ten(f)
            t_mm = tok("cT")
            t_e5mm_by_icg[icg] = t_mm

            def f(e, icg=icg, tt=tt, ps=ps, t_mm=t_mm):
                W(e, "cT", t_mm)
                if icg == 0 and tt == 0:
                    W(e, "cS", t_e1_cS)
                    W(e, "qX0", t_e1_qX0)
                    W(e, "qX1", t_e1_qX1)
                dst = fsb[:, D * tt:D * (tt + 1)]
                if icg == 0:
                    i = e.tensor_copy(dst, ps[:, 0:1024])
                else:
                    i = e.tensor_add(dst, dst, ps[:, 0:1024])
                inc(i, "cV")
            on_vec(f)
            t_fev[nf] = tok("cV")
            nf += 1
    t_e5 = tok("cV")

    flush()

    t_x1ld, t_yv, t_ywb = {}, {}, {}
    for tt in range(16):
        sl = tt % 4
        qx = "qX1"
        qy = "qY0" if tt % 2 == 0 else "qY1"

        def f(e, tt=tt, sl=sl, qx=qx):
            if tt == 0:
                W(e, "qZ0", t_x1wb_all[0])
                W(e, "qZ1", t_x1wb_all[1])
                W(e, "cV", t_e5)
                W(e, "cT", t_e5mm_by_icg[3])
                W(e, "qW1", t_wgld[7][1])
                W(e, "qW0", t_wdld[3])
            if tt >= 4:
                W(e, "cV", t_yv[tt - 4])
            W(e, qx, cnt[qx])
            i = e.dma_start(x1r[sl][:], x1d[128 * tt:128 * (tt + 1), :])
            inc(i, qx)
        on_sync(f)
        t_x1ld[tt] = tok(qx)

        def f(e, tt=tt, sl=sl):
            fslice = fsb[:, D * tt:D * (tt + 1)]
            i = e.tensor_scalar(fslice, fslice, psci[:, tt:tt + 1], None,
                                Alu.mult, Alu.add,
                                accum_out=stm[:, sl:sl + 1])
            inc(i, "cV")
        on_vec(f)
        t_fs = tok("cV")

        def f(e, tt=tt, sl=sl, t_fs=t_fs, qy=qy):
            W(e, "cV", t_fs)
            if tt >= 4:
                W(e, qy, t_ywb[tt - 4])
            i = e.activation(otl[sl][:],
                             fsb[:, D * tt:D * (tt + 1)], Act.Square,
                             accum_out=stq[:, sl:sl + 1])
            inc(i, "cS")
        on_scal(f)
        t_sq = tok("cS")

        def f(e, sl=sl, t_sq=t_sq):
            e.tensor_scalar(stm[:, sl:sl + 1], stm[:, sl:sl + 1], 1.0 / D,
                            None, Alu.mult)
            e.drain()
            e.tensor_mul(stq2[:, sl:sl + 1], stm[:, sl:sl + 1],
                         stm[:, sl:sl + 1])
            e.drain()
            W(e, "cS", t_sq)
            i = e.scalar_tensor_tensor(stv[:, sl:sl + 1], stq[:, sl:sl + 1],
                                       1.0 / D, stq2[:, sl:sl + 1],
                                       Alu.mult, Alu.subtract)
            inc(i, "cV")
        on_vec(f)
        t_var = tok("cV")

        def f(e, sl=sl, t_var=t_var):
            W(e, "cV", t_var)
            i = e.activation(stv[:, sl:sl + 1], stv[:, sl:sl + 1],
                             Act.Sqrt, bias=1e-5)
            inc(i, "cS")
        on_scal(f)
        t_r = tok("cS")

        def f(e, tt=tt, sl=sl, t_r=t_r, tl=t_x1ld[tt], qy=qy):
            W(e, "cS", t_r)
            W(e, "qX1", tl)
            if tt >= 4:
                W(e, qy, t_ywb[tt - 4])
            e.reciprocal(stv[:, sl:sl + 1], stv[:, sl:sl + 1])
            e.drain()
            e.tensor_scalar(otl[sl][:], fsb[:, D * tt:D * (tt + 1)],
                            stm[:, sl:sl + 1], stv[:, sl:sl + 1],
                            Alu.subtract, Alu.mult)
            e.drain()
            i = e.tensor_add(otl[sl][:], otl[sl][:], x1r[sl][:])
            inc(i, "cV")
        on_vec(f)
        t_yv[tt] = tok("cV")

        def f(e, tt=tt, sl=sl, tv=t_yv[tt], qy=qy):
            W(e, "cV", tv)
            W(e, qy, cnt[qy])
            i = e.dma_start(y[128 * tt:128 * (tt + 1), :], otl[sl][:])
            inc(i, qy)
        on_sync(f)
        t_ywb[tt] = tok(qy)

    def f(e):
        W(e, "qY0", cnt["qY0"])
        W(e, "qY1", cnt["qY1"])
    on_sync(f)

    flush()
    blk.__exit__(None, None, None)
    return nc


_NC_CACHE = [None]


def kernel(**inputs):
    def g(name):
        return np.ascontiguousarray(np.asarray(inputs[name], dtype=np.float32))

    x = g("L0_attn_x")
    ipT = np.ascontiguousarray(g("L0_attn_in_proj_w").T)
    ipb = g("L0_attn_in_proj_b").reshape(1, 3 * D)
    owT = np.ascontiguousarray(g("L0_attn_out_w").T)
    obr = g("L0_attn_out_b").reshape(1, D)
    w1T = np.ascontiguousarray(g("L0_attn_w1").T)
    b1v = g("L0_attn_b1")
    w2T = np.ascontiguousarray(g("L0_attn_w2").T)
    b2r = g("L0_attn_b2").reshape(1, D)
    wgT = np.ascontiguousarray(g("L0_ffn_w_gate").T)
    wuT = np.ascontiguousarray(g("L0_ffn_w_up").T)
    wdT = np.ascontiguousarray(g("L0_ffn_w_down").T)
    ddv = g("L0_attn_delta_decay")
    ln1v = g("L0_attn_ln1_w")
    ln2v = g("L0_attn_ln2_w")
    gwT8 = np.ascontiguousarray(g("L0_attn_gate_w")[16:24].T)
    gb8 = np.ascontiguousarray(g("L0_attn_gate_b")[16:24].reshape(8, 1))
    phi = g("L0_attn_phi_res")
    id8 = np.eye(8, dtype=np.float32)

    import ml_dtypes
    BF = ml_dtypes.bfloat16
    id128 = np.eye(128, dtype=BF)
    masks = []
    for h in range(2):
        mk = np.zeros((128, 64), dtype=np.float32)
        for r in range(128):
            for q in range(64):
                if r <= 64 * h + q:
                    mk[r, q] = 1.0
        masks.append(mk.astype(BF))

    if _NC_CACHE[0] is None:
        _NC_CACHE[0] = build_nc()
    nc = _NC_CACHE[0]

    in_maps = []
    for c in range(8):
        b, h = c // 2, c % 2
        in_maps.append({
            "x0": np.ascontiguousarray(x[b]),
            "ipT": ipT, "ipb": ipb, "owT": owT, "obr": obr,
            "w1T": w1T, "b1v": b1v, "w2T": w2T, "b2r": b2r,
            "wgT": wgT, "wuT": wuT, "wdT": wdT,
            "ddv": ddv, "ln1v": ln1v, "ln2v": ln2v,
            "gwT8": gwT8, "gb8": gb8, "phi": phi, "id8d": id8,
            "hcold": np.stack([np.full(128, h, np.float32),
                               np.full(128, 1.0 - h, np.float32)],
                              axis=1),
            "x0ownd": np.ascontiguousarray(
                x[b].reshape(32, 2, 64, D)[:, h].reshape(OWN, D)),
            "maskUd": masks[h],
            "id128d": id128,
        })

    res = run_bass_kernel_spmd(nc, in_maps, core_ids=list(range(8)))
    out = np.empty((B, S, D), dtype=np.float32)
    for c in range(8):
        b, h = c // 2, c % 2
        yc = np.asarray(res.results[c]["y"])
        out[b].reshape(32, 2, 64, D)[:, h] = yc.reshape(32, 64, D)
    return out


# version 19
# speedup vs baseline: 1.0733x; 1.0733x over previous
import numpy as np
import concourse.bass as bass
import concourse.mybir as mybir
from concourse.bass_utils import run_bass_kernel_spmd

F32 = mybir.dt.float32
BF16 = mybir.dt.bfloat16
I32 = mybir.dt.int32
Alu = mybir.AluOpType
Act = mybir.ActivationFunctionType
AX = mybir.AxisListType.X

B, S, D = 4, 4096, 1024
OWN = S // 2
NH, HD = 16, 64
DI = 4096
MAGIC = float(np.float32(1.5 * 2 ** 23))
KB = 1024


def build_nc():
    nc = bass.Bass("TRN2", target_bir_lowering=False)

    x0 = nc.dram_tensor("x0", [S, D], F32, kind="ExternalInput")
    ipT = nc.dram_tensor("ipT", [D, 3 * D], F32, kind="ExternalInput")
    ipb = nc.dram_tensor("ipb", [1, 3 * D], F32, kind="ExternalInput")
    owT = nc.dram_tensor("owT", [D, D], F32, kind="ExternalInput")
    obr = nc.dram_tensor("obr", [1, D], F32, kind="ExternalInput")
    w1Td = nc.dram_tensor("w1T", [D, DI], F32, kind="ExternalInput")
    b1v = nc.dram_tensor("b1v", [DI], F32, kind="ExternalInput")
    w2Td = nc.dram_tensor("w2T", [DI, D], F32, kind="ExternalInput")
    b2r = nc.dram_tensor("b2r", [1, D], F32, kind="ExternalInput")
    wfd = {"g": nc.dram_tensor("wgT", [D, DI], F32, kind="ExternalInput"),
           "u": nc.dram_tensor("wuT", [D, DI], F32, kind="ExternalInput"),
           "d": nc.dram_tensor("wdT", [DI, D], F32, kind="ExternalInput")}
    ddv = nc.dram_tensor("ddv", [D], F32, kind="ExternalInput")
    ln1v = nc.dram_tensor("ln1v", [D], F32, kind="ExternalInput")
    ln2v = nc.dram_tensor("ln2v", [D], F32, kind="ExternalInput")
    gwT8 = nc.dram_tensor("gwT8", [D, 8], F32, kind="ExternalInput")
    gb8d = nc.dram_tensor("gb8", [8, 1], F32, kind="ExternalInput")
    phid = nc.dram_tensor("phi", [8, 8], F32, kind="ExternalInput")
    id8d = nc.dram_tensor("id8d", [8, 8], F32, kind="ExternalInput")
    hcold = nc.dram_tensor("hcold", [128, 2], F32, kind="ExternalInput")
    x0ownd = nc.dram_tensor("x0ownd", [OWN, D], F32, kind="ExternalInput")
    maskUd = nc.dram_tensor("maskUd", [128, 64], BF16, kind="ExternalInput")
    id128d = nc.dram_tensor("id128d", [128, 128], BF16, kind="ExternalInput")

    y = nc.dram_tensor("y", [OWN, D], F32, kind="ExternalOutput")
    z1d = nc.dram_tensor("z1d", [S, D], BF16)
    qd = nc.dram_tensor("qd", [D, OWN], BF16)
    kd = nc.dram_tensor("kd", [D, S], BF16)
    xod = nc.dram_tensor("xod", [D, OWN], BF16)
    n2d = nc.dram_tensor("n2d", [D, OWN], BF16)
    x1d = nc.dram_tensor("x1d", [OWN, D], BF16)
    pdd = nc.dram_tensor("pdd", [OWN, DI], BF16)
    pqd = nc.dram_tensor("pqd", [OWN, DI], BF16)
    wqd = {"g": nc.dram_tensor("wgqd", [D, DI], BF16),
           "u": nc.dram_tensor("wuqd", [D, DI], BF16),
           "d": nc.dram_tensor("wdqd", [DI, D], BF16)}
    m64d = nc.dram_tensor("m64d", [8, 8], F32)

    def SBA(name, shape, dtype, off_kb):
        off = int(round(off_kb * KB / 32.0)) * 32
        return nc.alloc_sbuf_tensor_at(name, shape, dtype, offset=off)

    ones_b = SBA("ones_b", [128, 128], BF16, 24.0)
    ones_f = SBA("ones_f", [128, 128], F32, 24.25)
    mku = SBA("mku", [128, 64], BF16, 24.75)
    id128 = SBA("id128", [128, 128], BF16, 68.5)
    cst = SBA("cst", [128, 8], F32, 68.8)
    beta = SBA("beta", [128, 8], F32, 25.0)
    betm1 = SBA("betm1", [128, 8], F32, 25.05)
    betacol = SBA("betacol", [128, 1], F32, 25.975)
    ln1c = SBA("ln1c", [128, 8], F32, 25.1)
    ln2c = SBA("ln2c", [128, 8], F32, 25.15)
    b1c = SBA("b1c", [128, 32], F32, 25.2)
    ipbcT = SBA("ipbcT", [128, 24], F32, 25.35)
    bq8 = SBA("bq8", [128, 8], F32, 25.45)
    obc = SBA("obc", [128, 8], F32, 25.5)
    xs = SBA("xs", [128, 8], F32, 25.55)
    xsb = SBA("xsb", [128, 8], BF16, 25.6)
    hcol = SBA("hcol", [128, 2], F32, 25.65)
    sk1 = SBA("sk1", [8, 8], F32, 25.7)
    sk2 = SBA("sk2", [8, 8], F32, 25.75)
    skr = SBA("skr", [8, 1], F32, 25.8)
    id8s = SBA("id8s", [8, 8], F32, 25.85)
    g8 = SBA("g8", [8, 1], F32, 25.9)
    gb8s = SBA("gb8s", [8, 1], F32, 25.95)
    mpr = SBA("mpr", [1, 64], F32, 26.0)
    mpb = SBA("mpb", [128, 64], F32, 26.3)
    gw8b = SBA("gw8b", [128, 64], BF16, 26.6)
    gw8f = SBA("gw8f", [128, 64], F32, 26.75)
    wstat = SBA("wstat", [128, 48], F32, 27.05)
    wsc = SBA("wsc", [128, 8], F32, 27.3)
    zscin = SBA("zscin", [128, 16], F32, 27.35)
    invgc = SBA("invgc", [128, 16], F32, 27.45)
    invuc = SBA("invuc", [128, 16], F32, 27.55)
    pmxall = SBA("pmxall", [128, 128], F32, 27.65)
    pscm = SBA("pscm", [128, 16], F32, 28.15)
    psci = SBA("psci", [128, 16], F32, 28.25)
    ipbr_b = SBA("ipbr_b", [1, 3 * D], BF16, 28.4)
    obb2_b = SBA("obb2_b", [1, D], BF16, 34.4)
    tr1 = SBA("tr1", [1, D], F32, 36.4)
    tr2 = SBA("tr2", [1, D], F32, 40.4)
    r1row = SBA("r1row", [1, S], F32, 44.4)
    r2row = SBA("r2row", [1, OWN], F32, 60.4)

    R1, R2, R3 = 71.0, 135.0, 200.0
    x0s = [SBA(f"x0s{i}", [128, D], F32, R1 + 4 * i) for i in range(4)]
    z1s = [SBA(f"z1s{i}", [128, D], BF16, R1 + 24 + 2 * i) for i in range(4)]
    stm = SBA("stm", [128, 4], F32, R3 + 12)
    stq = SBA("stq", [128, 4], F32, R3 + 12.05)
    stv = SBA("stv", [128, 4], F32, R3 + 12.1)
    stq2 = SBA("stq2", [128, 4], F32, R3 + 12.15)
    sqs = [SBA(f"sqs{i}", [128, D], BF16, R1 + 16 + 2 * i) for i in range(4)]
    xdb = SBA("xdb", [128, 8 * S], BF16, R1)
    z1T2 = [SBA(f"z1T{i}", [128, S], BF16, R2 + 48 * i) for i in range(2)]
    sprime = SBA("sprime", [128, 2049], F32, R2 + 8)
    xdcs = SBA("xdcs", [128, S], F32, R2 + 16.2)
    sqc = SBA("sqc", [128, S], BF16, R2 + 32.2)
    xoc = SBA("xoc", [128, OWN], BF16, R2 + 40.2)
    msacc = SBA("msacc", [1, S], F32, R2 + 44.2)
    r1b = SBA("r1b", [128, S], F32, R3)
    n1o = SBA("n1o", [128, 8 * OWN], BF16, R2)
    vbuf = SBA("vbuf", [128, 32 * 1040], BF16, R2)
    wst = SBA("wst", [128, 8 * D], BF16, R3)
    qkev = [SBA(f"qkev{i}", [128, 512], BF16, R3 + 16 + i) for i in range(2)]
    obuf = SBA("obuf", [128, 8 * OWN], BF16, R1)
    pr = SBA("pr", [128, 4096], BF16, R1 + 32)
    kh = [SBA(f"kh{i}", [64, S], BF16, R1 + 40 + 8 * i) for i in range(2)]
    qh = [SBA(f"qh{i}", [64, 512], BF16, R1 + 56 + i) for i in range(2)]
    onsb = [SBA(f"onsb{i}", [64, 512], BF16, R1 + 58 + i) for i in range(2)]
    rlsb = SBA("rlsb", [128, 512], BF16, R1 + 60)
    rlbs = SBA("rlbs", [128, 512], F32, R1 + 62)
    wkc = SBA("wkc", [128, 2048], F32, R3 + 16 - 0.03125)
    owb = SBA("owb", [128, 8 * D], BF16, R3)
    bo = SBA("bo", [128, 16 * D], BF16, R1 + 32)
    s2 = SBA("s2", [128, 8 * OWN], BF16, R2)
    sq2s = SBA("sq2s", [128, 2048], BF16, R2 + 32)
    r2b = SBA("r2b", [128, OWN], F32, R2 + 36.2)
    xosr = [SBA(f"xosr{i}", [128, 2048], BF16, R3 + 12 + 4 * i) for i in range(2)]
    w1h = SBA("w1h", [128, 8 * 2048], BF16, R1)
    w2h = SBA("w2h", [128, 16 * D], BF16, R2)
    bigm = SBA("bigm", [128, 8 * D], BF16, R2 + 32)
    hth = SBA("hth", [128, 16 * 256], BF16, R2 + 48)
    n2s = [SBA(f"n2s{i}", [128, 2048], BF16, R2 + 56 + 4 * i) for i in range(2)]
    wqc2 = [SBA(f"wqc{i}", [128, 1024], F32, R3 + 4 * i) for i in range(2)]
    wqo = SBA("wqo", [128, 2048], BF16, R3 + 8)
    x0o = [SBA(f"x0o{i}", [128, D], F32, R1 + 4 * i) for i in range(2)]
    x1t = [SBA(f"x1t{i}", [128, D], F32, R1 + 8 + 4 * i) for i in range(2)]
    zt = [SBA(f"zt{i}", [128, D], F32, R1 + 16 + 4 * i) for i in range(2)]
    z2qs = [SBA(f"z2qs{i}", [128, D], BF16, R1 + 24 + 2 * i) for i in range(2)]
    z2qT = SBA("z2qT", [128, 8 * OWN], BF16, R2)
    wgs = [SBA(f"wgs{i}", [128, 8 * 512], BF16, R2 + 32 + 8 * i) for i in range(2)]
    wus = [SBA(f"wus{i}", [128, 8 * 512], BF16, R2 + 48 + 8 * i) for i in range(2)]
    gsb = SBA("gsb", [128, 512], F32, R3)
    psb = [SBA(f"psb{i}", [128, 512], F32, R3 + 2 + 2 * i) for i in range(2)]
    ptile2 = [SBA(f"ptile{i}", [128, DI], BF16, R3 + 6 + 8 * i)
              for i in range(2)]
    pwide = SBA("pwide", [128, DI], F32, R1)
    pqtile2 = [SBA(f"pqtile{i}", [128, DI], BF16, R1 + 16 + 8 * i)
               for i in range(2)]
    fsb = SBA("fsb", [128, 16 * D], F32, R1)
    pqt = SBA("pqt", [128, 8 * 2048], BF16, R2)
    wds = SBA("wds", [128, 8 * D], BF16, R2 + 32)
    x1r = [SBA(f"x1r{i}", [128, D], BF16, R2 + 2 * i) for i in range(4)]
    otl = [SBA(f"otl{i}", [128, D], F32, R2 + 8 + 4 * i) for i in range(4)]

    psA = nc.alloc_psum_tensor("psA", [128, 2048], F32)
    psC = nc.alloc_psum_tensor("psC", [128, 1024], F32)
    psD = nc.alloc_psum_tensor("psD", [128, 1024], F32)

    semnames = ["qA0", "qA1", "qB0", "qB1", "qC0", "qC1", "qD0", "qD1",
                "qE0", "qE1", "qF0", "qF1", "qW0", "qW1", "qX0", "qX1",
                "qY0", "qY1", "qZ0", "qZ1", "qV0", "qV1", "qP0", "cV", "cS", "cT", "cG"]
    sems, cnt = {}, {}
    for n in semnames:
        sems[n] = nc.semaphore(n).__enter__()
        cnt[n] = 0

    _cvals = [1e-5, 1e-6, 1.0 / 1024.0, 0.125]
    for _j, _v in enumerate(_cvals):
        nc.const_aps.register(F32, _v, cst[:, _j:_j + 1])

    blk = nc.Block()
    block = blk.__enter__()

    def inc(instr, name, v=None):
        if v is None:
            v = 16 if name.startswith("q") else 1
        cnt[name] += v
        instr.then_inc(sems[name], v)
        return cnt[name]

    def tok(name):
        return cnt[name]


    class _RecInstr:
        __slots__ = ("incs",)
        def __init__(self):
            self.incs = []
        def then_inc(self, sem, v):
            self.incs.append((sem, v))
            return self

    class _RecEngine:
        def __init__(self, name):
            self._name = name
            self._calls = []
        def __getattr__(self, m):
            def call(*a, **k):
                h = _RecInstr()
                self._calls.append((m, a, k, h))
                return h
            return call

    _PROX = {n: _RecEngine(n) for n in
             ("sync", "vector", "scalar", "tensor", "gpsimd")}

    def _replay(name):
        calls = _PROX[name]._calls
        _PROX[name]._calls = []
        def body(e):
            for m, a, k, h in calls:
                alp = k.get("_alp")
                if alp is not None:
                    k = {x: v for x, v in k.items() if x != "_alp"}
                    with nc.allow_low_precision(alp):
                        i = getattr(e, m)(*a, **k)
                else:
                    i = getattr(e, m)(*a, **k)
                for sem, v in h.incs:
                    i.then_inc(sem, v)
        return body

    def flush():
        for n, meth in (("sync", block.sync), ("tensor", block.tensor),
                        ("scalar", block.scalar), ("vector", block.vector),
                        ("gpsimd", block.gpsimd)):
            if _PROX[n]._calls:
                meth(_replay(n))

    def on_sync(f):
        f(_PROX["sync"])

    def on_vec(f):
        e = _PROX["vector"]
        e.wait_ge(sems["cV"], cnt["cV"])
        f(e)

    def on_scal(f):
        e = _PROX["scalar"]
        e.wait_ge(sems["cS"], cnt["cS"])
        f(e)

    def on_ten(f):
        f(_PROX["tensor"])

    def on_gp(f):
        f(_PROX["gpsimd"])

    def W(e, name, v):
        if v > 0:
            e.wait_ge(sems[name], v)

    def own_blend(e, dst_v, ev_v, od_v):
        e.tensor_scalar(dst_v, od_v, hcol[:, 0:1], None, Alu.mult)
        e.drain()
        e.scalar_tensor_tensor(dst_v, ev_v, hcol[:, 1:2], dst_v,
                               Alu.mult, Alu.add)

    def f(e):
        W(e, "qP0", cnt["qP0"])
        inc(e.dma_start(mku[:], maskUd[:, :]), "qP0")
        W(e, "qP0", cnt["qP0"])
        inc(e.dma_start(hcol[:], hcold[:, :]), "qP0")
        W(e, "qP0", cnt["qP0"])
        inc(e.dma_start(beta[:], bass.AP(ddv, 0, [[1, 128], [128, 8]]),
                        allow_slow_non_contiguous=True), "qP0")
        W(e, "qP0", cnt["qP0"])
        inc(e.dma_start(ln1c[:], bass.AP(ln1v, 0, [[1, 128], [128, 8]]),
                        allow_slow_non_contiguous=True), "qP0")
        W(e, "qP0", cnt["qP0"])
        inc(e.dma_start(ln2c[:], bass.AP(ln2v, 0, [[1, 128], [128, 8]]),
                        allow_slow_non_contiguous=True), "qP0")
        W(e, "qP0", cnt["qP0"])
        inc(e.dma_start(b1c[:], bass.AP(b1v, 0, [[1, 128], [128, 32]]),
                        allow_slow_non_contiguous=True), "qP0")
        W(e, "qP0", cnt["qP0"])
        inc(e.dma_start(ipbcT[:], bass.AP(ipb, 0, [[1, 128], [128, 24]]),
                        allow_slow_non_contiguous=True), "qP0")
        W(e, "qP0", cnt["qP0"])
        inc(e.dma_start(obc[:], bass.AP(obr, 0, [[1, 128], [128, 8]]),
                        allow_slow_non_contiguous=True), "qP0")
        W(e, "qP0", cnt["qP0"])
        inc(e.dma_start(sk1[:], phid[:, :]), "qP0")
        W(e, "qP0", cnt["qP0"])
        inc(e.dma_start(id8s[:], id8d[:, :]), "qP0")
        W(e, "qP0", cnt["qP0"])
        inc(e.dma_start(gb8s[:], gb8d[:, :]), "qP0")
        W(e, "qP0", cnt["qP0"])
        inc(e.dma_start(tr1[:], obr[:, :]), "qP0")
        W(e, "qP0", cnt["qP0"])
        inc(e.dma_start(tr2[:], b2r[:, :]), "qP0")
        W(e, "qP0", cnt["qP0"])
        inc(e.dma_start(gw8f[:],
                        bass.AP(gwT8, 0, [[8, 128], [1024, 8], [1, 8]]),
                        allow_slow_non_contiguous=True), "qP0")
        W(e, "qP0", cnt["qP0"])
        inc(e.dma_start(id128[:], id128d[:, :]), "qP0")
    on_sync(f)
    t_p0 = tok("qP0")

    def f(e):
        W(e, "qW0", cnt["qW0"])
        i = e.dma_start(ipbr_b[:], ipb[:, :])
        inc(i, "qW0")
    on_gp(f)
    t_ipbr = tok("qW0")

    def f(e):
        e.memset(ones_b[:], 1.0)
        e.memset(ones_f[:], 1.0)
        e.memset(cst[:, 0:1], 1e-5)
        e.memset(cst[:, 1:2], 1e-6)
        e.memset(cst[:, 2:3], 1.0 / 1024.0)
        e.memset(cst[:, 3:4], 0.125)
        e.memset(pmxall[:], 0.0)
        W(e, "qP0", t_p0)
        e.tensor_scalar(bq8[:], ipbcT[:, 0:8], 0.125, None, Alu.mult)
        e.tensor_copy(gw8b[:], gw8f[:])
        e.tensor_add(tr1[:], tr1[:], tr2[:])
        e.drain()
        i = e.tensor_copy(obb2_b[:], tr1[:])
        inc(i, "cV")
    on_vec(f)
    t_p0v = tok("cV")

    def f(e):
        W(e, "cV", t_p0v)
        e.activation(beta[:], beta[:], Act.Sigmoid)
        i = e.activation(sk1[:], sk1[:], Act.Exp)
        inc(i, "cS")
    on_scal(f)
    t_p0s = tok("cS")

    def f(e):
        W(e, "cS", t_p0s)
        i = e.tensor_scalar(betm1[:], beta[:], -1.0, None, Alu.add)
        inc(i, "cV")
    on_vec(f)

    cur = sk1
    for itn in range(31):
        def f(e, cur=cur, itn=itn):
            if itn < 30:
                e.reduce_sum(skr[:], cur[:], axis=AX)
                e.drain()
                e.reciprocal(skr[:], skr[:])
                e.drain()
                i = e.tensor_scalar(cur[:], cur[:], skr[:, 0:1], None,
                                    Alu.mult)
                inc(i, "cV")
            else:
                i = e.engine_nop()
                inc(i, "cV")
        on_vec(f)
        tv = tok("cV")

        def f(e, cur=cur, tv=tv):
            W(e, "cV", tv)
            i = e.transpose(psD[0:8, 0:8], cur[:], id8s[:])
            inc(i, "cT")
        on_ten(f)
        tt = tok("cT")
        nxt = sk2 if cur is sk1 else sk1

        def f(e, nxt=nxt, tt=tt):
            W(e, "cT", tt)
            i = e.tensor_copy(nxt[:], psD[0:8, 0:8])
            inc(i, "cV")
        on_vec(f)
        cur = nxt
    skHT = cur

    flush()

    t_z1done, t_z1wb = {}, {}
    for t in range(32):
        sl = t % 4
        qx = "qA0" if t % 2 == 0 else "qA1"
        qz = "qB0" if t % 2 == 0 else "qB1"

        def f(e, t=t, sl=sl, qx=qx):
            if t >= 4:
                W(e, "cV", t_z1done[t - 4])
            W(e, qx, cnt[qx])
            i = e.dma_start(x0s[sl][:], x0[128 * t:128 * (t + 1), :])
            inc(i, qx)
        on_sync(f)
        t_x0 = tok(qx)

        def f(e, sl=sl, t_x0=t_x0, qx=qx):
            W(e, qx, t_x0)
            i = e.activation(sqs[sl][:], x0s[sl][:], Act.Square,
                             accum_out=stq[:, sl:sl + 1])
            inc(i, "cS")
        on_scal(f)
        t_sq = tok("cS")

        def f(e, sl=sl, t_x0=t_x0, t_sq=t_sq, qx=qx):
            W(e, qx, t_x0)
            e.reduce_sum(stm[:, sl:sl + 1], x0s[sl][:], axis=AX)
            e.drain()
            e.tensor_scalar(stm[:, sl:sl + 1], stm[:, sl:sl + 1], 1.0 / D,
                            None, Alu.mult)
            e.drain()
            e.tensor_mul(stq2[:, sl:sl + 1], stm[:, sl:sl + 1],
                         stm[:, sl:sl + 1])
            e.drain()
            W(e, "cS", t_sq)
            i = e.scalar_tensor_tensor(stv[:, sl:sl + 1], stq[:, sl:sl + 1],
                                       1.0 / D, stq2[:, sl:sl + 1],
                                       Alu.mult, Alu.subtract)
            inc(i, "cV")
        on_vec(f)
        t_var = tok("cV")

        def f(e, sl=sl, t_var=t_var):
            W(e, "cV", t_var)
            i = e.activation(stv[:, sl:sl + 1], stv[:, sl:sl + 1], Act.Sqrt,
                             bias=1e-5)
            inc(i, "cS")
        on_scal(f)
        t_r = tok("cS")

        def f(e, t=t, sl=sl, t_r=t_r, qz=qz):
            W(e, "cS", t_r)
            if t >= 4:
                W(e, qz, t_z1wb[t - 4])
            e.reciprocal(stv[:, sl:sl + 1], stv[:, sl:sl + 1])
            e.drain()
            i = e.tensor_scalar(z1s[sl][:], x0s[sl][:], stm[:, sl:sl + 1],
                                stv[:, sl:sl + 1], Alu.subtract, Alu.mult)
            inc(i, "cV")
        on_vec(f)
        t_z1done[t] = tok("cV")

        def f(e, t=t, sl=sl, qz=qz, tz=t_z1done[t]):
            W(e, "cV", tz)
            W(e, qz, cnt[qz])
            i = e.dma_start(z1d[128 * t:128 * (t + 1), :], z1s[sl][:])
            inc(i, qz)
        on_gp(f)
        t_z1wb[t] = tok(qz)

    t_A_qA0, t_A_qA1 = tok("qA0"), tok("qA1")
    t_A_cS = tok("cS")
    t_A_qB0, t_A_qB1 = tok("qB0"), tok("qB1")

    flush()

    t_scan, t_sqd, t_xowb, t_sqmm_last = {}, {}, {}, {}
    for c in range(8):
        def f(e, c=c):
            if c >= 2:
                W(e, "cV", t_scan[c - 2])
            if c == 0:
                W(e, "qB0", t_z1wb[30])
                W(e, "qB1", t_z1wb[31])
            W(e, "qC0", cnt["qC0"])
            i = e.dma_start(z1T2[c % 2][:], z1d[:, 128 * c:128 * (c + 1)],
                            transpose=True)
            inc(i, "qC0")
        on_sync(f)
        t_z1T = tok("qC0")

        def f(e, c=c, t_z1T=t_z1T):
            W(e, "qC0", t_z1T)
            z1T = z1T2[c % 2]
            if c == 0:
                e.memset(sprime[:, 0:1], 0.0)
            else:
                W(e, "cS", t_sqd[c - 1])
            bcast = bass.AP(beta, c, [[8, 128], [0, 2048]])
            e.tensor_tensor_scan(sprime[:, 1:2049], bcast, z1T[:, 0:2048],
                                 0.0, Alu.mult, Alu.add)
            e.drain()
            e.scalar_tensor_tensor(xdcs[:, 0:2048], sprime[:, 0:2048],
                                   betm1[:, c:c + 1], z1T[:, 0:2048],
                                   Alu.mult, Alu.add)
            e.drain()
            e.tensor_copy(sprime[:, 0:1], sprime[:, 2048:2049])
            e.drain()
            e.tensor_tensor_scan(sprime[:, 1:2049], bcast, z1T[:, 2048:4096],
                                 sprime[:, 0:1], Alu.mult, Alu.add)
            e.drain()
            e.scalar_tensor_tensor(xdcs[:, 2048:4096], sprime[:, 0:2048],
                                   betm1[:, c:c + 1], z1T[:, 2048:4096],
                                   Alu.mult, Alu.add)
            e.drain()
            e.memset(sprime[:, 0:1], 0.0)
            e.drain()
            e.reduce_sum(xs[:, c:c + 1], xdcs[:], axis=AX)
            e.tensor_copy(xdb[:, S * c:S * (c + 1)], xdcs[:])
            if c >= 1:
                W(e, "qB0", t_xowb[c - 1])
            srcv = xdcs[:, :].rearrange("p (u t s) -> p u t s", t=2, s=64)
            dstv = xoc[:, :].rearrange("p (u s) -> p u s", s=64)
            own_blend(e, dstv, srcv[:, :, 0, :], srcv[:, :, 1, :])
            i = e.engine_nop()
            inc(i, "cV")
        on_vec(f)
        t_scan[c] = tok("cV")

        def f(e, c=c, tsc=t_scan[c]):
            W(e, "cV", tsc)
            if c >= 1:
                W(e, "cT", t_sqmm_last[c - 1])
            i = e.activation(sqc[:], xdcs[:], Act.Square)
            inc(i, "cS")
        on_scal(f)
        t_sqd[c] = tok("cS")

        def f(e, c=c, tsc=t_scan[c]):
            W(e, "cV", tsc)
            W(e, "qB0", cnt["qB0"])
            i = e.dma_start(xod[128 * c:128 * (c + 1), :], xoc[:])
            inc(i, "qB0")
        on_gp(f)
        t_xowb[c] = tok("qB0")

        for g in range(8):
            if g < 4:
                tgt = lambda g=g: psA[0:1, 512 * g:512 * (g + 1)]
            elif g < 6:
                tgt = lambda g=g: psC[0:1, 512 * (g - 4):512 * (g - 3)]
            else:
                tgt = lambda g=g: psD[0:1, 512 * (g - 6):512 * (g - 5)]

            def f(e, c=c, g=g, tsq=t_sqd[c], tgt=tgt):
                if g == 0:
                    W(e, "cS", tsq)
                i = e.matmul(tgt(), ones_b[:, 0:1],
                             sqc[:, 512 * g:512 * (g + 1)],
                             start=(c == 0), stop=(c == 7))
                inc(i, "cT")
            on_ten(f)
        t_sqmm_last[c] = tok("cT")

    def f(e, tm=t_sqmm_last[7]):
        W(e, "cT", tm)
        e.activation(r1row[0:1, 0:2048], psA[0:1, 0:2048], Act.Sqrt,
                     bias=1e-6, scale=1.0 / D)
        e.activation(r1row[0:1, 2048:3072], psC[0:1, 0:1024], Act.Sqrt,
                     bias=1e-6, scale=1.0 / D)
        i = e.activation(r1row[0:1, 3072:4096], psD[0:1, 0:1024], Act.Sqrt,
                         bias=1e-6, scale=1.0 / D)
        inc(i, "cS")
    on_scal(f)
    t_r1s = tok("cS")

    def f(e, t_r1s=t_r1s):
        W(e, "cS", t_r1s)
        i = e.reciprocal(r1row[:], r1row[:])
        inc(i, "cV")
    on_vec(f)
    t_r1 = tok("cV")

    t_r1c = {}
    for g in range(8):
        def f(e, g=g, t_r1=t_r1):
            if g == 0:
                W(e, "cV", t_r1)
            if g >= 4:
                W(e, "cV", t_r1c[g - 4])
            i = e.matmul(psA[:, 512 * (g % 4):512 * (g % 4) + 512],
                         ones_f[0:1, :], r1row[0:1, 512 * g:512 * (g + 1)],
                         start=True, stop=True)
            inc(i, "cT")
        on_ten(f)
        t_pe = tok("cT")

        def f(e, g=g, t_pe=t_pe):
            W(e, "cT", t_pe)
            if g == 0:
                W(e, "qA0", t_A_qA0)
                W(e, "qA1", t_A_qA1)
                W(e, "cS", t_A_cS)
                W(e, "qB0", t_A_qB0)
                W(e, "qB1", t_A_qB1)
            i = e.tensor_copy(r1b[:, 512 * g:512 * (g + 1)],
                              psA[:, 512 * (g % 4):512 * (g % 4) + 512])
            inc(i, "cV")
        on_vec(f)
        t_r1c[g] = tok("cV")
    t_r1b = tok("cV")

    def f(e):
        i = e.tensor_scalar(xsb[:], xs[:], 1.0 / S, None, Alu.mult)
        inc(i, "cV")
    on_vec(f)
    t_xsb = tok("cV")

    def f(e, t_xsb=t_xsb):
        W(e, "cV", t_xsb)
        W(e, "cS", t_r1s)
        for kc in range(8):
            i = e.matmul(psD[0:8, 0:1], gw8b[:, 8 * kc:8 * (kc + 1)],
                         xsb[:, kc:kc + 1], start=(kc == 0), stop=(kc == 7))
        inc(i, "cT")
    on_ten(f)
    t_gmm = tok("cT")

    def f(e, t_gmm=t_gmm):
        W(e, "cT", t_gmm)
        i = e.activation(g8[:], psD[0:8, 0:1], Act.Sigmoid,
                         bias=gb8s[:, 0:1])
        inc(i, "cS")
    on_scal(f)
    t_g8 = tok("cS")

    def f(e, t_g8=t_g8):
        W(e, "cS", t_g8)
        e.tensor_scalar(skHT[:], skHT[:], g8[:, 0:1], None, Alu.mult)
        e.drain()
        i = e.tensor_add(skHT[:], skHT[:], id8s[:])
        inc(i, "cV")
    on_vec(f)
    t_mpt = tok("cV")

    def f(e, t_mpt=t_mpt):
        W(e, "cV", t_mpt)
        W(e, "qB1", cnt["qB1"])
        i = e.dma_start(m64d[:, :], skHT[:])
        inc(i, "qB1")
    on_gp(f)
    t_m64w = tok("qB1")

    def f(e, t_m64w=t_m64w):
        W(e, "qB1", t_m64w)
        W(e, "qC1", cnt["qC1"])
        i = e.dma_start(mpr[:], bass.AP(m64d, 0, [[64, 1], [1, 64]]))
        inc(i, "qC1")
    on_sync(f)
    t_mprl = tok("qC1")

    def f(e, t_mprl=t_mprl):
        W(e, "qC1", t_mprl)
        i = e.matmul(psD[:, 0:64], ones_f[0:1, :], mpr[0:1, :],
                     start=True, stop=True)
        inc(i, "cT")
    on_ten(f)
    t_mpbT = tok("cT")

    def f(e, t_mpbT=t_mpbT):
        W(e, "cT", t_mpbT)
        i = e.tensor_copy(mpb[:], psD[:, 0:64])
        inc(i, "cV")
    on_vec(f)
    t_mpb = tok("cV")

    flush()

    for c in range(8):
        def f(e, c=c):
            e.scalar_tensor_tensor(xdb[:, S * c:S * (c + 1)],
                                   xdb[:, S * c:S * (c + 1)],
                                   ln1c[:, c:c + 1], r1b[:],
                                   Alu.mult, Alu.mult)
            e.drain()
            srcv = xdb[:, S * c:S * (c + 1)].rearrange(
                "p (u t s) -> p u t s", t=2, s=64)
            dstv = n1o[:, OWN * c:OWN * (c + 1)].rearrange(
                "p (u s) -> p u s", s=64)
            own_blend(e, dstv, srcv[:, :, 0, :], srcv[:, :, 1, :])
            i = e.engine_nop()
            inc(i, "cV")
        on_vec(f)
    t_n1all = tok("cV")

    flush()

    def f(e):
        W(e, "cV", t_n1all)
        for kc in range(8):
            W(e, "qW0", cnt["qW0"])
            inc(e.dma_start(wst[:, D * kc:D * (kc + 1)],
                            ipT[128 * kc:128 * (kc + 1), 0:D]), "qW0")
    on_gp(f)
    t_wq = tok("qW0")

    t_qkev, t_qkwb = {}, {}
    nq = 0
    for oc in range(8):
        for tg in range(4):
            bk = nq % 4

            def f(e, oc=oc, tg=tg, bk=bk, nq=nq):
                if nq == 0:
                    W(e, "qW0", t_wq)
                    W(e, "cV", t_n1all)
                if nq >= 4:
                    W(e, "cS", t_qkev[nq - 4])
                for kc in range(8):
                    i = e.matmul(
                        psA[:, 512 * bk:512 * (bk + 1)],
                        wst[:, D * kc + 128 * oc:D * kc + 128 * (oc + 1)],
                        n1o[:, OWN * kc + 512 * tg:OWN * kc + 512 * (tg + 1)],
                        start=(kc == 0), stop=(kc == 7))
                inc(i, "cT")
            on_ten(f)
            t_mm = tok("cT")

            def f(e, oc=oc, bk=bk, nq=nq, t_mm=t_mm):
                W(e, "cT", t_mm)
                if nq >= 2:
                    W(e, "qC0" if nq % 2 == 0 else "qC1", t_qkwb[nq - 2])
                i = e.activation(qkev[nq % 2][:],
                                 psA[:, 512 * bk:512 * (bk + 1)],
                                 Act.Identity, bias=bq8[:, oc:oc + 1],
                                 scale=0.125)
                inc(i, "cS")
            on_scal(f)
            t_qkev[nq] = tok("cS")

            def f(e, oc=oc, tg=tg, nq=nq, tev=t_qkev[nq]):
                W(e, "cS", tev)
                W(e, "qC0", cnt["qC0"])
                _s = "qC0" if nq % 2 == 0 else "qC1"
                W(e, _s, cnt[_s])
                i = e.dma_start(qd[128 * oc:128 * (oc + 1),
                                   512 * tg:512 * (tg + 1)], qkev[nq % 2][:])
                inc(i, "qC0" if nq % 2 == 0 else "qC1")
            on_sync(f)
            t_qkwb[nq] = tok("qC0" if nq % 2 == 0 else "qC1")
            nq += 1

    def f(e, tprev=tok("cT")):
        W(e, "cT", tprev)
        for kc in range(8):
            W(e, "qW1", cnt["qW1"])
            inc(e.dma_start(wst[:, D * kc:D * (kc + 1)],
                            ipT[128 * kc:128 * (kc + 1), D:2 * D]), "qW1")
    on_gp(f)
    t_wk = tok("qW1")

    for oc in range(8):
        for tg in range(8):
            bk = nq % 4

            def f(e, oc=oc, tg=tg, bk=bk, nq=nq):
                if oc == 0 and tg == 0:
                    W(e, "qW1", t_wk)
                W(e, "cS", t_qkev[nq - 4])
                for kc in range(8):
                    i = e.matmul(
                        psA[:, 512 * bk:512 * (bk + 1)],
                        wst[:, D * kc + 128 * oc:D * kc + 128 * (oc + 1)],
                        xdb[:, S * kc + 512 * tg:S * kc + 512 * (tg + 1)],
                        start=(kc == 0), stop=(kc == 7))
                inc(i, "cT")
            on_ten(f)
            t_mm = tok("cT")

            def f(e, oc=oc, bk=bk, nq=nq, t_mm=t_mm):
                W(e, "cT", t_mm)
                W(e, "qC0" if nq % 2 == 0 else "qC1", t_qkwb[nq - 2])
                i = e.activation(qkev[nq % 2][:],
                                 psA[:, 512 * bk:512 * (bk + 1)],
                                 Act.Identity, bias=ipbcT[:, 8 + oc:9 + oc])
                inc(i, "cS")
            on_scal(f)
            t_qkev[nq] = tok("cS")

            def f(e, oc=oc, tg=tg, nq=nq, tev=t_qkev[nq]):
                W(e, "cS", tev)
                W(e, "qC0", cnt["qC0"])
                _s = "qC0" if nq % 2 == 0 else "qC1"
                W(e, _s, cnt[_s])
                i = e.dma_start(kd[128 * oc:128 * (oc + 1),
                                   512 * tg:512 * (tg + 1)], qkev[nq % 2][:])
                inc(i, "qC0" if nq % 2 == 0 else "qC1")
            on_sync(f)
            t_qkwb[nq] = tok("qC0" if nq % 2 == 0 else "qC1")
            nq += 1
    t_kwb0, t_kwb1 = tok("qC0"), tok("qC1")

    def f(e, tprev=tok("cT"), tsc=tok("cS")):
        W(e, "cT", tprev)
        W(e, "cS", tsc)
        W(e, "qB0", t_xowb[7])
        i = e.memset(vbuf[:, 64::65], 1.0)
        inc(i, "cV")
    on_vec(f)
    t_vones = tok("cV")

    def f(e, tprev=tok("cT")):
        W(e, "cT", tprev)
        for kc in range(8):
            W(e, "qW0", cnt["qW0"])
            inc(e.dma_start(wst[:, D * kc:D * (kc + 1)],
                            ipT[128 * kc:128 * (kc + 1), 2 * D:3 * D]),
                "qW0")
    on_gp(f)
    t_wv = tok("qW0")

    t_vev = {}
    nv = 0
    for tt in range(32):
        for vg in range(2):
            bk = nv % 4

            def f(e, tt=tt, vg=vg, bk=bk, nv=nv):
                if nv == 0:
                    W(e, "qW0", t_wv)
                    W(e, "cS", t_qkev[nq - 1])
                if nv >= 4:
                    W(e, "cV", t_vev[nv - 4])
                for kc in range(8):
                    e.matmul(
                        psA[:, 512 * bk:512 * (bk + 1)],
                        xdb[:, S * kc + 128 * tt:S * kc + 128 * (tt + 1)],
                        wst[:, D * kc + 512 * vg:D * kc + 512 * (vg + 1)],
                        start=(kc == 0), stop=False)
                i = e.matmul(
                    psA[:, 512 * bk:512 * (bk + 1)], ones_b[0:1, :],
                    ipbr_b[0:1, 2 * D + 512 * vg:2 * D + 512 * (vg + 1)],
                    start=False, stop=True)
                inc(i, "cT")
            on_ten(f)
            t_mm = tok("cT")

            def f(e, tt=tt, vg=vg, bk=bk, t_mm=t_mm, nv=nv):
                W(e, "cT", t_mm)
                if nv == 0:
                    W(e, "qB0", t_xowb[7])
                dst = vbuf[:, 1040 * tt + 520 * vg:1040 * tt + 520 * (vg + 1)]
                dst = dst.rearrange("p (hh c) -> p hh c", c=65)[:, :, 0:64]
                i = e.tensor_copy(dst, psA[:, 512 * bk:512 * (bk + 1)])
                inc(i, "cV")
            on_vec(f)
            t_vev[nv] = tok("cV")
            nv += 1
    t_vall = tok("cV")

    t_vmm_last = tok("cT")

    flush()

    ws_jobs = [(wi, wn, sub) for wi, wn in enumerate(["g", "u", "d"])
               for sub in range(16)]
    nws_box = [0]
    t_wsacc = {}

    def emit_ws_job():
        if nws_box[0] >= len(ws_jobs):
            return
        nws = nws_box[0]
        wi, wn, sub = ws_jobs[nws]

        def f(e, wn=wn, sub=sub, nws=nws):
            if nws == 0:
                W(e, "qC0", t_kwb0)
                W(e, "qC1", t_kwb1)
                W(e, "cS", t_qkev[95])
                W(e, "cV", t_vall)
            if nws >= 1:
                W(e, "cV", t_wsacc[nws - 1])
            if wn != "d":
                ch, hf = sub // 2, sub % 2
                W(e, "qF0", cnt["qF0"])
                inc(e.dma_start(wkc[:],
                                wfd[wn][128 * ch:128 * (ch + 1),
                                        2048 * hf:2048 * (hf + 1)]), "qF0")
            else:
                W(e, "qF0", cnt["qF0"])
                inc(e.dma_start(wkc[:, 0:1024],
                                wfd[wn][256 * sub:256 * sub + 128, :]),
                    "qF0")
                W(e, "qF0", cnt["qF0"])
                inc(e.dma_start(wkc[:, 1024:2048],
                                wfd[wn][256 * sub + 128:256 * (sub + 1), :]),
                    "qF0")
        on_sync(f)
        t_ld = tok("qF0")

        def f(e, wi=wi, sub=sub, t_ld=t_ld):
            W(e, "qF0", t_ld)
            col = 16 * wi + sub
            i = e.tensor_reduce(wstat[:, col:col + 1], wkc[:], AX,
                                Alu.add, apply_absolute_value=True)
            inc(i, "cV")
        on_vec(f)
        t_wsacc[nws] = tok("cV")
        nws_box[0] += 1

    owb_box = [0]
    t_owb = [0]

    def emit_owb():
        if owb_box[0]:
            return
        owb_box[0] = 1

        def f(e):
            W(e, "cV", t_wsacc[len(ws_jobs) - 1])
            for kc in range(8):
                W(e, "qW1", cnt["qW1"])
                inc(e.dma_start(owb[:, D * kc:D * (kc + 1)],
                                owT[128 * kc:128 * (kc + 1), :]), "qW1")
        on_gp(f)
        t_owb[0] = tok("qW1")

    t_khld, t_qld = {}, {}
    t_avL, t_expL, t_on, t_odma, t_G = {}, {}, {}, {}, {}
    s_tok = []
    exp_tok = []
    av_tok = []
    m_tok = {}
    it = 0
    kg = 0
    for hd_ in range(NH):
        khsl = hd_ % 2

        def f(e, hd_=hd_, khsl=khsl):
            if hd_ == 0:
                W(e, "qC0", t_kwb0)
                W(e, "qC1", t_kwb1)
                W(e, "cT", t_vmm_last)
            if hd_ >= 2:
                pit = 4 * (hd_ - 2) + 3
                W(e, "cT", t_avL[pit])
            W(e, "qD0", cnt["qD0"])
            i = e.dma_start(kh[khsl][:, 0:S],
                            kd[64 * hd_:64 * (hd_ + 1), 0:S])
            inc(i, "qD0")
        on_sync(f)
        t_khld[hd_] = tok("qD0")

        for i4 in range(4):
            nkb = 8 * i4 + 8
            G = nkb // 2
            sl = it % 2
            t_G[it] = G

            def f(e, i4=i4, hd_=hd_, sl=sl, it=it):
                if it == 0:
                    W(e, "qC0", t_kwb0)
                    W(e, "qC1", t_kwb1)
                    W(e, "cT", t_vmm_last)
                if it >= 2:
                    W(e, "cT", t_avL[it - 2])
                W(e, "qD1", cnt["qD1"])
                i = e.dma_start(qh[sl][:], qd[64 * hd_:64 * (hd_ + 1),
                                              512 * i4:512 * (i4 + 1)])
                inc(i, "qD1")
            on_sync(f)
            t_qld[it] = tok("qD1")

            for b0 in range(0, nkb, 8):
                kg_b = kg
                for kb in range(b0, b0 + 8):
                    sA = kg % 4
                    cs = 64 * max(0, kb - 8 * i4)

                    def f(e, i4=i4, hd_=hd_, it=it, kb=kb, sA=sA, cs=cs,
                          kg=kg, khsl=khsl):
                        if kb == 0:
                            W(e, "qD1", t_qld[it])
                            if i4 == 0:
                                W(e, "qD0", t_khld[hd_])
                        if kg >= 4 and kg % 2 == 0:
                            W(e, "cS", exp_tok[kg - 4])
                        elif kg == 0:
                            W(e, "cV", t_vall)
                        i = e.matmul(
                            psA[:, 512 * sA + cs:512 * (sA + 1)],
                            kh[khsl][:, 128 * kb:128 * (kb + 1)],
                            qh[it % 2][:, cs:512],
                            start=True, stop=True)
                        inc(i, "cT")
                    on_ten(f)
                    s_tok.append(tok("cT"))

                    if kg % 2 == 1:
                        def f(e, kg=kg):
                            W(e, "cT", s_tok[kg])
                            if kg >= 8:
                                W(e, "cT", av_tok[kg - 8])
                            sP = (kg - 1) % 8
                            sAp = (kg - 1) % 4
                            i = e.activation(
                                pr[:, 512 * sP:512 * sP + 1024],
                                psA[:, 512 * sAp:512 * sAp + 1024],
                                Act.Exp)
                            inc(i, "cS")
                        on_scal(f)
                        exp_tok.append(tok("cS"))
                        exp_tok.append(exp_tok[-1])

                        for dkb in (kb - 1, kb):
                            if dkb >= 8 * i4:
                                kgd = kg - (kb - dkb)
                                rc = 64 * (dkb - 8 * i4)

                                def f(e, kgd=kgd, rc=rc, kg=kg):
                                    W(e, "cS", exp_tok[kg])
                                    base = 512 * (kgd % 8) + rc
                                    i = e.tensor_mul(pr[:, base:base + 64],
                                                     pr[:, base:base + 64],
                                                     mku[:])
                                    inc(i, "cV")
                                on_vec(f)
                                m_tok[kgd] = tok("cV")
                    kg += 1

                for kb in range(b0, b0 + 8):
                    kgv = kg_b + (kb - b0)
                    cs = 64 * max(0, kb - 8 * i4)

                    def f(e, it=it, hd_=hd_, kb=kb, kgv=kgv, cs=cs,
                          nkb=nkb):
                        if kb % 2 == 0:
                            W(e, "cS", exp_tok[kgv])
                        if kgv in m_tok:
                            W(e, "cV", m_tok[kgv])
                        pse = psC if it % 2 == 0 else psD
                        if kb == 0 and it >= 2:
                            W(e, "cV", t_on[it - 2])
                        sP = kgv % 8
                        i = e.matmul(pse[0:65, cs:512],
                                     vbuf[:, 1040 * kb + 65 * hd_:
                                          1040 * kb + 65 * hd_ + 65],
                                     pr[:, 512 * sP + cs:512 * (sP + 1)],
                                     start=(kb == 0), stop=(kb == nkb - 1))
                        inc(i, "cT")
                    on_ten(f)
                    av_tok.append(tok("cT"))
            t_avL[it] = av_tok[-1]
            t_expL[it] = exp_tok[-1]

            def f(e, it=it, ta=t_avL[it]):
                W(e, "cT", ta)
                pse = psC if it % 2 == 0 else psD
                i = e.reciprocal(rlsb[64:65, :], pse[64:65, 0:512],
                                 _alp="bf16 rl operand for PE broadcast")
                inc(i, "cV")
            on_vec(f)
            t_rl = tok("cV")

            def f(e, t_rl=t_rl, it=it):
                W(e, "cV", t_rl)
                pse = psC if it % 2 == 0 else psD
                i = e.matmul(pse[:, 512:1024], ones_b[64:65, :],
                             rlsb[64:65, 0:512], start=True, stop=True)
                inc(i, "cT")
            on_ten(f)
            t_rlb = tok("cT")

            def f(e, it=it, sl=sl, t_rlb=t_rlb):
                W(e, "cT", t_rlb)
                pse = psC if it % 2 == 0 else psD
                if it >= 2:
                    W(e, "qE0" if sl == 0 else "qE1", t_odma[it - 2])
                e.tensor_copy(rlbs[:], pse[:, 512:1024])
                e.drain()
                i = e.tensor_mul(onsb[sl][:], pse[0:64, 0:512],
                                 rlbs[0:64, :])
                inc(i, "cV")
            on_vec(f)
            t_on[it] = tok("cV")

            def f(e, it=it, sl=sl, i4=i4, hd_=hd_, ton=t_on[it]):
                W(e, "cV", ton)
                pb = 64 * (hd_ % 2)
                cb = 2048 * (hd_ // 2) + 512 * i4
                qs = "qE0" if sl == 0 else "qE1"
                W(e, qs, cnt[qs])
                i = e.dma_start(obuf[pb:pb + 64, cb:cb + 512], onsb[sl][:])
                inc(i, qs)
            on_gp(f)
            t_odma[it] = tok("qE0" if sl == 0 else "qE1")
            emit_ws_job()
            if nws_box[0] >= len(ws_jobs):
                emit_owb()
            it += 1
    t_attn_d0, t_attn_d1 = tok("qE0"), tok("qE1")
    emit_owb()
    t_C2_cS = tok("cS")
    t_C2_qD0, t_C2_qD1 = tok("qD0"), tok("qD1")

    def f(e, tws=t_wsacc[len(ws_jobs) - 1]):
        W(e, "cV", tws)
        W(e, "cS", t_expL[63])
        i = e.matmul(psA[0:1, 0:48], ones_f[:, 0:1], wstat[:, 0:48],
                     start=True, stop=True)
        inc(i, "cT")
    on_ten(f)
    t_wsmm = tok("cT")

    def f(e, t_wsmm=t_wsmm):
        W(e, "cT", t_wsmm)
        for wi in range(3):
            e.reduce_sum(wstat[0:1, wi:wi + 1],
                         psA[0:1, 16 * wi:16 * (wi + 1)], axis=AX)
        e.drain()
        i = e.tensor_scalar(wstat[0:1, 0:3], wstat[0:1, 0:3],
                            1.0 / (D * DI), 1e-5, Alu.mult, Alu.max)
        inc(i, "cV")
    on_vec(f)
    t_wmean = tok("cV")

    def f(e, t_wmean=t_wmean):
        W(e, "cV", t_wmean)
        i = e.matmul(psA[:, 512:515], ones_f[0:1, :], wstat[0:1, 0:3],
                     start=True, stop=True)
        inc(i, "cT")
    on_ten(f)
    t_wbc = tok("cT")

    def f(e, t_wbc=t_wbc):
        W(e, "cT", t_wbc)
        e.tensor_copy(wsc[:, 3:6], psA[:, 512:515])
        i = e.reciprocal(wsc[:, 0:3], psA[:, 512:515])
        inc(i, "cV")
    on_vec(f)
    t_wsc = tok("cV")

    flush()

    t_av_last = t_avL[63]
    t_on_last = t_on[63]
    t_exp_last = t_expL[63]

    def f(e):
        W(e, "qB0", t_xowb[7])
        W(e, "cT", t_av_last)
        W(e, "cV", t_vall)
        for c in range(8):
            W(e, "qX0", cnt["qX0"])
            inc(e.dma_start(s2[:, OWN * c:OWN * (c + 1)],
                            xod[128 * c:128 * (c + 1), :]), "qX0")
    on_sync(f)
    t_xo_ld = tok("qX0")

    t_d1ev = {}
    nd1 = 0
    for oc in range(8):
        for tg in range(4):
            bk = nd1 % 4

            def f(e, oc=oc, tg=tg, bk=bk, nd1=nd1):
                if nd1 == 0:
                    W(e, "qW1", t_owb[0])
                    W(e, "qE0", t_attn_d0)
                    W(e, "qE1", t_attn_d1)
                    W(e, "cS", t_exp_last)
                    W(e, "cV", t_wsc)
                if nd1 >= 4:
                    W(e, "cV", t_d1ev[nd1 - 4])
                for kc in range(8):
                    i = e.matmul(
                        psA[:, 512 * bk:512 * (bk + 1)],
                        owb[:, D * kc + 128 * oc:D * kc + 128 * (oc + 1)],
                        obuf[:, OWN * kc + 512 * tg:OWN * kc + 512 * (tg + 1)],
                        start=(kc == 0), stop=(kc == 7))
                inc(i, "cT")
            on_ten(f)
            t_mm = tok("cT")

            def f(e, oc=oc, tg=tg, bk=bk, t_mm=t_mm, nd1=nd1):
                W(e, "cT", t_mm)
                if nd1 == 0:
                    W(e, "qX0", t_xo_ld)
                sl_ = s2[:, OWN * oc + 512 * tg:OWN * oc + 512 * (tg + 1)]
                i = e.scalar_tensor_tensor(sl_, psA[:, 512 * bk:512 * (bk + 1)],
                                           obc[:, oc:oc + 1], sl_,
                                           Alu.add, Alu.add)
                inc(i, "cV")
            on_vec(f)
            t_d1ev[nd1] = tok("cV")
            nd1 += 1
    t_s2 = tok("cV")
    t_s2mm_last = tok("cT")

    t_d1b = {}
    nb1 = 0
    for tt in range(16):
        for og in range(2):
            ps = psC if nb1 % 2 == 0 else psD

            def f(e, tt=tt, og=og, ps=ps, nb1=nb1):
                if nb1 == 0:
                    W(e, "cV", t_on_last)
                    W(e, "cV", t_wsc)
                if nb1 >= 2:
                    W(e, "cV", t_d1b[nb1 - 2])
                for kc in range(8):
                    e.matmul(
                        ps[:, 0:512],
                        obuf[:, OWN * kc + 128 * tt:OWN * kc + 128 * (tt + 1)],
                        owb[:, D * kc + 512 * og:D * kc + 512 * (og + 1)],
                        start=(kc == 0), stop=False)
                i = e.matmul(ps[:, 0:512], ones_b[0:1, :],
                             obb2_b[0:1, 512 * og:512 * (og + 1)],
                             start=False, stop=True)
                inc(i, "cT")
            on_ten(f)
            t_mm = tok("cT")

            def f(e, tt=tt, og=og, ps=ps, t_mm=t_mm, nb1=nb1):
                W(e, "cT", t_mm)
                if nb1 == 0:
                    W(e, "cS", t_C2_cS)
                    W(e, "qD0", t_C2_qD0)
                    W(e, "qD1", t_C2_qD1)
                i = e.tensor_copy(
                    bo[:, D * tt + 512 * og:D * tt + 512 * (og + 1)],
                    ps[:, 0:512])
                inc(i, "cV")
            on_vec(f)
            t_d1b[nb1] = tok("cV")
            nb1 += 1
    t_bo1 = tok("cV")
    t_bo1_mm = tok("cT")

    flush()

    t_sq2, t_ms2, t_ms2mm = {}, {}, {}
    nms = 0
    for oc in range(8):
        def f(e, oc=oc):
            if oc == 0:
                W(e, "cV", t_s2)
                W(e, "cT", t_av_last)
            else:
                W(e, "cT", t_ms2mm[oc - 1])
            i = e.activation(sq2s[:], s2[:, OWN * oc:OWN * (oc + 1)],
                             Act.Square)
            inc(i, "cS")
        on_scal(f)
        t_sq2[oc] = tok("cS")

        for g in range(4):
            def f(e, oc=oc, g=g, ts=t_sq2[oc], nms=nms):
                if g == 0:
                    W(e, "cS", ts)
                if nms == 0:
                    W(e, "cV", t_bo1)
                if nms >= 2:
                    W(e, "cV", t_ms2[nms - 2])
                i = e.matmul(psC[0:1, 512 * (nms % 2):512 * (nms % 2) + 512],
                             ones_b[:, 0:1], sq2s[:, 512 * g:512 * (g + 1)],
                             start=True, stop=True)
                inc(i, "cT")
            on_ten(f)
            t_pe = tok("cT")

            def f(e, oc=oc, g=g, t_pe=t_pe, nms=nms):
                W(e, "cT", t_pe)
                if oc == 0:
                    i = e.tensor_copy(r2row[0:1, 512 * g:512 * (g + 1)],
                                      psC[0:1, 512 * (nms % 2):512 * (nms % 2) + 512])
                else:
                    i = e.tensor_add(r2row[0:1, 512 * g:512 * (g + 1)],
                                     r2row[0:1, 512 * g:512 * (g + 1)],
                                     psC[0:1, 512 * (nms % 2):512 * (nms % 2) + 512])
                inc(i, "cV")
            on_vec(f)
            t_ms2[nms] = tok("cV")
            nms += 1
        t_ms2mm[oc] = tok("cT")

    def f(e, tm=t_ms2[nms - 1]):
        W(e, "cV", tm)
        i = e.activation(r2row[:], r2row[:], Act.Sqrt, bias=1e-6,
                         scale=1.0 / D)
        inc(i, "cS")
    on_scal(f)
    t_r2s = tok("cS")

    def f(e, t_r2s=t_r2s):
        W(e, "cS", t_r2s)
        i = e.reciprocal(r2row[:], r2row[:])
        inc(i, "cV")
    on_vec(f)
    t_r2 = tok("cV")

    t_r2c = {}
    for g in range(4):
        def f(e, g=g, t_r2=t_r2):
            if g == 0:
                W(e, "cV", t_r2)
                W(e, "cV", t_s2)
            i = e.matmul(psA[:, 512 * g:512 * (g + 1)], ones_f[0:1, :],
                         r2row[0:1, 512 * g:512 * (g + 1)],
                         start=True, stop=True)
            inc(i, "cT")
        on_ten(f)
        t_pe = tok("cT")

        def f(e, g=g, t_pe=t_pe):
            W(e, "cT", t_pe)
            i = e.tensor_copy(r2b[:, 512 * g:512 * (g + 1)],
                              psA[:, 512 * g:512 * (g + 1)])
            inc(i, "cV")
        on_vec(f)
        t_r2c[g] = tok("cV")

    t_n2wb = {}
    for oc in range(8):
        def f(e, oc=oc):
            if oc == 0:
                W(e, "cV", t_r2c[3])
            i = e.scalar_tensor_tensor(s2[:, OWN * oc:OWN * (oc + 1)],
                                       s2[:, OWN * oc:OWN * (oc + 1)],
                                       ln2c[:, oc:oc + 1], r2b[:],
                                       Alu.mult, Alu.mult)
            inc(i, "cV")
        on_vec(f)
        t_n2 = tok("cV")

        def f(e, oc=oc, t_n2=t_n2):
            W(e, "cV", t_n2)
            W(e, "qZ0", cnt["qZ0"])
            i = e.dma_start(n2d[128 * oc:128 * (oc + 1), :],
                            s2[:, OWN * oc:OWN * (oc + 1)])
            inc(i, "qZ0")
        on_gp(f)
        t_n2wb[oc] = tok("qZ0")
    t_n2all = tok("qZ0")

    t_D2_cS = tok("cS")

    def f(e, t_mpb=t_mpb):
        W(e, "cV", t_mpb)
        W(e, "cT", t_av_last)
        W(e, "cS", t_D2_cS)
        for n_ in range(8):
            for m_ in range(8):
                i = e.tensor_scalar(
                    bigm[:, D * n_ + 128 * m_:D * n_ + 128 * (m_ + 1)],
                    id128[:], mpb[:, 8 * n_ + m_:8 * n_ + m_ + 1],
                    None, Alu.mult)
        inc(i, "cV")
    on_vec(f)
    t_bigm = tok("cV")

    flush()

    wq_jobs = []
    for wi, wn in enumerate(["g", "u", "d"]):
        if wn != "d":
            for ch in range(8):
                for qf in range(4):
                    wq_jobs.append((wn, wi, ch, qf, 1024))
        else:
            for ch in range(32):
                wq_jobs.append((wn, wi, ch, 0, 1024))
    nqw_box = [0]
    t_wqv, t_wqwb, t_wqld = {}, {}, {}

    def _wq_load(j):
        if j >= len(wq_jobs) or j in t_wqld:
            return
        wn, wi, ch, qf, wd = wq_jobs[j]

        def f(e, wn=wn, ch=ch, qf=qf, wd=wd, j=j):
            if j == 0:
                W(e, "cT", t_bo1_mm)
                W(e, "qW0", t_wv)
                W(e, "qW1", t_owb[0])
            if j >= 2:
                W(e, "cV", t_wqv[j - 2])
            W(e, "qF1", cnt["qF1"])
            i = e.dma_start(wqc2[j % 2][:, 0:wd],
                            wfd[wn][128 * ch:128 * (ch + 1),
                                    1024 * qf:1024 * qf + wd])
            inc(i, "qF1")
        on_gp(f)
        t_wqld[j] = tok("qF1")

    def emit_wq_jobs(kjobs):
        for _ in range(kjobs):
            if nqw_box[0] >= len(wq_jobs):
                return
            nqw = nqw_box[0]
            if nqw == 0:
                _wq_load(0)
            _wq_load(nqw + 1)
            wn, wi, ch, qf, wd = wq_jobs[nqw]

            def f(e, wi=wi, wd=wd, nqw=nqw):
                W(e, "qF1", t_wqld[nqw])
                if nqw == 0:
                    W(e, "cV", t_wsc)
                if nqw >= 1:
                    W(e, "qZ0" if (nqw - 1) % 2 == 0 else "qZ1",
                      t_wqwb[nqw - 1])
                wqs = wqc2[nqw % 2]
                e.tensor_scalar(wqs[:, 0:wd], wqs[:, 0:wd],
                                wsc[:, wi:wi + 1], MAGIC, Alu.mult, Alu.add)
                e.drain()
                e.tensor_scalar(wqs[:, 0:wd], wqs[:, 0:wd],
                                MAGIC, -1.0, Alu.subtract, Alu.max)
                e.drain()
                i = e.tensor_scalar(wqo[:, 0:wd], wqs[:, 0:wd],
                                    1.0, None, Alu.min)
                inc(i, "cV")
            on_vec(f)
            t_wqv[nqw] = tok("cV")

            def f(e, wn=wn, ch=ch, qf=qf, wd=wd, tv=t_wqv[nqw], nqw=nqw):
                W(e, "cV", tv)
                qs = "qZ0" if nqw % 2 == 0 else "qZ1"
                if nqw == 0:
                    W(e, "qZ0", t_n2all)
                W(e, qs, cnt[qs])
                i = e.dma_start(wqd[wn][128 * ch:128 * (ch + 1),
                                        1024 * qf:1024 * qf + wd],
                                wqo[:, 0:wd])
                inc(i, qs)
            on_gp(f)
            t_wqwb[nqw] = tok("qZ0" if nqw % 2 == 0 else "qZ1")
            nqw_box[0] += 1

    t_w1mm_last = {0: 0, 1: 0}
    t_w2mm_last = {0: 0, 1: 0}
    t_gelu = {}
    t_boev = {}
    t_n2ld, t_xold = {}, {}
    t_w1ld, t_w2ld = {}, {}
    nbo = 0
    ngel = 0
    for half in range(2):
        def f(e, half=half):
            if half == 0:
                W(e, "cT", t_bo1_mm)
                W(e, "qZ0", t_n2all)
                W(e, "cV", t_bigm)
                W(e, "qE0", t_attn_d0)
                W(e, "qE1", t_attn_d1)
            else:
                W(e, "cT", t_w1mm_last[0])
                W(e, "cT", t_w2mm_last[0])
            for kc in range(8):
                W(e, "qW0", cnt["qW0"])
                inc(e.dma_start(w1h[:, 2048 * kc:2048 * (kc + 1)],
                                w1Td[128 * kc:128 * (kc + 1),
                                     2048 * half:2048 * (half + 1)]), "qW0")
            for j in range(16):
                W(e, "qW1", cnt["qW1"])
                inc(e.dma_start(w2h[:, D * j:D * (j + 1)],
                                w2Td[2048 * half + 128 * j:
                                     2048 * half + 128 * (j + 1), :]),
                    "qW1")
        on_gp(f)
        t_w1ld[half] = tok("qW0")
        t_w2ld[half] = tok("qW1")

        for g in range(8):
            gi = half * 8 + g
            sl = g % 2

            def f(e, half=half, g=g, gi=gi, sl=sl):
                if gi == 0:
                    W(e, "qZ0", t_n2all)
                    W(e, "cT", t_av_last)
                    W(e, "cV", t_wsacc[47])
                if gi >= 2:
                    W(e, "cT", t_w1mm_last_by_g[gi - 2])
                W(e, "qX1", cnt["qX1"])
                i = e.dma_start(
                    n2s[sl][:, :].rearrange("p (c j) -> p c j", j=256),
                    bass.AP(n2d, 256 * g,
                            [[OWN, 128], [128 * OWN, 8], [1, 256]]))
                inc(i, "qX1")
                if half == 1:
                    if gi == 8:
                        W(e, "qW0", t_wv)
                        W(e, "cS", t_qkev[95])
                    if gi >= 2:
                        W(e, "cT", t_mhc_last_by_g.get(gi - 2, 0))
                    W(e, "qX0", cnt["qX0"])
                    i = e.dma_start(
                        xosr[sl][:, :].rearrange("p (c j) -> p c j", j=256),
                        bass.AP(xod, 256 * g,
                                [[OWN, 128], [128 * OWN, 8], [1, 256]]))
                    inc(i, "qX0")
            if half == 0 and g == 0:
                t_w1mm_last_by_g = {}
                t_mhc_last_by_g = {}
            on_sync(f)
            t_n2ld[gi] = tok("qX1")
            t_xold[gi] = tok("qX0")

            for hc in range(16):
                bk = (gi * 16 + hc) % 4

                def f(e, half=half, g=g, gi=gi, sl=sl, hc=hc, bk=bk):
                    if hc == 0:
                        W(e, "qW0", t_w1ld[half])
                        W(e, "qX1", t_n2ld[gi])
                        if gi == 0:
                            W(e, "cV", t_r2c[3])
                    nglob = gi * 16 + hc
                    if nglob >= 4:
                        W(e, "cS", t_gelu[nglob - 4])
                    for kc in range(8):
                        i = e.matmul(
                            psA[:, 512 * bk:512 * bk + 256],
                            w1h[:, 2048 * kc + 128 * hc:
                                 2048 * kc + 128 * (hc + 1)],
                            n2s[sl][:, 256 * kc:256 * (kc + 1)],
                            start=(kc == 0), stop=(kc == 7))
                    inc(i, "cT")
                on_ten(f)
                t_w1mm = tok("cT")
                t_w1mm_last[half] = t_w1mm
                t_w1mm_last_by_g[gi] = t_w1mm

                def f(e, half=half, gi=gi, hc=hc, bk=bk, t_w1mm=t_w1mm,
                      ngel=ngel):
                    W(e, "cT", t_w1mm)
                    if ngel == 0:
                        W(e, "cV", t_bo1)
                    if ngel >= 16:
                        W(e, "cT", t_w2mm_by_g.get(gi - 1, 0))
                    i = e.activation(hth[:, 256 * hc:256 * (hc + 1)],
                                     psA[:, 512 * bk:512 * bk + 256],
                                     Act.Gelu,
                                     bias=b1c[:, 16 * half + hc:
                                              16 * half + hc + 1])
                    inc(i, "cS")
                if half == 0 and g == 0 and hc == 0:
                    t_w2mm_by_g = {}
                on_scal(f)
                t_gelu[gi * 16 + hc] = tok("cS")
                ngel += 1

            for tt2 in range(2):
                for og in range(2):
                    psel = nbo % 4
                    ps = (psC if psel < 2 else psD)
                    pcol = 512 * (psel % 2)

                    def f(e, half=half, g=g, gi=gi, sl=sl, tt2=tt2, og=og,
                          ps=ps, pcol=pcol, nbo=nbo):
                        if tt2 == 0 and og == 0:
                            W(e, "cS", t_gelu[gi * 16 + 15])
                            W(e, "qW1", t_w2ld[half])
                            if half == 1:
                                W(e, "qX0", t_xold[gi])
                            if gi == 0:
                                W(e, "cV", t_bo1)
                        if nbo >= 4:
                            W(e, "cV", t_boev[nbo - 4])
                        for kc2 in range(16):
                            i = e.matmul(
                                ps[:, pcol:pcol + 512],
                                hth[:, 256 * kc2 + 128 * tt2:
                                     256 * kc2 + 128 * (tt2 + 1)],
                                w2h[:, D * kc2 + 512 * og:
                                     D * kc2 + 512 * (og + 1)],
                                start=(kc2 == 0),
                                stop=(half == 0 and kc2 == 15))
                        if half == 1:
                            for kc3 in range(8):
                                i = e.matmul(
                                    ps[:, pcol:pcol + 512],
                                    xosr[sl][:, 256 * kc3 + 128 * tt2:
                                             256 * kc3 + 128 * (tt2 + 1)],
                                    bigm[:, D * kc3 + 512 * og:
                                         D * kc3 + 512 * (og + 1)],
                                    start=False, stop=(kc3 == 7))
                        inc(i, "cT")
                    on_ten(f)
                    t_w2mm = tok("cT")
                    t_w2mm_last[half] = t_w2mm
                    t_w2mm_by_g[gi] = t_w2mm
                    if half == 1:
                        t_mhc_last_by_g[gi] = t_w2mm

                    def f(e, half=half, g=g, tt2=tt2, og=og, ps=ps,
                          pcol=pcol, t_w2mm=t_w2mm):
                        W(e, "cT", t_w2mm)
                        tt = 2 * g + tt2
                        dst = bo[:, D * tt + 512 * og:D * tt + 512 * (og + 1)]
                        if half == 0:
                            i = e.tensor_add(dst, dst, ps[:, pcol:pcol + 512])
                        else:
                            i = e.tensor_add(dst, dst, ps[:, pcol:pcol + 512])
                        inc(i, "cV")
                    on_vec(f)
                    t_boev[nbo] = tok("cV")
                    nbo += 1
            emit_wq_jobs(6)
    t_d3 = tok("cV")
    t_d3mm = tok("cT")

    flush()

    t_x1wb, t_z2wb = {}, {}
    t_e1v = {}
    for tt in range(16):
        sl = tt % 2
        qx = "qX0" if sl == 0 else "qX1"

        def f(e, tt=tt, sl=sl, qx=qx):
            if tt == 0:
                W(e, "cV", t_d3)
                W(e, "cT", t_d3mm)
                W(e, "qW0", t_w1ld[1])
            if tt >= 2:
                W(e, "cV", t_e1v[tt - 2])
            W(e, qx, cnt[qx])
            i = e.dma_start(x0o[sl][:],
                            x0ownd[128 * tt:128 * (tt + 1), :])
            inc(i, qx)
        on_sync(f)
        t_x0ld = tok(qx)

        def f(e, tt=tt, sl=sl):
            if tt == 0:
                W(e, "cV", t_d3)
            bos = bo[:, D * tt:D * (tt + 1)]
            i = e.activation(zt[sl][:], bos, Act.Square,
                             accum_out=stq[:, sl:sl + 1])
            inc(i, "cS")
        on_scal(f)
        t_sq = tok("cS")

        def f(e, tt=tt, sl=sl, t_sq=t_sq, t_x0ld=t_x0ld, qx=qx):
            bos = bo[:, D * tt:D * (tt + 1)]
            e.reduce_sum(stm[:, sl:sl + 1], bos, axis=AX)
            e.drain()
            e.tensor_scalar(stm[:, sl:sl + 1], stm[:, sl:sl + 1], 1.0 / D,
                            None, Alu.mult)
            e.drain()
            e.tensor_mul(stq2[:, sl:sl + 1], stm[:, sl:sl + 1],
                         stm[:, sl:sl + 1])
            e.drain()
            W(e, "cS", t_sq)
            i = e.scalar_tensor_tensor(stv[:, sl:sl + 1], stq[:, sl:sl + 1],
                                       1.0 / D, stq2[:, sl:sl + 1],
                                       Alu.mult, Alu.subtract)
            inc(i, "cV")
        on_vec(f)
        t_var = tok("cV")

        def f(e, sl=sl, t_var=t_var):
            W(e, "cV", t_var)
            i = e.activation(stv[:, sl:sl + 1], stv[:, sl:sl + 1],
                             Act.Sqrt, bias=1e-5)
            inc(i, "cS")
        on_scal(f)
        t_r = tok("cS")

        def f(e, tt=tt, sl=sl, t_r=t_r, t_x0ld=t_x0ld, qx=qx):
            W(e, "cS", t_r)
            W(e, qx, t_x0ld)
            if tt >= 2:
                W(e, "qZ0" if sl == 0 else "qZ1", t_x1wb[tt - 2])
            e.reciprocal(stv[:, sl:sl + 1], stv[:, sl:sl + 1])
            e.drain()
            bos = bo[:, D * tt:D * (tt + 1)]
            e.tensor_scalar(x1t[sl][:], bos, stm[:, sl:sl + 1],
                            stv[:, sl:sl + 1], Alu.subtract, Alu.mult)
            e.drain()
            i = e.tensor_add(x1t[sl][:], x1t[sl][:], x0o[sl][:])
            inc(i, "cV")
        on_vec(f)
        t_x1 = tok("cV")

        def f(e, tt=tt, sl=sl, t_x1=t_x1):
            W(e, "cV", t_x1)
            qs = "qZ0" if sl == 0 else "qZ1"
            if tt == 0:
                W(e, "qZ0", t_wqwb[len(wq_jobs) - 2])
            if tt == 1:
                W(e, "qZ1", t_wqwb[len(wq_jobs) - 1])
            W(e, qs, cnt[qs])
            i = e.dma_start(x1d[128 * tt:128 * (tt + 1), :], x1t[sl][:])
            inc(i, qs)
        on_gp(f)
        t_x1wb[tt] = tok("qZ0" if sl == 0 else "qZ1")

        def f(e, tt=tt, sl=sl, t_x1=t_x1):
            W(e, "cV", t_x1)
            if tt >= 2:
                W(e, "cV", t_e1v[tt - 2])
            i = e.activation(zt[sl][:], x1t[sl][:], Act.Square,
                             accum_out=stq[:, sl:sl + 1])
            inc(i, "cS")
        on_scal(f)
        t_sqz = tok("cS")

        def f(e, sl=sl, t_sqz=t_sqz):
            e.reduce_sum(stm[:, sl:sl + 1], x1t[sl][:], axis=AX)
            e.drain()
            e.tensor_scalar(stm[:, sl:sl + 1], stm[:, sl:sl + 1], 1.0 / D,
                            None, Alu.mult)
            e.drain()
            e.tensor_mul(stq2[:, sl:sl + 1], stm[:, sl:sl + 1],
                         stm[:, sl:sl + 1])
            e.drain()
            W(e, "cS", t_sqz)
            i = e.scalar_tensor_tensor(stv[:, sl:sl + 1], stq[:, sl:sl + 1],
                                       1.0 / D, stq2[:, sl:sl + 1],
                                       Alu.mult, Alu.subtract)
            inc(i, "cV")
        on_vec(f)
        t_varz = tok("cV")

        def f(e, sl=sl, t_varz=t_varz):
            W(e, "cV", t_varz)
            i = e.activation(stv[:, sl:sl + 1], stv[:, sl:sl + 1],
                             Act.Sqrt, bias=1e-5)
            inc(i, "cS")
        on_scal(f)
        t_rz = tok("cS")

        def f(e, tt=tt, sl=sl, t_rz=t_rz):
            W(e, "cS", t_rz)
            if tt >= 2:
                W(e, "qV0" if sl == 0 else "qV1", t_z2wb[tt - 2])
            e.reciprocal(stv[:, sl:sl + 1], stv[:, sl:sl + 1])
            e.drain()
            i = e.tensor_scalar(zt[sl][:], x1t[sl][:], stm[:, sl:sl + 1],
                                stv[:, sl:sl + 1], Alu.subtract, Alu.mult)
            e.drain()
            i = e.tensor_reduce(stq[:, sl:sl + 1], zt[sl][:], AX,
                                Alu.max, apply_absolute_value=True)
            e.drain()
            e.tensor_scalar(stq[:, sl:sl + 1], stq[:, sl:sl + 1], 1e-5,
                            1.0 / 127.0, Alu.max, Alu.mult)
            e.drain()
            e.tensor_copy(zscin[:, tt:tt + 1], stq[:, sl:sl + 1])
            e.reciprocal(stq2[:, sl:sl + 1], stq[:, sl:sl + 1])
            e.drain()
            e.tensor_scalar(zt[sl][:], zt[sl][:], stq2[:, sl:sl + 1],
                            MAGIC, Alu.mult, Alu.add)
            e.drain()
            e.tensor_scalar(zt[sl][:], zt[sl][:], MAGIC, -128.0,
                            Alu.subtract, Alu.max)
            e.drain()
            i = e.tensor_scalar(z2qs[sl][:], zt[sl][:], 127.0, None,
                                Alu.min)
            inc(i, "cV")
        on_vec(f)
        t_e1v[tt] = tok("cV")

        def f(e, tt=tt, sl=sl, tv=t_e1v[tt]):
            W(e, "cV", tv)
            qs = "qV0" if sl == 0 else "qV1"
            W(e, qs, cnt[qs])
            i = e.dma_start(z1d[128 * tt:128 * (tt + 1), :], z2qs[sl][:])
            inc(i, qs)
        on_gp(f)
        t_z2wb[tt] = tok("qV0" if sl == 0 else "qV1")
    t_e1 = tok("cV")
    t_z2wb_all = (tok("qV0"), tok("qV1"))
    t_x1wb_all = (tok("qZ0"), tok("qZ1"))
    t_e1_cS = tok("cS")
    t_e1_qX0, t_e1_qX1 = tok("qX0"), tok("qX1")

    flush()

    def f(e):
        W(e, "qV0", t_z2wb_all[0])
        W(e, "qV1", t_z2wb_all[1])
        W(e, "cV", t_e1)
        W(e, "cT", t_d3mm)
        W(e, "qW1", t_w2ld[1])
        W(e, "qX1", t_n2ld[15])
        W(e, "cS", t_gelu[255])
        for c in range(8):
            W(e, "qX0", cnt["qX0"])
            inc(e.dma_start(z2qT[:, OWN * c:OWN * (c + 1)],
                            z1d[0:OWN, 128 * c:128 * (c + 1)],
                            transpose=True), "qX0")
    on_sync(f)
    t_z2qT = tok("qX0")

    def f(e):
        W(e, "cV", t_e1)
        W(e, "cV", t_wsc)
        e.tensor_scalar(invgc[:], zscin[:], wsc[:, 3:4], None, Alu.mult)
        i = e.tensor_scalar(invuc[:], zscin[:], wsc[:, 4:5], None, Alu.mult)
        inc(i, "cV")
    on_vec(f)
    t_invg = tok("cV")

    t_wgld, t_ev3, t_pwb = {}, {}, {}
    ne3 = 0
    for ig in range(8):
        sl = ig % 2

        def f(e, ig=ig, sl=sl):
            if ig == 0:
                W(e, "qZ0", t_wqwb[len(wq_jobs) - 2])
                W(e, "qZ1", t_wqwb[len(wq_jobs) - 1])
                W(e, "cT", t_d3mm)
                W(e, "cV", t_d3)
                W(e, "cS", t_gelu[255])
                W(e, "qX1", t_n2ld[15])
            if ig >= 2:
                W(e, "cT", t_e3mm_by_ig[ig - 2])
            W(e, "qW0", cnt["qW0"])
            i = e.dma_start(
                wgs[sl][:, :].rearrange("p (c j) -> p c j", j=512),
                bass.AP(wqd["g"], 512 * ig,
                        [[DI, 128], [128 * DI, 8], [1, 512]]))
            inc(i, "qW0")
            W(e, "qW1", cnt["qW1"])
            i = e.dma_start(
                wus[sl][:, :].rearrange("p (c j) -> p c j", j=512),
                bass.AP(wqd["u"], 512 * ig,
                        [[DI, 128], [128 * DI, 8], [1, 512]]))
            inc(i, "qW1")
        if ig == 0:
            t_e3mm_by_ig = {}
        on_gp(f)
        t_wgld[ig] = (tok("qW0"), tok("qW1"))

        for tt in range(16):
            pse = psC if ne3 % 2 == 0 else psD

            def f(e, ig=ig, sl=sl, tt=tt, ne3=ne3, pse=pse):
                if tt == 0:
                    W(e, "qW0", t_wgld[ig][0])
                    W(e, "qW1", t_wgld[ig][1])
                    if ig == 0:
                        W(e, "qX0", t_z2qT)
                        W(e, "cV", t_e1)
                if ne3 >= 2:
                    W(e, "cV", t_ev3[ne3 - 2])
                for kc in range(8):
                    e.matmul(pse[:, 0:512],
                             z2qT[:, OWN * kc + 128 * tt:
                                  OWN * kc + 128 * (tt + 1)],
                             wgs[sl][:, 512 * kc:512 * (kc + 1)],
                             start=(kc == 0), stop=(kc == 7))
                for kc in range(8):
                    i = e.matmul(pse[:, 512:1024],
                                 z2qT[:, OWN * kc + 128 * tt:
                                      OWN * kc + 128 * (tt + 1)],
                                 wus[sl][:, 512 * kc:512 * (kc + 1)],
                                 start=(kc == 0), stop=(kc == 7))
                inc(i, "cT")
            on_ten(f)
            t_mm = tok("cT")
            t_e3mm_by_ig[ig] = t_mm

            def f(e, tt=tt, t_mm=t_mm, pse=pse, ig=ig, ne3=ne3):
                W(e, "cT", t_mm)
                if ig == 0 and tt == 0:
                    W(e, "cV", t_invg)
                if ne3 >= 1:
                    W(e, "cV", t_ev3[ne3 - 1])
                i = e.activation(gsb[:], pse[:, 0:512], Act.Silu,
                                 scale=invgc[:, tt:tt + 1])
                inc(i, "cS")
            on_scal(f)
            t_silu = tok("cS")

            def f(e, ig=ig, tt=tt, ne3=ne3, t_silu=t_silu, pse=pse):
                W(e, "cS", t_silu)
                if ne3 >= 2:
                    W(e, "qZ0" if ne3 % 2 == 0 else "qZ1", t_pwb[ne3 - 2])
                sl2 = ne3 % 2
                e.scalar_tensor_tensor(psb[sl2][:], pse[:, 512:1024],
                                       invuc[:, tt:tt + 1], gsb[:],
                                       Alu.mult, Alu.mult)
                e.drain()
                col = 16 * ig + tt
                i = e.tensor_reduce(pmxall[:, col:col + 1], psb[sl2][:], AX,
                                    Alu.max, apply_absolute_value=True)
                inc(i, "cV")
            on_vec(f)
            t_ev3[ne3] = tok("cV")

            def f(e, ig=ig, tt=tt, ne3=ne3, tv=t_ev3[ne3]):
                W(e, "cV", tv)
                qs = "qZ0" if ne3 % 2 == 0 else "qZ1"
                if ne3 == 0:
                    W(e, "qZ0", t_x1wb_all[0])
                if ne3 == 1:
                    W(e, "qZ1", t_x1wb_all[1])
                W(e, qs, cnt[qs])
                i = e.dma_start(pdd[128 * tt:128 * (tt + 1),
                                    512 * ig:512 * (ig + 1)],
                                psb[ne3 % 2][:])
                inc(i, qs)
            on_gp(f)
            t_pwb[ne3] = tok("qZ0" if ne3 % 2 == 0 else "qZ1")
            ne3 += 1
    t_e3 = tok("cV")
    t_e3_cS = tok("cS")

    flush()

    def f(e, t_e3=t_e3):
        W(e, "cV", t_e3)
        vw = pmxall[:, :].rearrange("p (i t) -> p t i", t=16)
        e.tensor_reduce(pscm[:], vw, AX, Alu.max)
        e.drain()
        e.tensor_scalar(pscm[:], pscm[:], 1e-5, 1.0 / 127.0, Alu.max,
                        Alu.mult)
        e.drain()
        e.tensor_scalar(psci[:], pscm[:], wsc[:, 5:6], None, Alu.mult)
        e.drain()
        i = e.reciprocal(pscm[:], pscm[:])
        inc(i, "cV")
    on_vec(f)
    t_psc = tok("cV")

    t_pld, t_pqv, t_pqwb = {}, {}, {}
    for tt in range(16):
        def f(e, tt=tt):
            if tt == 0:
                W(e, "qZ0", t_pwb[126])
                W(e, "qZ1", t_pwb[127])
            if tt >= 2:
                W(e, "cV", t_pqv[tt - 2])
            W(e, "qX1", cnt["qX1"])
            i = e.dma_start(ptile2[tt % 2][:], pdd[128 * tt:128 * (tt + 1), :])
            inc(i, "qX1")
        on_sync(f)
        t_pld[tt] = tok("qX1")

        def f(e, tt=tt, tl=t_pld[tt]):
            W(e, "qX1", tl)
            if tt == 0:
                W(e, "cV", t_psc)
            if tt >= 2:
                W(e, "qZ1", t_pqwb[tt - 2])
            e.tensor_scalar(pwide[:], ptile2[tt % 2][:], pscm[:, tt:tt + 1],
                            MAGIC, Alu.mult, Alu.add)
            e.drain()
            e.tensor_scalar(pwide[:], pwide[:], MAGIC, -128.0,
                            Alu.subtract, Alu.max)
            e.drain()
            i = e.tensor_scalar(pqtile2[tt % 2][:], pwide[:], 127.0, None,
                                Alu.min)
            inc(i, "cV")
        on_vec(f)
        t_pqv[tt] = tok("cV")

        def f(e, tt=tt, tv=t_pqv[tt]):
            W(e, "cV", tv)
            W(e, "qZ1", cnt["qZ1"])
            i = e.dma_start(pqd[128 * tt:128 * (tt + 1), :], pqtile2[tt % 2][:])
            inc(i, "qZ1")
        on_gp(f)
        t_pqwb[tt] = tok("qZ1")
    t_pq_all = tok("qZ1")

    flush()

    t_pqt, t_wdld, t_fev = {}, {}, {}
    nf = 0
    for icg in range(4):
        def f(e, icg=icg):
            if icg == 0:
                W(e, "qZ1", t_pqwb[7])
                W(e, "cV", t_e3)
                W(e, "cT", t_e3mm_by_ig[7])
                W(e, "cS", t_e3_cS)
            if icg >= 1:
                W(e, "cT", t_e5mm_by_icg[icg - 1])
            for ic in range(8):
                W(e, "qX0", cnt["qX0"])
                inc(e.dma_start(pqt[:, 2048 * ic:2048 * ic + 1024],
                                pqd[0:1024, 128 * (8 * icg + ic):
                                    128 * (8 * icg + ic + 1)],
                                transpose=True), "qX0")
        if icg == 0:
            t_e5mm_by_icg = {}
        on_sync(f)
        t_pqt[icg] = tok("qX0")

        def f(e, icg=icg):
            if icg == 0:
                W(e, "qZ1", t_pq_all)
            for ic in range(8):
                W(e, "qX0", cnt["qX0"])
                inc(e.dma_start(pqt[:, 2048 * ic + 1024:2048 * (ic + 1)],
                                pqd[1024:2048, 128 * (8 * icg + ic):
                                    128 * (8 * icg + ic + 1)],
                                transpose=True), "qX0")
        on_sync(f)
        t_pqt_h1 = {} if icg == 0 else t_pqt_h1
        t_pqt_h1[icg] = tok("qX0")

        def f(e, icg=icg):
            if icg == 0:
                W(e, "cV", t_e3)
                W(e, "cT", t_e3mm_by_ig[7])
            if icg >= 1:
                W(e, "cT", t_e5mm_by_icg[icg - 1])
            W(e, "qW0", cnt["qW0"])
            i = e.dma_start(
                wds[:, :].rearrange("p (c j) -> p c j", j=D),
                bass.AP(wqd["d"], 128 * 8 * icg * D,
                        [[D, 128], [128 * D, 8], [1, D]]))
            inc(i, "qW0")
        on_gp(f)
        t_wdld[icg] = tok("qW0")

        for tt in range(16):
            psel = nf % 2
            ps = psC if psel == 0 else psD

            def f(e, icg=icg, tt=tt, ps=ps, nf=nf):
                if tt == 0:
                    W(e, "qX0", t_pqt[icg])
                    W(e, "qW0", t_wdld[icg])
                if tt == 8:
                    W(e, "qX0", t_pqt_h1[icg])
                if nf >= 2:
                    W(e, "cV", t_fev[nf - 2])
                for ic in range(8):
                    for og in range(2):
                        i = e.matmul(
                            ps[:, 512 * og:512 * (og + 1)],
                            pqt[:, 2048 * ic + 128 * tt:
                                2048 * ic + 128 * (tt + 1)],
                            wds[:, D * ic + 512 * og:D * ic + 512 * (og + 1)],
                            start=(ic == 0), stop=(ic == 7))
                inc(i, "cT")
            on_ten(f)
            t_mm = tok("cT")
            t_e5mm_by_icg[icg] = t_mm

            def f(e, icg=icg, tt=tt, ps=ps, t_mm=t_mm):
                W(e, "cT", t_mm)
                if icg == 0 and tt == 0:
                    W(e, "cS", t_e1_cS)
                    W(e, "qX0", t_e1_qX0)
                    W(e, "qX1", t_e1_qX1)
                if icg == 0 and tt == 4:
                    W(e, "qZ1", t_pq_all)
                dst = fsb[:, D * tt:D * (tt + 1)]
                if icg == 0:
                    i = e.tensor_copy(dst, ps[:, 0:1024])
                else:
                    i = e.tensor_add(dst, dst, ps[:, 0:1024])
                inc(i, "cV")
            on_vec(f)
            t_fev[nf] = tok("cV")
            nf += 1
    t_e5 = tok("cV")

    flush()

    t_x1ld, t_yv, t_ywb = {}, {}, {}
    for tt in range(16):
        sl = tt % 4
        qx = "qX1"
        qy = "qY0" if tt % 2 == 0 else "qY1"

        def f(e, tt=tt, sl=sl, qx=qx):
            if tt == 0:
                W(e, "qZ0", t_x1wb_all[0])
                W(e, "qZ1", t_x1wb_all[1])
                W(e, "cV", t_e5)
                W(e, "cT", t_e5mm_by_icg[3])
                W(e, "qW1", t_wgld[7][1])
                W(e, "qW0", t_wdld[3])
            if tt >= 4:
                W(e, "cV", t_yv[tt - 4])
            W(e, qx, cnt[qx])
            i = e.dma_start(x1r[sl][:], x1d[128 * tt:128 * (tt + 1), :])
            inc(i, qx)
        on_sync(f)
        t_x1ld[tt] = tok(qx)

        def f(e, tt=tt, sl=sl):
            fslice = fsb[:, D * tt:D * (tt + 1)]
            i = e.tensor_scalar(fslice, fslice, psci[:, tt:tt + 1], None,
                                Alu.mult, Alu.add,
                                accum_out=stm[:, sl:sl + 1])
            inc(i, "cV")
        on_vec(f)
        t_fs = tok("cV")

        def f(e, tt=tt, sl=sl, t_fs=t_fs, qy=qy):
            W(e, "cV", t_fs)
            if tt >= 4:
                W(e, qy, t_ywb[tt - 4])
            i = e.activation(otl[sl][:],
                             fsb[:, D * tt:D * (tt + 1)], Act.Square,
                             accum_out=stq[:, sl:sl + 1])
            inc(i, "cS")
        on_scal(f)
        t_sq = tok("cS")

        def f(e, sl=sl, t_sq=t_sq):
            e.tensor_scalar(stm[:, sl:sl + 1], stm[:, sl:sl + 1], 1.0 / D,
                            None, Alu.mult)
            e.drain()
            e.tensor_mul(stq2[:, sl:sl + 1], stm[:, sl:sl + 1],
                         stm[:, sl:sl + 1])
            e.drain()
            W(e, "cS", t_sq)
            i = e.scalar_tensor_tensor(stv[:, sl:sl + 1], stq[:, sl:sl + 1],
                                       1.0 / D, stq2[:, sl:sl + 1],
                                       Alu.mult, Alu.subtract)
            inc(i, "cV")
        on_vec(f)
        t_var = tok("cV")

        def f(e, sl=sl, t_var=t_var):
            W(e, "cV", t_var)
            i = e.activation(stv[:, sl:sl + 1], stv[:, sl:sl + 1],
                             Act.Sqrt, bias=1e-5)
            inc(i, "cS")
        on_scal(f)
        t_r = tok("cS")

        def f(e, tt=tt, sl=sl, t_r=t_r, tl=t_x1ld[tt], qy=qy):
            W(e, "cS", t_r)
            W(e, "qX1", tl)
            if tt >= 4:
                W(e, qy, t_ywb[tt - 4])
            e.reciprocal(stv[:, sl:sl + 1], stv[:, sl:sl + 1])
            e.drain()
            e.tensor_scalar(otl[sl][:], fsb[:, D * tt:D * (tt + 1)],
                            stm[:, sl:sl + 1], stv[:, sl:sl + 1],
                            Alu.subtract, Alu.mult)
            e.drain()
            i = e.tensor_add(otl[sl][:], otl[sl][:], x1r[sl][:])
            inc(i, "cV")
        on_vec(f)
        t_yv[tt] = tok("cV")

        def f(e, tt=tt, sl=sl, tv=t_yv[tt], qy=qy):
            W(e, "cV", tv)
            W(e, qy, cnt[qy])
            i = e.dma_start(y[128 * tt:128 * (tt + 1), :], otl[sl][:])
            inc(i, qy)
        on_sync(f)
        t_ywb[tt] = tok(qy)

    def f(e):
        W(e, "qY0", cnt["qY0"])
        W(e, "qY1", cnt["qY1"])
    on_sync(f)

    flush()
    blk.__exit__(None, None, None)
    return nc


_NC_CACHE = [None]


def kernel(**inputs):
    def g(name):
        return np.ascontiguousarray(np.asarray(inputs[name], dtype=np.float32))

    x = g("L0_attn_x")
    ipT = np.ascontiguousarray(g("L0_attn_in_proj_w").T)
    ipb = g("L0_attn_in_proj_b").reshape(1, 3 * D)
    owT = np.ascontiguousarray(g("L0_attn_out_w").T)
    obr = g("L0_attn_out_b").reshape(1, D)
    w1T = np.ascontiguousarray(g("L0_attn_w1").T)
    b1v = g("L0_attn_b1")
    w2T = np.ascontiguousarray(g("L0_attn_w2").T)
    b2r = g("L0_attn_b2").reshape(1, D)
    wgT = np.ascontiguousarray(g("L0_ffn_w_gate").T)
    wuT = np.ascontiguousarray(g("L0_ffn_w_up").T)
    wdT = np.ascontiguousarray(g("L0_ffn_w_down").T)
    ddv = g("L0_attn_delta_decay")
    ln1v = g("L0_attn_ln1_w")
    ln2v = g("L0_attn_ln2_w")
    gwT8 = np.ascontiguousarray(g("L0_attn_gate_w")[16:24].T)
    gb8 = np.ascontiguousarray(g("L0_attn_gate_b")[16:24].reshape(8, 1))
    phi = g("L0_attn_phi_res")
    id8 = np.eye(8, dtype=np.float32)

    import ml_dtypes
    BF = ml_dtypes.bfloat16
    id128 = np.eye(128, dtype=BF)
    masks = []
    for h in range(2):
        mk = np.zeros((128, 64), dtype=np.float32)
        for r in range(128):
            for q in range(64):
                if r <= 64 * h + q:
                    mk[r, q] = 1.0
        masks.append(mk.astype(BF))

    if _NC_CACHE[0] is None:
        _NC_CACHE[0] = build_nc()
    nc = _NC_CACHE[0]

    in_maps = []
    for c in range(8):
        b, h = c // 2, c % 2
        in_maps.append({
            "x0": np.ascontiguousarray(x[b]),
            "ipT": ipT, "ipb": ipb, "owT": owT, "obr": obr,
            "w1T": w1T, "b1v": b1v, "w2T": w2T, "b2r": b2r,
            "wgT": wgT, "wuT": wuT, "wdT": wdT,
            "ddv": ddv, "ln1v": ln1v, "ln2v": ln2v,
            "gwT8": gwT8, "gb8": gb8, "phi": phi, "id8d": id8,
            "hcold": np.stack([np.full(128, h, np.float32),
                               np.full(128, 1.0 - h, np.float32)],
                              axis=1),
            "x0ownd": np.ascontiguousarray(
                x[b].reshape(32, 2, 64, D)[:, h].reshape(OWN, D)),
            "maskUd": masks[h],
            "id128d": id128,
        })

    res = run_bass_kernel_spmd(nc, in_maps, core_ids=list(range(8)))
    out = np.empty((B, S, D), dtype=np.float32)
    for c in range(8):
        b, h = c // 2, c % 2
        yc = np.asarray(res.results[c]["y"])
        out[b].reshape(32, 2, 64, D)[:, h] = yc.reshape(32, 64, D)
    return out


# version 20
# speedup vs baseline: 1.0779x; 1.0043x over previous
import numpy as np
import concourse.bass as bass
import concourse.mybir as mybir
from concourse.bass_utils import run_bass_kernel_spmd

F32 = mybir.dt.float32
BF16 = mybir.dt.bfloat16
I32 = mybir.dt.int32
Alu = mybir.AluOpType
Act = mybir.ActivationFunctionType
AX = mybir.AxisListType.X

B, S, D = 4, 4096, 1024
OWN = S // 2
NH, HD = 16, 64
DI = 4096
MAGIC = float(np.float32(1.5 * 2 ** 23))
KB = 1024


def build_nc():
    nc = bass.Bass("TRN2", target_bir_lowering=False)

    x0 = nc.dram_tensor("x0", [S, D], F32, kind="ExternalInput")
    ipT = nc.dram_tensor("ipT", [D, 3 * D], F32, kind="ExternalInput")
    ipb = nc.dram_tensor("ipb", [1, 3 * D], F32, kind="ExternalInput")
    owT = nc.dram_tensor("owT", [D, D], F32, kind="ExternalInput")
    obr = nc.dram_tensor("obr", [1, D], F32, kind="ExternalInput")
    w1Td = nc.dram_tensor("w1T", [D, DI], F32, kind="ExternalInput")
    b1v = nc.dram_tensor("b1v", [DI], F32, kind="ExternalInput")
    w2Td = nc.dram_tensor("w2T", [DI, D], F32, kind="ExternalInput")
    b2r = nc.dram_tensor("b2r", [1, D], F32, kind="ExternalInput")
    wfd = {"g": nc.dram_tensor("wgT", [D, DI], F32, kind="ExternalInput"),
           "u": nc.dram_tensor("wuT", [D, DI], F32, kind="ExternalInput"),
           "d": nc.dram_tensor("wdT", [DI, D], F32, kind="ExternalInput")}
    ddv = nc.dram_tensor("ddv", [D], F32, kind="ExternalInput")
    ln1v = nc.dram_tensor("ln1v", [D], F32, kind="ExternalInput")
    ln2v = nc.dram_tensor("ln2v", [D], F32, kind="ExternalInput")
    gwT8 = nc.dram_tensor("gwT8", [D, 8], F32, kind="ExternalInput")
    gb8d = nc.dram_tensor("gb8", [8, 1], F32, kind="ExternalInput")
    phid = nc.dram_tensor("phi", [8, 8], F32, kind="ExternalInput")
    id8d = nc.dram_tensor("id8d", [8, 8], F32, kind="ExternalInput")
    hcold = nc.dram_tensor("hcold", [128, 2], F32, kind="ExternalInput")
    x0ownd = nc.dram_tensor("x0ownd", [OWN, D], F32, kind="ExternalInput")
    maskUd = nc.dram_tensor("maskUd", [128, 64], BF16, kind="ExternalInput")
    id128d = nc.dram_tensor("id128d", [128, 128], BF16, kind="ExternalInput")

    y = nc.dram_tensor("y", [OWN, D], F32, kind="ExternalOutput")
    z1d = nc.dram_tensor("z1d", [S, D], BF16)
    qd = nc.dram_tensor("qd", [D, OWN], BF16)
    kd = nc.dram_tensor("kd", [D, S], BF16)
    xod = nc.dram_tensor("xod", [D, OWN], BF16)
    n2d = nc.dram_tensor("n2d", [D, OWN], BF16)
    x1d = nc.dram_tensor("x1d", [OWN, D], BF16)
    pdd = nc.dram_tensor("pdd", [OWN, DI], BF16)
    pqd = nc.dram_tensor("pqd", [OWN, DI], BF16)
    wqd = {"g": nc.dram_tensor("wgqd", [D, DI], BF16),
           "u": nc.dram_tensor("wuqd", [D, DI], BF16),
           "d": nc.dram_tensor("wdqd", [DI, D], BF16)}
    m64d = nc.dram_tensor("m64d", [8, 8], F32)

    def SBA(name, shape, dtype, off_kb):
        off = int(round(off_kb * KB / 32.0)) * 32
        return nc.alloc_sbuf_tensor_at(name, shape, dtype, offset=off)

    ones_b = SBA("ones_b", [128, 128], BF16, 24.0)
    ones_f = SBA("ones_f", [128, 128], F32, 24.25)
    mku = SBA("mku", [128, 64], BF16, 24.75)
    id128 = SBA("id128", [128, 128], BF16, 68.5)
    cst = SBA("cst", [128, 8], F32, 68.8)
    beta = SBA("beta", [128, 8], F32, 25.0)
    betm1 = SBA("betm1", [128, 8], F32, 25.05)
    betacol = SBA("betacol", [128, 1], F32, 25.975)
    ln1c = SBA("ln1c", [128, 8], F32, 25.1)
    ln2c = SBA("ln2c", [128, 8], F32, 25.15)
    b1c = SBA("b1c", [128, 32], F32, 25.2)
    ipbcT = SBA("ipbcT", [128, 24], F32, 25.35)
    bq8 = SBA("bq8", [128, 8], F32, 25.45)
    obc = SBA("obc", [128, 8], F32, 25.5)
    xs = SBA("xs", [128, 8], F32, 25.55)
    xsb = SBA("xsb", [128, 8], BF16, 25.6)
    hcol = SBA("hcol", [128, 2], F32, 25.65)
    sk1 = SBA("sk1", [8, 8], F32, 25.7)
    sk2 = SBA("sk2", [8, 8], F32, 25.75)
    skr = SBA("skr", [8, 1], F32, 25.8)
    id8s = SBA("id8s", [8, 8], F32, 25.85)
    g8 = SBA("g8", [8, 1], F32, 25.9)
    gb8s = SBA("gb8s", [8, 1], F32, 25.95)
    mpr = SBA("mpr", [1, 64], F32, 26.0)
    mpb = SBA("mpb", [128, 64], F32, 26.3)
    gw8b = SBA("gw8b", [128, 64], BF16, 26.6)
    gw8f = SBA("gw8f", [128, 64], F32, 26.75)
    wstat = SBA("wstat", [128, 48], F32, 27.05)
    wsc = SBA("wsc", [128, 8], F32, 27.3)
    zscin = SBA("zscin", [128, 16], F32, 27.35)
    invgc = SBA("invgc", [128, 16], F32, 27.45)
    invuc = SBA("invuc", [128, 16], F32, 27.55)
    pmxall = SBA("pmxall", [128, 128], F32, 27.65)
    pscm = SBA("pscm", [128, 16], F32, 28.15)
    psci = SBA("psci", [128, 16], F32, 28.25)
    ipbr_b = SBA("ipbr_b", [1, 3 * D], BF16, 28.4)
    obb2_b = SBA("obb2_b", [1, D], BF16, 34.4)
    tr1 = SBA("tr1", [1, D], F32, 36.4)
    tr2 = SBA("tr2", [1, D], F32, 40.4)
    r1row = SBA("r1row", [1, S], F32, 44.4)
    r2row = SBA("r2row", [1, OWN], F32, 60.4)

    R1, R2, R3 = 71.0, 135.0, 200.0
    x0s = [SBA(f"x0s{i}", [128, D], F32, R1 + 4 * i) for i in range(4)]
    z1s = [SBA(f"z1s{i}", [128, D], BF16, R1 + 24 + 2 * i) for i in range(4)]
    stm = SBA("stm", [128, 4], F32, R3 + 12)
    stq = SBA("stq", [128, 4], F32, R3 + 12.05)
    stv = SBA("stv", [128, 4], F32, R3 + 12.1)
    stq2 = SBA("stq2", [128, 4], F32, R3 + 12.15)
    sqs = [SBA(f"sqs{i}", [128, D], BF16, R1 + 16 + 2 * i) for i in range(4)]
    xdb = SBA("xdb", [128, 8 * S], BF16, R1)
    z1T2 = [SBA(f"z1T{i}", [128, S], BF16, R2 + 48 * i) for i in range(2)]
    sprime = SBA("sprime", [128, 2049], F32, R2 + 8)
    xdcs = SBA("xdcs", [128, S], F32, R2 + 16.2)
    sqc = SBA("sqc", [128, S], BF16, R2 + 32.2)
    xoc = SBA("xoc", [128, OWN], BF16, R2 + 40.2)
    msacc = SBA("msacc", [1, S], F32, R2 + 44.2)
    r1b = SBA("r1b", [128, S], F32, R3)
    n1o = SBA("n1o", [128, 8 * OWN], BF16, R2)
    vbuf = SBA("vbuf", [128, 32 * 1040], BF16, R2)
    wst = SBA("wst", [128, 8 * D], BF16, R3)
    qkev = [SBA(f"qkev{i}", [128, 512], BF16, R3 + 16 + i) for i in range(2)]
    obuf = SBA("obuf", [128, 8 * OWN], BF16, R1)
    pr = SBA("pr", [128, 4096], BF16, R1 + 32)
    kh = [SBA(f"kh{i}", [64, S], BF16, R1 + 40 + 8 * i) for i in range(2)]
    qh = [SBA(f"qh{i}", [64, 512], BF16, R1 + 56 + i) for i in range(2)]
    onsb = [SBA(f"onsb{i}", [64, 512], BF16, R1 + 58 + i) for i in range(2)]
    rlsb = SBA("rlsb", [128, 512], BF16, R1 + 60)
    rlbs = SBA("rlbs", [128, 512], F32, R1 + 62)
    wkc = SBA("wkc", [128, 2048], F32, R3 + 16 - 0.03125)
    owb = SBA("owb", [128, 8 * D], BF16, R3)
    bo = SBA("bo", [128, 16 * D], BF16, R1 + 32)
    s2 = SBA("s2", [128, 8 * OWN], BF16, R2)
    sq2s = SBA("sq2s", [128, 2048], BF16, R2 + 32)
    r2b = SBA("r2b", [128, OWN], F32, R2 + 36.2)
    xosr = [SBA(f"xosr{i}", [128, 2048], BF16, R3 + 12 + 4 * i) for i in range(2)]
    w1h = SBA("w1h", [128, 8 * 2048], BF16, R1)
    w2h = SBA("w2h", [128, 16 * D], BF16, R2)
    bigm = SBA("bigm", [128, 8 * D], BF16, R2 + 32)
    hth = SBA("hth", [128, 16 * 256], BF16, R2 + 48)
    n2s = [SBA(f"n2s{i}", [128, 2048], BF16, R2 + 56 + 4 * i) for i in range(2)]
    wqc2 = [SBA(f"wqc{i}", [128, 1024], F32, R3 + 4 * i) for i in range(2)]
    wqo = SBA("wqo", [128, 2048], BF16, R3 + 8)
    x0o = [SBA(f"x0o{i}", [128, D], F32, R1 + 4 * i) for i in range(2)]
    x1t = [SBA(f"x1t{i}", [128, D], F32, R1 + 8 + 4 * i) for i in range(2)]
    zt = [SBA(f"zt{i}", [128, D], F32, R1 + 16 + 4 * i) for i in range(2)]
    z2qs = [SBA(f"z2qs{i}", [128, D], BF16, R1 + 24 + 2 * i) for i in range(2)]
    z2qT = SBA("z2qT", [128, 8 * OWN], BF16, R2)
    wgs = [SBA(f"wgs{i}", [128, 8 * 512], BF16, R2 + 32 + 8 * i) for i in range(2)]
    wus = [SBA(f"wus{i}", [128, 8 * 512], BF16, R2 + 48 + 8 * i) for i in range(2)]
    gsb = SBA("gsb", [128, 512], F32, R3)
    psb = [SBA(f"psb{i}", [128, 512], F32, R3 + 2 + 2 * i) for i in range(2)]
    ptile2 = [SBA(f"ptile{i}", [128, DI], BF16, R3 + 6 + 8 * i)
              for i in range(2)]
    pwide = SBA("pwide", [128, DI], F32, R1)
    pqtile2 = [SBA(f"pqtile{i}", [128, DI], BF16, R1 + 16 + 8 * i)
               for i in range(2)]
    fsb = SBA("fsb", [128, 16 * D], F32, R1)
    pqt = SBA("pqt", [128, 8 * 2048], BF16, R2)
    wds = SBA("wds", [128, 8 * D], BF16, R2 + 32)
    x1r = [SBA(f"x1r{i}", [128, D], BF16, R2 + 2 * i) for i in range(4)]
    otl = [SBA(f"otl{i}", [128, D], F32, R2 + 8 + 4 * i) for i in range(4)]

    psA = nc.alloc_psum_tensor("psA", [128, 2048], F32)
    psC = nc.alloc_psum_tensor("psC", [128, 1024], F32)
    psD = nc.alloc_psum_tensor("psD", [128, 1024], F32)

    semnames = ["qA0", "qA1", "qB0", "qB1", "qC0", "qC1", "qD0", "qD1",
                "qE0", "qE1", "qF0", "qF1", "qW0", "qW1", "qX0", "qX1",
                "qY0", "qY1", "qZ0", "qZ1", "qV0", "qV1", "qP0", "cV", "cS", "cT", "cG"]
    sems, cnt = {}, {}
    for n in semnames:
        sems[n] = nc.semaphore(n).__enter__()
        cnt[n] = 0

    _cvals = [1e-5, 1e-6, 1.0 / 1024.0, 0.125]
    for _j, _v in enumerate(_cvals):
        nc.const_aps.register(F32, _v, cst[:, _j:_j + 1])

    blk = nc.Block()
    block = blk.__enter__()

    def inc(instr, name, v=None):
        if v is None:
            v = 16 if name.startswith("q") else 1
        cnt[name] += v
        instr.then_inc(sems[name], v)
        return cnt[name]

    def tok(name):
        return cnt[name]


    class _RecInstr:
        __slots__ = ("incs",)
        def __init__(self):
            self.incs = []
        def then_inc(self, sem, v):
            self.incs.append((sem, v))
            return self

    class _RecEngine:
        def __init__(self, name):
            self._name = name
            self._calls = []
        def __getattr__(self, m):
            def call(*a, **k):
                h = _RecInstr()
                self._calls.append((m, a, k, h))
                return h
            return call

    _PROX = {n: _RecEngine(n) for n in
             ("sync", "vector", "scalar", "tensor", "gpsimd")}

    def _replay(name):
        calls = _PROX[name]._calls
        _PROX[name]._calls = []
        def body(e):
            for m, a, k, h in calls:
                alp = k.get("_alp")
                if alp is not None:
                    k = {x: v for x, v in k.items() if x != "_alp"}
                    with nc.allow_low_precision(alp):
                        i = getattr(e, m)(*a, **k)
                else:
                    i = getattr(e, m)(*a, **k)
                for sem, v in h.incs:
                    i.then_inc(sem, v)
        return body

    def flush():
        for n, meth in (("sync", block.sync), ("tensor", block.tensor),
                        ("scalar", block.scalar), ("vector", block.vector),
                        ("gpsimd", block.gpsimd)):
            if _PROX[n]._calls:
                meth(_replay(n))

    def on_sync(f):
        f(_PROX["sync"])

    def on_vec(f):
        e = _PROX["vector"]
        e.wait_ge(sems["cV"], cnt["cV"])
        f(e)

    def on_scal(f):
        e = _PROX["scalar"]
        e.wait_ge(sems["cS"], cnt["cS"])
        f(e)

    def on_ten(f):
        f(_PROX["tensor"])

    def on_gp(f):
        f(_PROX["gpsimd"])

    def W(e, name, v):
        if v > 0:
            e.wait_ge(sems[name], v)

    def own_blend(e, dst_v, ev_v, od_v):
        e.tensor_scalar(dst_v, od_v, hcol[:, 0:1], None, Alu.mult)
        e.drain()
        e.scalar_tensor_tensor(dst_v, ev_v, hcol[:, 1:2], dst_v,
                               Alu.mult, Alu.add)

    def f(e):
        W(e, "qP0", cnt["qP0"])
        inc(e.dma_start(mku[:], maskUd[:, :]), "qP0")
        W(e, "qP0", cnt["qP0"])
        inc(e.dma_start(hcol[:], hcold[:, :]), "qP0")
        W(e, "qP0", cnt["qP0"])
        inc(e.dma_start(beta[:], bass.AP(ddv, 0, [[1, 128], [128, 8]]),
                        allow_slow_non_contiguous=True), "qP0")
        W(e, "qP0", cnt["qP0"])
        inc(e.dma_start(ln1c[:], bass.AP(ln1v, 0, [[1, 128], [128, 8]]),
                        allow_slow_non_contiguous=True), "qP0")
        W(e, "qP0", cnt["qP0"])
        inc(e.dma_start(ln2c[:], bass.AP(ln2v, 0, [[1, 128], [128, 8]]),
                        allow_slow_non_contiguous=True), "qP0")
        W(e, "qP0", cnt["qP0"])
        inc(e.dma_start(b1c[:], bass.AP(b1v, 0, [[1, 128], [128, 32]]),
                        allow_slow_non_contiguous=True), "qP0")
        W(e, "qP0", cnt["qP0"])
        inc(e.dma_start(ipbcT[:], bass.AP(ipb, 0, [[1, 128], [128, 24]]),
                        allow_slow_non_contiguous=True), "qP0")
        W(e, "qP0", cnt["qP0"])
        inc(e.dma_start(obc[:], bass.AP(obr, 0, [[1, 128], [128, 8]]),
                        allow_slow_non_contiguous=True), "qP0")
        W(e, "qP0", cnt["qP0"])
        inc(e.dma_start(sk1[:], phid[:, :]), "qP0")
        W(e, "qP0", cnt["qP0"])
        inc(e.dma_start(id8s[:], id8d[:, :]), "qP0")
        W(e, "qP0", cnt["qP0"])
        inc(e.dma_start(gb8s[:], gb8d[:, :]), "qP0")
        W(e, "qP0", cnt["qP0"])
        inc(e.dma_start(tr1[:], obr[:, :]), "qP0")
        W(e, "qP0", cnt["qP0"])
        inc(e.dma_start(tr2[:], b2r[:, :]), "qP0")
        W(e, "qP0", cnt["qP0"])
        inc(e.dma_start(gw8f[:],
                        bass.AP(gwT8, 0, [[8, 128], [1024, 8], [1, 8]]),
                        allow_slow_non_contiguous=True), "qP0")
        W(e, "qP0", cnt["qP0"])
        inc(e.dma_start(id128[:], id128d[:, :]), "qP0")
    on_sync(f)
    t_p0 = tok("qP0")

    def f(e):
        W(e, "qW0", cnt["qW0"])
        i = e.dma_start(ipbr_b[:], ipb[:, :])
        inc(i, "qW0")
    on_gp(f)
    t_ipbr = tok("qW0")

    def f(e):
        e.memset(ones_b[:], 1.0)
        e.memset(ones_f[:], 1.0)
        e.memset(cst[:, 0:1], 1e-5)
        e.memset(cst[:, 1:2], 1e-6)
        e.memset(cst[:, 2:3], 1.0 / 1024.0)
        e.memset(cst[:, 3:4], 0.125)
        e.memset(pmxall[:], 0.0)
        W(e, "qP0", t_p0)
        e.tensor_scalar(bq8[:], ipbcT[:, 0:8], 0.125, None, Alu.mult)
        e.tensor_copy(gw8b[:], gw8f[:])
        e.tensor_add(tr1[:], tr1[:], tr2[:])
        e.drain()
        i = e.tensor_copy(obb2_b[:], tr1[:])
        inc(i, "cV")
    on_vec(f)
    t_p0v = tok("cV")

    def f(e):
        W(e, "cV", t_p0v)
        e.activation(beta[:], beta[:], Act.Sigmoid)
        i = e.activation(sk1[:], sk1[:], Act.Exp)
        inc(i, "cS")
    on_scal(f)
    t_p0s = tok("cS")

    def f(e):
        W(e, "cS", t_p0s)
        i = e.tensor_scalar(betm1[:], beta[:], -1.0, None, Alu.add)
        inc(i, "cV")
    on_vec(f)

    cur = sk1
    for itn in range(31):
        def f(e, cur=cur, itn=itn):
            if itn < 30:
                e.reduce_sum(skr[:], cur[:], axis=AX)
                e.drain()
                e.reciprocal(skr[:], skr[:])
                e.drain()
                i = e.tensor_scalar(cur[:], cur[:], skr[:, 0:1], None,
                                    Alu.mult)
                inc(i, "cV")
            else:
                i = e.engine_nop()
                inc(i, "cV")
        on_vec(f)
        tv = tok("cV")

        def f(e, cur=cur, tv=tv):
            W(e, "cV", tv)
            i = e.transpose(psD[0:8, 0:8], cur[:], id8s[:])
            inc(i, "cT")
        on_ten(f)
        tt = tok("cT")
        nxt = sk2 if cur is sk1 else sk1

        def f(e, nxt=nxt, tt=tt):
            W(e, "cT", tt)
            i = e.tensor_copy(nxt[:], psD[0:8, 0:8])
            inc(i, "cV")
        on_vec(f)
        cur = nxt
    skHT = cur

    flush()

    t_z1done, t_z1wb = {}, {}
    for t in range(32):
        sl = t % 4
        qx = "qA0" if t % 2 == 0 else "qA1"
        qz = "qB0" if t % 2 == 0 else "qB1"

        def f(e, t=t, sl=sl, qx=qx):
            if t >= 4:
                W(e, "cV", t_z1done[t - 4])
            W(e, qx, cnt[qx])
            i = e.dma_start(x0s[sl][:], x0[128 * t:128 * (t + 1), :])
            inc(i, qx)
        on_sync(f)
        t_x0 = tok(qx)

        def f(e, sl=sl, t_x0=t_x0, qx=qx):
            W(e, qx, t_x0)
            i = e.activation(sqs[sl][:], x0s[sl][:], Act.Square,
                             accum_out=stq[:, sl:sl + 1])
            inc(i, "cS")
        on_scal(f)
        t_sq = tok("cS")

        def f(e, sl=sl, t_x0=t_x0, t_sq=t_sq, qx=qx):
            W(e, qx, t_x0)
            e.reduce_sum(stm[:, sl:sl + 1], x0s[sl][:], axis=AX)
            e.drain()
            e.tensor_scalar(stm[:, sl:sl + 1], stm[:, sl:sl + 1], 1.0 / D,
                            None, Alu.mult)
            e.drain()
            e.tensor_mul(stq2[:, sl:sl + 1], stm[:, sl:sl + 1],
                         stm[:, sl:sl + 1])
            e.drain()
            W(e, "cS", t_sq)
            i = e.scalar_tensor_tensor(stv[:, sl:sl + 1], stq[:, sl:sl + 1],
                                       1.0 / D, stq2[:, sl:sl + 1],
                                       Alu.mult, Alu.subtract)
            inc(i, "cV")
        on_vec(f)
        t_var = tok("cV")

        def f(e, sl=sl, t_var=t_var):
            W(e, "cV", t_var)
            i = e.activation(stv[:, sl:sl + 1], stv[:, sl:sl + 1], Act.Sqrt,
                             bias=1e-5)
            inc(i, "cS")
        on_scal(f)
        t_r = tok("cS")

        def f(e, t=t, sl=sl, t_r=t_r, qz=qz):
            W(e, "cS", t_r)
            if t >= 4:
                W(e, qz, t_z1wb[t - 4])
            e.reciprocal(stv[:, sl:sl + 1], stv[:, sl:sl + 1])
            e.drain()
            i = e.tensor_scalar(z1s[sl][:], x0s[sl][:], stm[:, sl:sl + 1],
                                stv[:, sl:sl + 1], Alu.subtract, Alu.mult)
            inc(i, "cV")
        on_vec(f)
        t_z1done[t] = tok("cV")

        def f(e, t=t, sl=sl, qz=qz, tz=t_z1done[t]):
            W(e, "cV", tz)
            W(e, qz, cnt[qz])
            i = e.dma_start(z1d[128 * t:128 * (t + 1), :], z1s[sl][:])
            inc(i, qz)
        on_gp(f)
        t_z1wb[t] = tok(qz)

    t_A_qA0, t_A_qA1 = tok("qA0"), tok("qA1")
    t_A_cS = tok("cS")
    t_A_qB0, t_A_qB1 = tok("qB0"), tok("qB1")

    flush()

    t_scan, t_sqd, t_xowb, t_sqmm_last = {}, {}, {}, {}
    for c in range(8):
        def f(e, c=c):
            if c >= 2:
                W(e, "cV", t_scan[c - 2])
            if c == 0:
                W(e, "qB0", t_z1wb[30])
                W(e, "qB1", t_z1wb[31])
            W(e, "qC0", cnt["qC0"])
            i = e.dma_start(z1T2[c % 2][:], z1d[:, 128 * c:128 * (c + 1)],
                            transpose=True)
            inc(i, "qC0")
        on_sync(f)
        t_z1T = tok("qC0")

        def f(e, c=c, t_z1T=t_z1T):
            W(e, "qC0", t_z1T)
            z1T = z1T2[c % 2]
            if c == 0:
                e.memset(sprime[:, 0:1], 0.0)
            else:
                W(e, "cS", t_sqd[c - 1])
            bcast = bass.AP(beta, c, [[8, 128], [0, 2048]])
            e.tensor_tensor_scan(sprime[:, 1:2049], bcast, z1T[:, 0:2048],
                                 0.0, Alu.mult, Alu.add)
            e.drain()
            e.scalar_tensor_tensor(xdcs[:, 0:2048], sprime[:, 0:2048],
                                   betm1[:, c:c + 1], z1T[:, 0:2048],
                                   Alu.mult, Alu.add)
            e.drain()
            e.tensor_copy(sprime[:, 0:1], sprime[:, 2048:2049])
            e.drain()
            e.tensor_tensor_scan(sprime[:, 1:2049], bcast, z1T[:, 2048:4096],
                                 sprime[:, 0:1], Alu.mult, Alu.add)
            e.drain()
            e.scalar_tensor_tensor(xdcs[:, 2048:4096], sprime[:, 0:2048],
                                   betm1[:, c:c + 1], z1T[:, 2048:4096],
                                   Alu.mult, Alu.add)
            e.drain()
            e.memset(sprime[:, 0:1], 0.0)
            e.drain()
            e.reduce_sum(xs[:, c:c + 1], xdcs[:], axis=AX)
            e.tensor_copy(xdb[:, S * c:S * (c + 1)], xdcs[:])
            if c >= 1:
                W(e, "qB0", t_xowb[c - 1])
            srcv = xdcs[:, :].rearrange("p (u t s) -> p u t s", t=2, s=64)
            dstv = xoc[:, :].rearrange("p (u s) -> p u s", s=64)
            own_blend(e, dstv, srcv[:, :, 0, :], srcv[:, :, 1, :])
            i = e.engine_nop()
            inc(i, "cV")
        on_vec(f)
        t_scan[c] = tok("cV")

        def f(e, c=c, tsc=t_scan[c]):
            W(e, "cV", tsc)
            if c >= 1:
                W(e, "cT", t_sqmm_last[c - 1])
            i = e.activation(sqc[:], xdcs[:], Act.Square)
            inc(i, "cS")
        on_scal(f)
        t_sqd[c] = tok("cS")

        def f(e, c=c, tsc=t_scan[c]):
            W(e, "cV", tsc)
            W(e, "qB0", cnt["qB0"])
            i = e.dma_start(xod[128 * c:128 * (c + 1), :], xoc[:])
            inc(i, "qB0")
        on_gp(f)
        t_xowb[c] = tok("qB0")

        for g in range(8):
            if g < 4:
                tgt = lambda g=g: psA[0:1, 512 * g:512 * (g + 1)]
            elif g < 6:
                tgt = lambda g=g: psC[0:1, 512 * (g - 4):512 * (g - 3)]
            else:
                tgt = lambda g=g: psD[0:1, 512 * (g - 6):512 * (g - 5)]

            def f(e, c=c, g=g, tsq=t_sqd[c], tgt=tgt):
                if g == 0:
                    W(e, "cS", tsq)
                i = e.matmul(tgt(), ones_b[:, 0:1],
                             sqc[:, 512 * g:512 * (g + 1)],
                             start=(c == 0), stop=(c == 7))
                inc(i, "cT")
            on_ten(f)
        t_sqmm_last[c] = tok("cT")

    def f(e, tm=t_sqmm_last[7]):
        W(e, "cT", tm)
        e.activation(r1row[0:1, 0:2048], psA[0:1, 0:2048], Act.Sqrt,
                     bias=1e-6, scale=1.0 / D)
        e.activation(r1row[0:1, 2048:3072], psC[0:1, 0:1024], Act.Sqrt,
                     bias=1e-6, scale=1.0 / D)
        i = e.activation(r1row[0:1, 3072:4096], psD[0:1, 0:1024], Act.Sqrt,
                         bias=1e-6, scale=1.0 / D)
        inc(i, "cS")
    on_scal(f)
    t_r1s = tok("cS")

    def f(e, t_r1s=t_r1s):
        W(e, "cS", t_r1s)
        i = e.reciprocal(r1row[:], r1row[:])
        inc(i, "cV")
    on_vec(f)
    t_r1 = tok("cV")

    t_r1c = {}
    for g in range(8):
        def f(e, g=g, t_r1=t_r1):
            if g == 0:
                W(e, "cV", t_r1)
            if g >= 4:
                W(e, "cV", t_r1c[g - 4])
            i = e.matmul(psA[:, 512 * (g % 4):512 * (g % 4) + 512],
                         ones_f[0:1, :], r1row[0:1, 512 * g:512 * (g + 1)],
                         start=True, stop=True)
            inc(i, "cT")
        on_ten(f)
        t_pe = tok("cT")

        def f(e, g=g, t_pe=t_pe):
            W(e, "cT", t_pe)
            if g == 0:
                W(e, "qA0", t_A_qA0)
                W(e, "qA1", t_A_qA1)
                W(e, "cS", t_A_cS)
                W(e, "qB0", t_A_qB0)
                W(e, "qB1", t_A_qB1)
            i = e.tensor_copy(r1b[:, 512 * g:512 * (g + 1)],
                              psA[:, 512 * (g % 4):512 * (g % 4) + 512])
            inc(i, "cV")
        on_vec(f)
        t_r1c[g] = tok("cV")
    t_r1b = tok("cV")

    def f(e):
        i = e.tensor_scalar(xsb[:], xs[:], 1.0 / S, None, Alu.mult)
        inc(i, "cV")
    on_vec(f)
    t_xsb = tok("cV")

    def f(e, t_xsb=t_xsb):
        W(e, "cV", t_xsb)
        W(e, "cS", t_r1s)
        for kc in range(8):
            i = e.matmul(psD[0:8, 0:1], gw8b[:, 8 * kc:8 * (kc + 1)],
                         xsb[:, kc:kc + 1], start=(kc == 0), stop=(kc == 7))
        inc(i, "cT")
    on_ten(f)
    t_gmm = tok("cT")

    def f(e, t_gmm=t_gmm):
        W(e, "cT", t_gmm)
        i = e.activation(g8[:], psD[0:8, 0:1], Act.Sigmoid,
                         bias=gb8s[:, 0:1])
        inc(i, "cS")
    on_scal(f)
    t_g8 = tok("cS")

    def f(e, t_g8=t_g8):
        W(e, "cS", t_g8)
        e.tensor_scalar(skHT[:], skHT[:], g8[:, 0:1], None, Alu.mult)
        e.drain()
        i = e.tensor_add(skHT[:], skHT[:], id8s[:])
        inc(i, "cV")
    on_vec(f)
    t_mpt = tok("cV")

    def f(e, t_mpt=t_mpt):
        W(e, "cV", t_mpt)
        W(e, "qB1", cnt["qB1"])
        i = e.dma_start(m64d[:, :], skHT[:])
        inc(i, "qB1")
    on_gp(f)
    t_m64w = tok("qB1")

    def f(e, t_m64w=t_m64w):
        W(e, "qB1", t_m64w)
        W(e, "qC1", cnt["qC1"])
        i = e.dma_start(mpr[:], bass.AP(m64d, 0, [[64, 1], [1, 64]]))
        inc(i, "qC1")
    on_sync(f)
    t_mprl = tok("qC1")

    def f(e, t_mprl=t_mprl):
        W(e, "qC1", t_mprl)
        i = e.matmul(psD[:, 0:64], ones_f[0:1, :], mpr[0:1, :],
                     start=True, stop=True)
        inc(i, "cT")
    on_ten(f)
    t_mpbT = tok("cT")

    def f(e, t_mpbT=t_mpbT):
        W(e, "cT", t_mpbT)
        i = e.tensor_copy(mpb[:], psD[:, 0:64])
        inc(i, "cV")
    on_vec(f)
    t_mpb = tok("cV")

    flush()

    for c in range(8):
        def f(e, c=c):
            e.scalar_tensor_tensor(xdb[:, S * c:S * (c + 1)],
                                   xdb[:, S * c:S * (c + 1)],
                                   ln1c[:, c:c + 1], r1b[:],
                                   Alu.mult, Alu.mult)
            e.drain()
            srcv = xdb[:, S * c:S * (c + 1)].rearrange(
                "p (u t s) -> p u t s", t=2, s=64)
            dstv = n1o[:, OWN * c:OWN * (c + 1)].rearrange(
                "p (u s) -> p u s", s=64)
            own_blend(e, dstv, srcv[:, :, 0, :], srcv[:, :, 1, :])
            i = e.engine_nop()
            inc(i, "cV")
        on_vec(f)
    t_n1all = tok("cV")

    flush()

    def f(e):
        W(e, "cV", t_n1all)
        for kc in range(8):
            W(e, "qW0", cnt["qW0"])
            inc(e.dma_start(wst[:, D * kc:D * (kc + 1)],
                            ipT[128 * kc:128 * (kc + 1), 0:D]), "qW0")
    on_gp(f)
    t_wq = tok("qW0")

    t_qkev, t_qkwb = {}, {}
    nq = 0
    for oc in range(8):
        for tg in range(4):
            bk = nq % 4

            def f(e, oc=oc, tg=tg, bk=bk, nq=nq):
                if nq == 0:
                    W(e, "qW0", t_wq)
                    W(e, "cV", t_n1all)
                if nq >= 4:
                    W(e, "cS", t_qkev[nq - 4])
                for kc in range(8):
                    i = e.matmul(
                        psA[:, 512 * bk:512 * (bk + 1)],
                        wst[:, D * kc + 128 * oc:D * kc + 128 * (oc + 1)],
                        n1o[:, OWN * kc + 512 * tg:OWN * kc + 512 * (tg + 1)],
                        start=(kc == 0), stop=(kc == 7))
                inc(i, "cT")
            on_ten(f)
            t_mm = tok("cT")

            def f(e, oc=oc, bk=bk, nq=nq, t_mm=t_mm):
                W(e, "cT", t_mm)
                if nq >= 2:
                    W(e, "qC0" if nq % 2 == 0 else "qC1", t_qkwb[nq - 2])
                i = e.activation(qkev[nq % 2][:],
                                 psA[:, 512 * bk:512 * (bk + 1)],
                                 Act.Identity, bias=bq8[:, oc:oc + 1],
                                 scale=0.125)
                inc(i, "cS")
            on_scal(f)
            t_qkev[nq] = tok("cS")

            def f(e, oc=oc, tg=tg, nq=nq, tev=t_qkev[nq]):
                W(e, "cS", tev)
                W(e, "qC0", cnt["qC0"])
                _s = "qC0" if nq % 2 == 0 else "qC1"
                W(e, _s, cnt[_s])
                i = e.dma_start(qd[128 * oc:128 * (oc + 1),
                                   512 * tg:512 * (tg + 1)], qkev[nq % 2][:])
                inc(i, "qC0" if nq % 2 == 0 else "qC1")
            on_sync(f)
            t_qkwb[nq] = tok("qC0" if nq % 2 == 0 else "qC1")
            nq += 1

    def f(e, tprev=tok("cT")):
        W(e, "cT", tprev)
        for kc in range(8):
            W(e, "qW1", cnt["qW1"])
            inc(e.dma_start(wst[:, D * kc:D * (kc + 1)],
                            ipT[128 * kc:128 * (kc + 1), D:2 * D]), "qW1")
    on_gp(f)
    t_wk = tok("qW1")

    for oc in range(8):
        for tg in range(8):
            bk = nq % 4

            def f(e, oc=oc, tg=tg, bk=bk, nq=nq):
                if oc == 0 and tg == 0:
                    W(e, "qW1", t_wk)
                W(e, "cS", t_qkev[nq - 4])
                for kc in range(8):
                    i = e.matmul(
                        psA[:, 512 * bk:512 * (bk + 1)],
                        wst[:, D * kc + 128 * oc:D * kc + 128 * (oc + 1)],
                        xdb[:, S * kc + 512 * tg:S * kc + 512 * (tg + 1)],
                        start=(kc == 0), stop=(kc == 7))
                inc(i, "cT")
            on_ten(f)
            t_mm = tok("cT")

            def f(e, oc=oc, bk=bk, nq=nq, t_mm=t_mm):
                W(e, "cT", t_mm)
                W(e, "qC0" if nq % 2 == 0 else "qC1", t_qkwb[nq - 2])
                i = e.activation(qkev[nq % 2][:],
                                 psA[:, 512 * bk:512 * (bk + 1)],
                                 Act.Identity, bias=ipbcT[:, 8 + oc:9 + oc])
                inc(i, "cS")
            on_scal(f)
            t_qkev[nq] = tok("cS")

            def f(e, oc=oc, tg=tg, nq=nq, tev=t_qkev[nq]):
                W(e, "cS", tev)
                W(e, "qC0", cnt["qC0"])
                _s = "qC0" if nq % 2 == 0 else "qC1"
                W(e, _s, cnt[_s])
                i = e.dma_start(kd[128 * oc:128 * (oc + 1),
                                   512 * tg:512 * (tg + 1)], qkev[nq % 2][:])
                inc(i, "qC0" if nq % 2 == 0 else "qC1")
            on_sync(f)
            t_qkwb[nq] = tok("qC0" if nq % 2 == 0 else "qC1")
            nq += 1
    t_kwb0, t_kwb1 = tok("qC0"), tok("qC1")

    def f(e, tprev=tok("cT"), tsc=tok("cS")):
        W(e, "cT", tprev)
        W(e, "cS", tsc)
        W(e, "qB0", t_xowb[7])
        i = e.memset(vbuf[:, 64::65], 1.0)
        inc(i, "cV")
    on_vec(f)
    t_vones = tok("cV")

    def f(e, tprev=tok("cT")):
        W(e, "cT", tprev)
        for kc in range(8):
            W(e, "qW0", cnt["qW0"])
            inc(e.dma_start(wst[:, D * kc:D * (kc + 1)],
                            ipT[128 * kc:128 * (kc + 1), 2 * D:3 * D]),
                "qW0")
    on_gp(f)
    t_wv = tok("qW0")

    t_vev = {}
    nv = 0
    for tt in range(32):
        for vg in range(2):
            bk = nv % 4

            def f(e, tt=tt, vg=vg, bk=bk, nv=nv):
                if nv == 0:
                    W(e, "qW0", t_wv)
                    W(e, "cS", t_qkev[nq - 1])
                if nv >= 4:
                    W(e, "cV", t_vev[nv - 4])
                for kc in range(8):
                    e.matmul(
                        psA[:, 512 * bk:512 * (bk + 1)],
                        xdb[:, S * kc + 128 * tt:S * kc + 128 * (tt + 1)],
                        wst[:, D * kc + 512 * vg:D * kc + 512 * (vg + 1)],
                        start=(kc == 0), stop=False)
                i = e.matmul(
                    psA[:, 512 * bk:512 * (bk + 1)], ones_b[0:1, :],
                    ipbr_b[0:1, 2 * D + 512 * vg:2 * D + 512 * (vg + 1)],
                    start=False, stop=True)
                inc(i, "cT")
            on_ten(f)
            t_mm = tok("cT")

            def f(e, tt=tt, vg=vg, bk=bk, t_mm=t_mm, nv=nv):
                W(e, "cT", t_mm)
                if nv == 0:
                    W(e, "qB0", t_xowb[7])
                dst = vbuf[:, 1040 * tt + 520 * vg:1040 * tt + 520 * (vg + 1)]
                dst = dst.rearrange("p (hh c) -> p hh c", c=65)[:, :, 0:64]
                i = e.tensor_copy(dst, psA[:, 512 * bk:512 * (bk + 1)])
                inc(i, "cV")
            on_vec(f)
            t_vev[nv] = tok("cV")
            nv += 1
    t_vall = tok("cV")

    t_vmm_last = tok("cT")

    flush()

    ws_jobs = [(wi, wn, sub) for wi, wn in enumerate(["g", "u", "d"])
               for sub in range(16)]
    nws_box = [0]
    t_wsacc = {}

    def emit_ws_job():
        if nws_box[0] >= len(ws_jobs):
            return
        nws = nws_box[0]
        wi, wn, sub = ws_jobs[nws]

        def f(e, wn=wn, sub=sub, nws=nws):
            if nws == 0:
                W(e, "qC0", t_kwb0)
                W(e, "qC1", t_kwb1)
                W(e, "cS", t_qkev[95])
                W(e, "cV", t_vall)
            if nws >= 1:
                W(e, "cV", t_wsacc[nws - 1])
            if wn != "d":
                ch, hf = sub // 2, sub % 2
                W(e, "qF0", cnt["qF0"])
                inc(e.dma_start(wkc[:],
                                wfd[wn][128 * ch:128 * (ch + 1),
                                        2048 * hf:2048 * (hf + 1)]), "qF0")
            else:
                W(e, "qF0", cnt["qF0"])
                inc(e.dma_start(wkc[:, 0:1024],
                                wfd[wn][256 * sub:256 * sub + 128, :]),
                    "qF0")
                W(e, "qF0", cnt["qF0"])
                inc(e.dma_start(wkc[:, 1024:2048],
                                wfd[wn][256 * sub + 128:256 * (sub + 1), :]),
                    "qF0")
        on_sync(f)
        t_ld = tok("qF0")

        def f(e, wi=wi, sub=sub, t_ld=t_ld):
            W(e, "qF0", t_ld)
            col = 16 * wi + sub
            i = e.tensor_reduce(wstat[:, col:col + 1], wkc[:], AX,
                                Alu.add, apply_absolute_value=True)
            inc(i, "cV")
        on_vec(f)
        t_wsacc[nws] = tok("cV")
        nws_box[0] += 1

    owb_box = [0]
    t_owb = [0]

    def emit_owb():
        if owb_box[0]:
            return
        owb_box[0] = 1

        def f(e):
            W(e, "cV", t_wsacc[len(ws_jobs) - 1])
            for kc in range(8):
                W(e, "qW1", cnt["qW1"])
                inc(e.dma_start(owb[:, D * kc:D * (kc + 1)],
                                owT[128 * kc:128 * (kc + 1), :]), "qW1")
        on_gp(f)
        t_owb[0] = tok("qW1")

    t_khld, t_qld = {}, {}
    t_avL, t_expL, t_on, t_odma, t_G = {}, {}, {}, {}, {}
    s_tok = []
    exp_tok = []
    av_tok = []
    m_tok = {}
    it = 0
    kg = 0
    for hd_ in range(NH):
        khsl = hd_ % 2

        def f(e, hd_=hd_, khsl=khsl):
            if hd_ == 0:
                W(e, "qC0", t_kwb0)
                W(e, "qC1", t_kwb1)
                W(e, "cT", t_vmm_last)
            if hd_ >= 2:
                pit = 4 * (hd_ - 2) + 3
                W(e, "cT", t_avL[pit])
            W(e, "qD0", cnt["qD0"])
            i = e.dma_start(kh[khsl][:, 0:S],
                            kd[64 * hd_:64 * (hd_ + 1), 0:S])
            inc(i, "qD0")
        on_sync(f)
        t_khld[hd_] = tok("qD0")

        for i4 in range(4):
            nkb = 8 * i4 + 8
            G = nkb // 2
            sl = it % 2
            t_G[it] = G

            def f(e, i4=i4, hd_=hd_, sl=sl, it=it):
                if it == 0:
                    W(e, "qC0", t_kwb0)
                    W(e, "qC1", t_kwb1)
                    W(e, "cT", t_vmm_last)
                if it >= 2:
                    W(e, "cT", t_avL[it - 2])
                W(e, "qD1", cnt["qD1"])
                i = e.dma_start(qh[sl][:], qd[64 * hd_:64 * (hd_ + 1),
                                              512 * i4:512 * (i4 + 1)])
                inc(i, "qD1")
            on_sync(f)
            t_qld[it] = tok("qD1")

            for b0 in range(0, nkb, 8):
                kg_b = kg
                for kb in range(b0, b0 + 8):
                    sA = kg % 4
                    cs = 64 * max(0, kb - 8 * i4)

                    def f(e, i4=i4, hd_=hd_, it=it, kb=kb, sA=sA, cs=cs,
                          kg=kg, khsl=khsl):
                        if kb == 0:
                            W(e, "qD1", t_qld[it])
                            if i4 == 0:
                                W(e, "qD0", t_khld[hd_])
                        if kg >= 4 and kg % 2 == 0:
                            W(e, "cS", exp_tok[kg - 4])
                        elif kg == 0:
                            W(e, "cV", t_vall)
                        i = e.matmul(
                            psA[:, 512 * sA + cs:512 * (sA + 1)],
                            kh[khsl][:, 128 * kb:128 * (kb + 1)],
                            qh[it % 2][:, cs:512],
                            start=True, stop=True)
                        inc(i, "cT")
                    on_ten(f)
                    s_tok.append(tok("cT"))

                    if kg % 2 == 1:
                        def f(e, kg=kg):
                            W(e, "cT", s_tok[kg])
                            if kg >= 8:
                                W(e, "cT", av_tok[kg - 8])
                            sP = (kg - 1) % 8
                            sAp = (kg - 1) % 4
                            i = e.activation(
                                pr[:, 512 * sP:512 * sP + 1024],
                                psA[:, 512 * sAp:512 * sAp + 1024],
                                Act.Exp)
                            inc(i, "cS")
                        on_scal(f)
                        exp_tok.append(tok("cS"))
                        exp_tok.append(exp_tok[-1])

                        for dkb in (kb - 1, kb):
                            if dkb >= 8 * i4:
                                kgd = kg - (kb - dkb)
                                rc = 64 * (dkb - 8 * i4)

                                def f(e, kgd=kgd, rc=rc, kg=kg):
                                    W(e, "cS", exp_tok[kg])
                                    base = 512 * (kgd % 8) + rc
                                    i = e.tensor_mul(pr[:, base:base + 64],
                                                     pr[:, base:base + 64],
                                                     mku[:])
                                    inc(i, "cV")
                                on_vec(f)
                                m_tok[kgd] = tok("cV")
                    kg += 1

                for kb in range(b0, b0 + 8):
                    kgv = kg_b + (kb - b0)
                    cs = 64 * max(0, kb - 8 * i4)

                    def f(e, it=it, hd_=hd_, kb=kb, kgv=kgv, cs=cs,
                          nkb=nkb):
                        if kb % 2 == 0:
                            W(e, "cS", exp_tok[kgv])
                        if kgv in m_tok:
                            W(e, "cV", m_tok[kgv])
                        pse = psC if it % 2 == 0 else psD
                        if kb == 0 and it >= 2:
                            W(e, "cV", t_on[it - 2])
                        sP = kgv % 8
                        i = e.matmul(pse[0:65, cs:512],
                                     vbuf[:, 1040 * kb + 65 * hd_:
                                          1040 * kb + 65 * hd_ + 65],
                                     pr[:, 512 * sP + cs:512 * (sP + 1)],
                                     start=(kb == 0), stop=(kb == nkb - 1))
                        inc(i, "cT")
                    on_ten(f)
                    av_tok.append(tok("cT"))
            t_avL[it] = av_tok[-1]
            t_expL[it] = exp_tok[-1]

            def f(e, it=it, ta=t_avL[it]):
                W(e, "cT", ta)
                pse = psC if it % 2 == 0 else psD
                i = e.reciprocal(rlsb[64:65, :], pse[64:65, 0:512],
                                 _alp="bf16 rl operand for PE broadcast")
                inc(i, "cV")
            on_vec(f)
            t_rl = tok("cV")

            def f(e, t_rl=t_rl, it=it):
                W(e, "cV", t_rl)
                pse = psC if it % 2 == 0 else psD
                i = e.matmul(pse[:, 512:1024], ones_b[64:65, :],
                             rlsb[64:65, 0:512], start=True, stop=True)
                inc(i, "cT")
            on_ten(f)
            t_rlb = tok("cT")

            def f(e, it=it, sl=sl, t_rlb=t_rlb):
                W(e, "cT", t_rlb)
                pse = psC if it % 2 == 0 else psD
                if it >= 2:
                    W(e, "qE0" if sl == 0 else "qE1", t_odma[it - 2])
                e.tensor_copy(rlbs[:], pse[:, 512:1024])
                e.drain()
                i = e.tensor_mul(onsb[sl][:], pse[0:64, 0:512],
                                 rlbs[0:64, :])
                inc(i, "cV")
            on_vec(f)
            t_on[it] = tok("cV")

            def f(e, it=it, sl=sl, i4=i4, hd_=hd_, ton=t_on[it]):
                W(e, "cV", ton)
                pb = 64 * (hd_ % 2)
                cb = 2048 * (hd_ // 2) + 512 * i4
                qs = "qE0" if sl == 0 else "qE1"
                W(e, qs, cnt[qs])
                i = e.dma_start(obuf[pb:pb + 64, cb:cb + 512], onsb[sl][:])
                inc(i, qs)
            on_gp(f)
            t_odma[it] = tok("qE0" if sl == 0 else "qE1")
            emit_ws_job()
            if nws_box[0] >= len(ws_jobs):
                emit_owb()
            it += 1
    t_attn_d0, t_attn_d1 = tok("qE0"), tok("qE1")
    emit_owb()
    t_C2_cS = tok("cS")
    t_C2_qD0, t_C2_qD1 = tok("qD0"), tok("qD1")

    def f(e, tws=t_wsacc[len(ws_jobs) - 1]):
        W(e, "cV", tws)
        W(e, "cS", t_expL[63])
        i = e.matmul(psA[0:1, 0:48], ones_f[:, 0:1], wstat[:, 0:48],
                     start=True, stop=True)
        inc(i, "cT")
    on_ten(f)
    t_wsmm = tok("cT")

    def f(e, t_wsmm=t_wsmm):
        W(e, "cT", t_wsmm)
        for wi in range(3):
            e.reduce_sum(wstat[0:1, wi:wi + 1],
                         psA[0:1, 16 * wi:16 * (wi + 1)], axis=AX)
        e.drain()
        i = e.tensor_scalar(wstat[0:1, 0:3], wstat[0:1, 0:3],
                            1.0 / (D * DI), 1e-5, Alu.mult, Alu.max)
        inc(i, "cV")
    on_vec(f)
    t_wmean = tok("cV")

    def f(e, t_wmean=t_wmean):
        W(e, "cV", t_wmean)
        i = e.matmul(psA[:, 512:515], ones_f[0:1, :], wstat[0:1, 0:3],
                     start=True, stop=True)
        inc(i, "cT")
    on_ten(f)
    t_wbc = tok("cT")

    def f(e, t_wbc=t_wbc):
        W(e, "cT", t_wbc)
        e.tensor_copy(wsc[:, 3:6], psA[:, 512:515])
        i = e.reciprocal(wsc[:, 0:3], psA[:, 512:515])
        inc(i, "cV")
    on_vec(f)
    t_wsc = tok("cV")

    flush()

    t_av_last = t_avL[63]
    t_on_last = t_on[63]
    t_exp_last = t_expL[63]

    def f(e):
        W(e, "qB0", t_xowb[7])
        W(e, "cT", t_av_last)
        W(e, "cV", t_vall)
        for c in range(8):
            W(e, "qX0", cnt["qX0"])
            inc(e.dma_start(s2[:, OWN * c:OWN * (c + 1)],
                            xod[128 * c:128 * (c + 1), :]), "qX0")
    on_sync(f)
    t_xo_ld = tok("qX0")

    t_d1ev = {}
    nd1 = 0
    for oc in range(8):
        for tg in range(4):
            bk = nd1 % 4

            def f(e, oc=oc, tg=tg, bk=bk, nd1=nd1):
                if nd1 == 0:
                    W(e, "qW1", t_owb[0])
                    W(e, "qE0", t_attn_d0)
                    W(e, "qE1", t_attn_d1)
                    W(e, "cS", t_exp_last)
                    W(e, "cV", t_wsc)
                if nd1 >= 4:
                    W(e, "cV", t_d1ev[nd1 - 4])
                for kc in range(8):
                    i = e.matmul(
                        psA[:, 512 * bk:512 * (bk + 1)],
                        owb[:, D * kc + 128 * oc:D * kc + 128 * (oc + 1)],
                        obuf[:, OWN * kc + 512 * tg:OWN * kc + 512 * (tg + 1)],
                        start=(kc == 0), stop=(kc == 7))
                inc(i, "cT")
            on_ten(f)
            t_mm = tok("cT")

            def f(e, oc=oc, tg=tg, bk=bk, t_mm=t_mm, nd1=nd1):
                W(e, "cT", t_mm)
                if nd1 == 0:
                    W(e, "qX0", t_xo_ld)
                sl_ = s2[:, OWN * oc + 512 * tg:OWN * oc + 512 * (tg + 1)]
                i = e.scalar_tensor_tensor(sl_, psA[:, 512 * bk:512 * (bk + 1)],
                                           obc[:, oc:oc + 1], sl_,
                                           Alu.add, Alu.add)
                inc(i, "cV")
            on_vec(f)
            t_d1ev[nd1] = tok("cV")
            nd1 += 1
    t_s2 = tok("cV")
    t_s2mm_last = tok("cT")

    t_d1b = {}
    nb1 = 0
    for tt in range(16):
        for og in range(2):
            ps = psC if nb1 % 2 == 0 else psD

            def f(e, tt=tt, og=og, ps=ps, nb1=nb1):
                if nb1 == 0:
                    W(e, "cV", t_on_last)
                    W(e, "cV", t_wsc)
                if nb1 >= 2:
                    W(e, "cV", t_d1b[nb1 - 2])
                for kc in range(8):
                    e.matmul(
                        ps[:, 0:512],
                        obuf[:, OWN * kc + 128 * tt:OWN * kc + 128 * (tt + 1)],
                        owb[:, D * kc + 512 * og:D * kc + 512 * (og + 1)],
                        start=(kc == 0), stop=False)
                i = e.matmul(ps[:, 0:512], ones_b[0:1, :],
                             obb2_b[0:1, 512 * og:512 * (og + 1)],
                             start=False, stop=True)
                inc(i, "cT")
            on_ten(f)
            t_mm = tok("cT")

            def f(e, tt=tt, og=og, ps=ps, t_mm=t_mm, nb1=nb1):
                W(e, "cT", t_mm)
                if nb1 == 0:
                    W(e, "cS", t_C2_cS)
                    W(e, "qD0", t_C2_qD0)
                    W(e, "qD1", t_C2_qD1)
                i = e.tensor_copy(
                    bo[:, D * tt + 512 * og:D * tt + 512 * (og + 1)],
                    ps[:, 0:512])
                inc(i, "cV")
            on_vec(f)
            t_d1b[nb1] = tok("cV")
            nb1 += 1
    t_bo1 = tok("cV")
    t_bo1_mm = tok("cT")

    flush()

    t_sq2, t_ms2, t_ms2mm = {}, {}, {}
    nms = 0
    for oc in range(8):
        def f(e, oc=oc):
            if oc == 0:
                W(e, "cV", t_s2)
                W(e, "cT", t_av_last)
            else:
                W(e, "cT", t_ms2mm[oc - 1])
            i = e.activation(sq2s[:], s2[:, OWN * oc:OWN * (oc + 1)],
                             Act.Square)
            inc(i, "cS")
        on_scal(f)
        t_sq2[oc] = tok("cS")

        for g in range(4):
            def f(e, oc=oc, g=g, ts=t_sq2[oc], nms=nms):
                if g == 0:
                    W(e, "cS", ts)
                if nms == 0:
                    W(e, "cV", t_bo1)
                if nms >= 2:
                    W(e, "cV", t_ms2[nms - 2])
                i = e.matmul(psC[0:1, 512 * (nms % 2):512 * (nms % 2) + 512],
                             ones_b[:, 0:1], sq2s[:, 512 * g:512 * (g + 1)],
                             start=True, stop=True)
                inc(i, "cT")
            on_ten(f)
            t_pe = tok("cT")

            def f(e, oc=oc, g=g, t_pe=t_pe, nms=nms):
                W(e, "cT", t_pe)
                if oc == 0:
                    i = e.tensor_copy(r2row[0:1, 512 * g:512 * (g + 1)],
                                      psC[0:1, 512 * (nms % 2):512 * (nms % 2) + 512])
                else:
                    i = e.tensor_add(r2row[0:1, 512 * g:512 * (g + 1)],
                                     r2row[0:1, 512 * g:512 * (g + 1)],
                                     psC[0:1, 512 * (nms % 2):512 * (nms % 2) + 512])
                inc(i, "cV")
            on_vec(f)
            t_ms2[nms] = tok("cV")
            nms += 1
        t_ms2mm[oc] = tok("cT")

    def f(e, tm=t_ms2[nms - 1]):
        W(e, "cV", tm)
        i = e.activation(r2row[:], r2row[:], Act.Sqrt, bias=1e-6,
                         scale=1.0 / D)
        inc(i, "cS")
    on_scal(f)
    t_r2s = tok("cS")

    def f(e, t_r2s=t_r2s):
        W(e, "cS", t_r2s)
        i = e.reciprocal(r2row[:], r2row[:])
        inc(i, "cV")
    on_vec(f)
    t_r2 = tok("cV")

    t_r2c = {}
    for g in range(4):
        def f(e, g=g, t_r2=t_r2):
            if g == 0:
                W(e, "cV", t_r2)
                W(e, "cV", t_s2)
            i = e.matmul(psA[:, 512 * g:512 * (g + 1)], ones_f[0:1, :],
                         r2row[0:1, 512 * g:512 * (g + 1)],
                         start=True, stop=True)
            inc(i, "cT")
        on_ten(f)
        t_pe = tok("cT")

        def f(e, g=g, t_pe=t_pe):
            W(e, "cT", t_pe)
            i = e.tensor_copy(r2b[:, 512 * g:512 * (g + 1)],
                              psA[:, 512 * g:512 * (g + 1)])
            inc(i, "cV")
        on_vec(f)
        t_r2c[g] = tok("cV")

    t_n2wb = {}
    for oc in range(8):
        def f(e, oc=oc):
            if oc == 0:
                W(e, "cV", t_r2c[3])
            i = e.scalar_tensor_tensor(s2[:, OWN * oc:OWN * (oc + 1)],
                                       s2[:, OWN * oc:OWN * (oc + 1)],
                                       ln2c[:, oc:oc + 1], r2b[:],
                                       Alu.mult, Alu.mult)
            inc(i, "cV")
        on_vec(f)
        t_n2 = tok("cV")

        def f(e, oc=oc, t_n2=t_n2):
            W(e, "cV", t_n2)
            W(e, "qZ0", cnt["qZ0"])
            i = e.dma_start(n2d[128 * oc:128 * (oc + 1), :],
                            s2[:, OWN * oc:OWN * (oc + 1)])
            inc(i, "qZ0")
        on_gp(f)
        t_n2wb[oc] = tok("qZ0")
    t_n2all = tok("qZ0")

    t_D2_cS = tok("cS")

    def f(e, t_mpb=t_mpb):
        W(e, "cV", t_mpb)
        W(e, "cT", t_av_last)
        W(e, "cS", t_D2_cS)
        for n_ in range(8):
            for m_ in range(8):
                i = e.tensor_scalar(
                    bigm[:, D * n_ + 128 * m_:D * n_ + 128 * (m_ + 1)],
                    id128[:], mpb[:, 8 * n_ + m_:8 * n_ + m_ + 1],
                    None, Alu.mult)
        inc(i, "cV")
    on_vec(f)
    t_bigm = tok("cV")

    flush()

    wq_jobs = []
    for wi, wn in enumerate(["g", "u", "d"]):
        if wn != "d":
            for ch in range(8):
                for qf in range(4):
                    wq_jobs.append((wn, wi, ch, qf, 1024))
        else:
            for ch in range(32):
                wq_jobs.append((wn, wi, ch, 0, 1024))
    nqw_box = [0]
    t_wqv, t_wqwb, t_wqld = {}, {}, {}

    def _wq_load(j):
        if j >= len(wq_jobs) or j in t_wqld:
            return
        wn, wi, ch, qf, wd = wq_jobs[j]

        def f(e, wn=wn, ch=ch, qf=qf, wd=wd, j=j):
            if j == 0:
                W(e, "cT", t_bo1_mm)
                W(e, "qW0", t_wv)
                W(e, "qW1", t_owb[0])
            if j >= 2:
                W(e, "cV", t_wqv[j - 2])
            W(e, "qF1", cnt["qF1"])
            i = e.dma_start(wqc2[j % 2][:, 0:wd],
                            wfd[wn][128 * ch:128 * (ch + 1),
                                    1024 * qf:1024 * qf + wd])
            inc(i, "qF1")
        on_gp(f)
        t_wqld[j] = tok("qF1")

    def emit_wq_jobs(kjobs):
        for _ in range(kjobs):
            if nqw_box[0] >= len(wq_jobs):
                return
            nqw = nqw_box[0]
            if nqw == 0:
                _wq_load(0)
            _wq_load(nqw + 1)
            wn, wi, ch, qf, wd = wq_jobs[nqw]

            def f(e, wi=wi, wd=wd, nqw=nqw):
                W(e, "qF1", t_wqld[nqw])
                if nqw == 0:
                    W(e, "cV", t_wsc)
                if nqw >= 1:
                    W(e, "qZ0" if (nqw - 1) % 2 == 0 else "qZ1",
                      t_wqwb[nqw - 1])
                wqs = wqc2[nqw % 2]
                e.tensor_scalar(wqs[:, 0:wd], wqs[:, 0:wd],
                                wsc[:, wi:wi + 1], MAGIC, Alu.mult, Alu.add)
                e.drain()
                e.tensor_scalar(wqs[:, 0:wd], wqs[:, 0:wd],
                                MAGIC, -1.0, Alu.subtract, Alu.max)
                e.drain()
                i = e.tensor_scalar(wqo[:, 0:wd], wqs[:, 0:wd],
                                    1.0, None, Alu.min)
                inc(i, "cV")
            on_vec(f)
            t_wqv[nqw] = tok("cV")

            def f(e, wn=wn, ch=ch, qf=qf, wd=wd, tv=t_wqv[nqw], nqw=nqw):
                W(e, "cV", tv)
                qs = "qZ0" if nqw % 2 == 0 else "qZ1"
                if nqw == 0:
                    W(e, "qZ0", t_n2all)
                W(e, qs, cnt[qs])
                i = e.dma_start(wqd[wn][128 * ch:128 * (ch + 1),
                                        1024 * qf:1024 * qf + wd],
                                wqo[:, 0:wd])
                inc(i, qs)
            on_gp(f)
            t_wqwb[nqw] = tok("qZ0" if nqw % 2 == 0 else "qZ1")
            nqw_box[0] += 1

    t_w1mm_last = {0: 0, 1: 0}
    t_w2mm_last = {0: 0, 1: 0}
    t_gelu = {}
    t_boev = {}
    t_n2ld, t_xold = {}, {}
    t_w1ld, t_w2ld = {}, {}
    nbo = 0
    ngel = 0
    for half in range(2):
        def f(e, half=half):
            if half == 0:
                W(e, "cT", t_bo1_mm)
                W(e, "qZ0", t_n2all)
                W(e, "cV", t_bigm)
                W(e, "qE0", t_attn_d0)
                W(e, "qE1", t_attn_d1)
            else:
                W(e, "cT", t_w1mm_last[0])
                W(e, "cT", t_w2mm_last[0])
            for kc in range(8):
                W(e, "qW0", cnt["qW0"])
                inc(e.dma_start(w1h[:, 2048 * kc:2048 * (kc + 1)],
                                w1Td[128 * kc:128 * (kc + 1),
                                     2048 * half:2048 * (half + 1)]), "qW0")
            for j in range(16):
                W(e, "qW1", cnt["qW1"])
                inc(e.dma_start(w2h[:, D * j:D * (j + 1)],
                                w2Td[2048 * half + 128 * j:
                                     2048 * half + 128 * (j + 1), :]),
                    "qW1")
        on_gp(f)
        t_w1ld[half] = tok("qW0")
        t_w2ld[half] = tok("qW1")

        for g in range(8):
            gi = half * 8 + g
            sl = g % 2

            def f(e, half=half, g=g, gi=gi, sl=sl):
                if gi == 0:
                    W(e, "qZ0", t_n2all)
                    W(e, "cT", t_av_last)
                    W(e, "cV", t_wsacc[47])
                if gi >= 2:
                    W(e, "cT", t_w1mm_last_by_g[gi - 2])
                W(e, "qX1", cnt["qX1"])
                i = e.dma_start(
                    n2s[sl][:, :].rearrange("p (c j) -> p c j", j=256),
                    bass.AP(n2d, 256 * g,
                            [[OWN, 128], [128 * OWN, 8], [1, 256]]))
                inc(i, "qX1")
                if half == 1:
                    if gi == 8:
                        W(e, "qW0", t_wv)
                        W(e, "cS", t_qkev[95])
                    if gi >= 2:
                        W(e, "cT", t_mhc_last_by_g.get(gi - 2, 0))
                    W(e, "qX0", cnt["qX0"])
                    i = e.dma_start(
                        xosr[sl][:, :].rearrange("p (c j) -> p c j", j=256),
                        bass.AP(xod, 256 * g,
                                [[OWN, 128], [128 * OWN, 8], [1, 256]]))
                    inc(i, "qX0")
            if half == 0 and g == 0:
                t_w1mm_last_by_g = {}
                t_mhc_last_by_g = {}
            on_sync(f)
            t_n2ld[gi] = tok("qX1")
            t_xold[gi] = tok("qX0")

            for hc in range(16):
                bk = (gi * 16 + hc) % 4

                def f(e, half=half, g=g, gi=gi, sl=sl, hc=hc, bk=bk):
                    if hc == 0:
                        W(e, "qW0", t_w1ld[half])
                        W(e, "qX1", t_n2ld[gi])
                        if gi == 0:
                            W(e, "cV", t_r2c[3])
                    nglob = gi * 16 + hc
                    if nglob >= 4:
                        W(e, "cS", t_gelu[nglob - 4])
                    for kc in range(8):
                        i = e.matmul(
                            psA[:, 512 * bk:512 * bk + 256],
                            w1h[:, 2048 * kc + 128 * hc:
                                 2048 * kc + 128 * (hc + 1)],
                            n2s[sl][:, 256 * kc:256 * (kc + 1)],
                            start=(kc == 0), stop=(kc == 7))
                    inc(i, "cT")
                on_ten(f)
                t_w1mm = tok("cT")
                t_w1mm_last[half] = t_w1mm
                t_w1mm_last_by_g[gi] = t_w1mm

                def f(e, half=half, gi=gi, hc=hc, bk=bk, t_w1mm=t_w1mm,
                      ngel=ngel):
                    W(e, "cT", t_w1mm)
                    if ngel == 0:
                        W(e, "cV", t_bo1)
                    if ngel >= 16:
                        W(e, "cT", t_w2mm_by_g.get(gi - 1, 0))
                    i = e.activation(hth[:, 256 * hc:256 * (hc + 1)],
                                     psA[:, 512 * bk:512 * bk + 256],
                                     Act.Gelu,
                                     bias=b1c[:, 16 * half + hc:
                                              16 * half + hc + 1])
                    inc(i, "cS")
                if half == 0 and g == 0 and hc == 0:
                    t_w2mm_by_g = {}
                on_scal(f)
                t_gelu[gi * 16 + hc] = tok("cS")
                ngel += 1

            for tt2 in range(2):
                for og in range(2):
                    psel = nbo % 4
                    ps = (psC if psel < 2 else psD)
                    pcol = 512 * (psel % 2)

                    def f(e, half=half, g=g, gi=gi, sl=sl, tt2=tt2, og=og,
                          ps=ps, pcol=pcol, nbo=nbo):
                        if tt2 == 0 and og == 0:
                            W(e, "cS", t_gelu[gi * 16 + 15])
                            W(e, "qW1", t_w2ld[half])
                            if half == 1:
                                W(e, "qX0", t_xold[gi])
                            if gi == 0:
                                W(e, "cV", t_bo1)
                        if nbo >= 4:
                            W(e, "cV", t_boev[nbo - 4])
                        for kc2 in range(16):
                            i = e.matmul(
                                ps[:, pcol:pcol + 512],
                                hth[:, 256 * kc2 + 128 * tt2:
                                     256 * kc2 + 128 * (tt2 + 1)],
                                w2h[:, D * kc2 + 512 * og:
                                     D * kc2 + 512 * (og + 1)],
                                start=(kc2 == 0),
                                stop=(half == 0 and kc2 == 15))
                        if half == 1:
                            for kc3 in range(8):
                                i = e.matmul(
                                    ps[:, pcol:pcol + 512],
                                    xosr[sl][:, 256 * kc3 + 128 * tt2:
                                             256 * kc3 + 128 * (tt2 + 1)],
                                    bigm[:, D * kc3 + 512 * og:
                                         D * kc3 + 512 * (og + 1)],
                                    start=False, stop=(kc3 == 7))
                        inc(i, "cT")
                    on_ten(f)
                    t_w2mm = tok("cT")
                    t_w2mm_last[half] = t_w2mm
                    t_w2mm_by_g[gi] = t_w2mm
                    if half == 1:
                        t_mhc_last_by_g[gi] = t_w2mm

                    def f(e, half=half, g=g, tt2=tt2, og=og, ps=ps,
                          pcol=pcol, t_w2mm=t_w2mm):
                        W(e, "cT", t_w2mm)
                        tt = 2 * g + tt2
                        dst = bo[:, D * tt + 512 * og:D * tt + 512 * (og + 1)]
                        if half == 0:
                            i = e.tensor_add(dst, dst, ps[:, pcol:pcol + 512])
                        else:
                            i = e.tensor_add(dst, dst, ps[:, pcol:pcol + 512])
                        inc(i, "cV")
                    on_vec(f)
                    t_boev[nbo] = tok("cV")
                    nbo += 1
            emit_wq_jobs(6)
    t_d3 = tok("cV")
    t_d3mm = tok("cT")

    flush()

    t_x1wb, t_z2wb = {}, {}
    t_e1v = {}
    for tt in range(16):
        sl = tt % 2
        qx = "qX0" if sl == 0 else "qX1"

        def f(e, tt=tt, sl=sl, qx=qx):
            if tt == 0:
                W(e, "cV", t_d3)
                W(e, "cT", t_d3mm)
                W(e, "qW0", t_w1ld[1])
            if tt >= 2:
                W(e, "cV", t_e1v[tt - 2])
            W(e, qx, cnt[qx])
            i = e.dma_start(x0o[sl][:],
                            x0ownd[128 * tt:128 * (tt + 1), :])
            inc(i, qx)
        on_sync(f)
        t_x0ld = tok(qx)

        def f(e, tt=tt, sl=sl):
            if tt == 0:
                W(e, "cV", t_d3)
            bos = bo[:, D * tt:D * (tt + 1)]
            i = e.activation(zt[sl][:], bos, Act.Square,
                             accum_out=stq[:, sl:sl + 1])
            inc(i, "cS")
        on_scal(f)
        t_sq = tok("cS")

        def f(e, tt=tt, sl=sl, t_sq=t_sq, t_x0ld=t_x0ld, qx=qx):
            bos = bo[:, D * tt:D * (tt + 1)]
            e.reduce_sum(stm[:, sl:sl + 1], bos, axis=AX)
            e.drain()
            e.tensor_scalar(stm[:, sl:sl + 1], stm[:, sl:sl + 1], 1.0 / D,
                            None, Alu.mult)
            e.drain()
            e.tensor_mul(stq2[:, sl:sl + 1], stm[:, sl:sl + 1],
                         stm[:, sl:sl + 1])
            e.drain()
            W(e, "cS", t_sq)
            i = e.scalar_tensor_tensor(stv[:, sl:sl + 1], stq[:, sl:sl + 1],
                                       1.0 / D, stq2[:, sl:sl + 1],
                                       Alu.mult, Alu.subtract)
            inc(i, "cV")
        on_vec(f)
        t_var = tok("cV")

        def f(e, sl=sl, t_var=t_var):
            W(e, "cV", t_var)
            i = e.activation(stv[:, sl:sl + 1], stv[:, sl:sl + 1],
                             Act.Sqrt, bias=1e-5)
            inc(i, "cS")
        on_scal(f)
        t_r = tok("cS")

        def f(e, tt=tt, sl=sl, t_r=t_r, t_x0ld=t_x0ld, qx=qx):
            W(e, "cS", t_r)
            W(e, qx, t_x0ld)
            if tt >= 2:
                W(e, "qZ0" if sl == 0 else "qZ1", t_x1wb[tt - 2])
            e.reciprocal(stv[:, sl:sl + 1], stv[:, sl:sl + 1])
            e.drain()
            bos = bo[:, D * tt:D * (tt + 1)]
            e.tensor_scalar(x1t[sl][:], bos, stm[:, sl:sl + 1],
                            stv[:, sl:sl + 1], Alu.subtract, Alu.mult)
            e.drain()
            i = e.tensor_add(x1t[sl][:], x1t[sl][:], x0o[sl][:])
            inc(i, "cV")
        on_vec(f)
        t_x1 = tok("cV")

        def f(e, tt=tt, sl=sl, t_x1=t_x1):
            W(e, "cV", t_x1)
            qs = "qZ0" if sl == 0 else "qZ1"
            if tt == 0:
                W(e, "qZ0", t_wqwb[len(wq_jobs) - 2])
            if tt == 1:
                W(e, "qZ1", t_wqwb[len(wq_jobs) - 1])
            W(e, qs, cnt[qs])
            i = e.dma_start(x1d[128 * tt:128 * (tt + 1), :], x1t[sl][:])
            inc(i, qs)
        on_gp(f)
        t_x1wb[tt] = tok("qZ0" if sl == 0 else "qZ1")

        def f(e, tt=tt, sl=sl, t_x1=t_x1):
            W(e, "cV", t_x1)
            if tt >= 2:
                W(e, "cV", t_e1v[tt - 2])
            i = e.activation(zt[sl][:], x1t[sl][:], Act.Square,
                             accum_out=stq[:, sl:sl + 1])
            inc(i, "cS")
        on_scal(f)
        t_sqz = tok("cS")

        def f(e, sl=sl, t_sqz=t_sqz):
            e.reduce_sum(stm[:, sl:sl + 1], x1t[sl][:], axis=AX)
            e.drain()
            e.tensor_scalar(stm[:, sl:sl + 1], stm[:, sl:sl + 1], 1.0 / D,
                            None, Alu.mult)
            e.drain()
            e.tensor_mul(stq2[:, sl:sl + 1], stm[:, sl:sl + 1],
                         stm[:, sl:sl + 1])
            e.drain()
            W(e, "cS", t_sqz)
            i = e.scalar_tensor_tensor(stv[:, sl:sl + 1], stq[:, sl:sl + 1],
                                       1.0 / D, stq2[:, sl:sl + 1],
                                       Alu.mult, Alu.subtract)
            inc(i, "cV")
        on_vec(f)
        t_varz = tok("cV")

        def f(e, sl=sl, t_varz=t_varz):
            W(e, "cV", t_varz)
            i = e.activation(stv[:, sl:sl + 1], stv[:, sl:sl + 1],
                             Act.Sqrt, bias=1e-5)
            inc(i, "cS")
        on_scal(f)
        t_rz = tok("cS")

        def f(e, tt=tt, sl=sl, t_rz=t_rz):
            W(e, "cS", t_rz)
            if tt >= 2:
                W(e, "qV0" if sl == 0 else "qV1", t_z2wb[tt - 2])
            e.reciprocal(stv[:, sl:sl + 1], stv[:, sl:sl + 1])
            e.drain()
            i = e.tensor_scalar(zt[sl][:], x1t[sl][:], stm[:, sl:sl + 1],
                                stv[:, sl:sl + 1], Alu.subtract, Alu.mult)
            e.drain()
            i = e.tensor_reduce(stq[:, sl:sl + 1], zt[sl][:], AX,
                                Alu.max, apply_absolute_value=True)
            e.drain()
            e.tensor_scalar(stq[:, sl:sl + 1], stq[:, sl:sl + 1], 1e-5,
                            1.0 / 127.0, Alu.max, Alu.mult)
            e.drain()
            e.tensor_copy(zscin[:, tt:tt + 1], stq[:, sl:sl + 1])
            e.reciprocal(stq2[:, sl:sl + 1], stq[:, sl:sl + 1])
            e.drain()
            e.tensor_scalar(zt[sl][:], zt[sl][:], stq2[:, sl:sl + 1],
                            MAGIC, Alu.mult, Alu.add)
            e.drain()
            e.tensor_scalar(zt[sl][:], zt[sl][:], MAGIC, -128.0,
                            Alu.subtract, Alu.max)
            e.drain()
            i = e.tensor_scalar(z2qs[sl][:], zt[sl][:], 127.0, None,
                                Alu.min)
            inc(i, "cV")
        on_vec(f)
        t_e1v[tt] = tok("cV")

        def f(e, tt=tt, sl=sl, tv=t_e1v[tt]):
            W(e, "cV", tv)
            qs = "qV0" if sl == 0 else "qV1"
            W(e, qs, cnt[qs])
            i = e.dma_start(z1d[128 * tt:128 * (tt + 1), :], z2qs[sl][:])
            inc(i, qs)
        on_gp(f)
        t_z2wb[tt] = tok("qV0" if sl == 0 else "qV1")
    t_e1 = tok("cV")
    t_z2wb_all = (tok("qV0"), tok("qV1"))
    t_x1wb_all = (tok("qZ0"), tok("qZ1"))
    t_e1_cS = tok("cS")
    t_e1_qX0, t_e1_qX1 = tok("qX0"), tok("qX1")

    flush()

    def f(e):
        W(e, "qV0", t_z2wb[6])
        W(e, "qV1", t_z2wb[7])
        W(e, "qZ0", t_n2all)
        W(e, "cT", t_d3mm)
        W(e, "qW1", t_w2ld[1])
        W(e, "qX1", t_n2ld[15])
        W(e, "cS", t_gelu[255])
        for c in range(8):
            W(e, "qX0", cnt["qX0"])
            inc(e.dma_start(z2qT[:, OWN * c:OWN * c + 1024],
                            z1d[0:1024, 128 * c:128 * (c + 1)],
                            transpose=True), "qX0")
    on_sync(f)
    t_z2qT = tok("qX0")

    def f(e):
        W(e, "qV0", t_z2wb_all[0])
        W(e, "qV1", t_z2wb_all[1])
        for c in range(8):
            W(e, "qX0", cnt["qX0"])
            inc(e.dma_start(z2qT[:, OWN * c + 1024:OWN * (c + 1)],
                            z1d[1024:OWN, 128 * c:128 * (c + 1)],
                            transpose=True), "qX0")
    on_sync(f)
    t_z2qT_h1 = tok("qX0")

    def f(e):
        W(e, "cV", t_e1)
        W(e, "cV", t_wsc)
        e.tensor_scalar(invgc[:], zscin[:], wsc[:, 3:4], None, Alu.mult)
        i = e.tensor_scalar(invuc[:], zscin[:], wsc[:, 4:5], None, Alu.mult)
        inc(i, "cV")
    on_vec(f)
    t_invg = tok("cV")

    t_wgld, t_ev3, t_pwb = {}, {}, {}
    ne3 = 0
    for ig in range(8):
        sl = ig % 2

        def f(e, ig=ig, sl=sl):
            if ig == 0:
                W(e, "qZ0", t_wqwb[len(wq_jobs) - 2])
                W(e, "qZ1", t_wqwb[len(wq_jobs) - 1])
                W(e, "cT", t_d3mm)
                W(e, "cV", t_d3)
                W(e, "cS", t_gelu[255])
                W(e, "qX1", t_n2ld[15])
            if ig >= 2:
                W(e, "cT", t_e3mm_by_ig[ig - 2])
            W(e, "qW0", cnt["qW0"])
            i = e.dma_start(
                wgs[sl][:, :].rearrange("p (c j) -> p c j", j=512),
                bass.AP(wqd["g"], 512 * ig,
                        [[DI, 128], [128 * DI, 8], [1, 512]]))
            inc(i, "qW0")
            W(e, "qW1", cnt["qW1"])
            i = e.dma_start(
                wus[sl][:, :].rearrange("p (c j) -> p c j", j=512),
                bass.AP(wqd["u"], 512 * ig,
                        [[DI, 128], [128 * DI, 8], [1, 512]]))
            inc(i, "qW1")
        if ig == 0:
            t_e3mm_by_ig = {}
        on_gp(f)
        t_wgld[ig] = (tok("qW0"), tok("qW1"))

        for tt in range(16):
            pse = psC if ne3 % 2 == 0 else psD

            def f(e, ig=ig, sl=sl, tt=tt, ne3=ne3, pse=pse):
                if tt == 0:
                    W(e, "qW0", t_wgld[ig][0])
                    W(e, "qW1", t_wgld[ig][1])
                    if ig == 0:
                        W(e, "qX0", t_z2qT)
                if tt == 8 and ig == 0:
                    W(e, "qX0", t_z2qT_h1)
                if ne3 >= 2:
                    W(e, "cV", t_ev3[ne3 - 2])
                for kc in range(8):
                    e.matmul(pse[:, 0:512],
                             z2qT[:, OWN * kc + 128 * tt:
                                  OWN * kc + 128 * (tt + 1)],
                             wgs[sl][:, 512 * kc:512 * (kc + 1)],
                             start=(kc == 0), stop=(kc == 7))
                for kc in range(8):
                    i = e.matmul(pse[:, 512:1024],
                                 z2qT[:, OWN * kc + 128 * tt:
                                      OWN * kc + 128 * (tt + 1)],
                                 wus[sl][:, 512 * kc:512 * (kc + 1)],
                                 start=(kc == 0), stop=(kc == 7))
                inc(i, "cT")
            on_ten(f)
            t_mm = tok("cT")
            t_e3mm_by_ig[ig] = t_mm

            def f(e, tt=tt, t_mm=t_mm, pse=pse, ig=ig, ne3=ne3):
                W(e, "cT", t_mm)
                if ig == 0 and tt == 0:
                    W(e, "cV", t_invg)
                if ne3 >= 1:
                    W(e, "cV", t_ev3[ne3 - 1])
                i = e.activation(gsb[:], pse[:, 0:512], Act.Silu,
                                 scale=invgc[:, tt:tt + 1])
                inc(i, "cS")
            on_scal(f)
            t_silu = tok("cS")

            def f(e, ig=ig, tt=tt, ne3=ne3, t_silu=t_silu, pse=pse):
                W(e, "cS", t_silu)
                if ne3 >= 2:
                    W(e, "qZ0" if ne3 % 2 == 0 else "qZ1", t_pwb[ne3 - 2])
                sl2 = ne3 % 2
                e.scalar_tensor_tensor(psb[sl2][:], pse[:, 512:1024],
                                       invuc[:, tt:tt + 1], gsb[:],
                                       Alu.mult, Alu.mult)
                e.drain()
                col = 16 * ig + tt
                i = e.tensor_reduce(pmxall[:, col:col + 1], psb[sl2][:], AX,
                                    Alu.max, apply_absolute_value=True)
                inc(i, "cV")
            on_vec(f)
            t_ev3[ne3] = tok("cV")

            def f(e, ig=ig, tt=tt, ne3=ne3, tv=t_ev3[ne3]):
                W(e, "cV", tv)
                qs = "qZ0" if ne3 % 2 == 0 else "qZ1"
                if ne3 == 0:
                    W(e, "qZ0", t_x1wb_all[0])
                if ne3 == 1:
                    W(e, "qZ1", t_x1wb_all[1])
                W(e, qs, cnt[qs])
                i = e.dma_start(pdd[128 * tt:128 * (tt + 1),
                                    512 * ig:512 * (ig + 1)],
                                psb[ne3 % 2][:])
                inc(i, qs)
            on_gp(f)
            t_pwb[ne3] = tok("qZ0" if ne3 % 2 == 0 else "qZ1")
            ne3 += 1
    t_e3 = tok("cV")
    t_e3_cS = tok("cS")

    flush()

    def f(e, t_e3=t_e3):
        W(e, "cV", t_e3)
        vw = pmxall[:, :].rearrange("p (i t) -> p t i", t=16)
        e.tensor_reduce(pscm[:], vw, AX, Alu.max)
        e.drain()
        e.tensor_scalar(pscm[:], pscm[:], 1e-5, 1.0 / 127.0, Alu.max,
                        Alu.mult)
        e.drain()
        e.tensor_scalar(psci[:], pscm[:], wsc[:, 5:6], None, Alu.mult)
        e.drain()
        i = e.reciprocal(pscm[:], pscm[:])
        inc(i, "cV")
    on_vec(f)
    t_psc = tok("cV")

    t_pld, t_pqv, t_pqwb = {}, {}, {}
    for tt in range(16):
        def f(e, tt=tt):
            if tt == 0:
                W(e, "qZ0", t_pwb[126])
                W(e, "qZ1", t_pwb[127])
            if tt >= 2:
                W(e, "cV", t_pqv[tt - 2])
            W(e, "qX1", cnt["qX1"])
            i = e.dma_start(ptile2[tt % 2][:], pdd[128 * tt:128 * (tt + 1), :])
            inc(i, "qX1")
        on_sync(f)
        t_pld[tt] = tok("qX1")

        def f(e, tt=tt, tl=t_pld[tt]):
            W(e, "qX1", tl)
            if tt == 0:
                W(e, "cV", t_psc)
            if tt >= 2:
                W(e, "qZ1", t_pqwb[tt - 2])
            e.tensor_scalar(pwide[:], ptile2[tt % 2][:], pscm[:, tt:tt + 1],
                            MAGIC, Alu.mult, Alu.add)
            e.drain()
            e.tensor_scalar(pwide[:], pwide[:], MAGIC, -128.0,
                            Alu.subtract, Alu.max)
            e.drain()
            i = e.tensor_scalar(pqtile2[tt % 2][:], pwide[:], 127.0, None,
                                Alu.min)
            inc(i, "cV")
        on_vec(f)
        t_pqv[tt] = tok("cV")

        def f(e, tt=tt, tv=t_pqv[tt]):
            W(e, "cV", tv)
            W(e, "qZ1", cnt["qZ1"])
            i = e.dma_start(pqd[128 * tt:128 * (tt + 1), :], pqtile2[tt % 2][:])
            inc(i, "qZ1")
        on_gp(f)
        t_pqwb[tt] = tok("qZ1")
    t_pq_all = tok("qZ1")

    flush()

    t_pqt, t_wdld, t_fev = {}, {}, {}
    nf = 0
    for icg in range(4):
        def f(e, icg=icg):
            if icg == 0:
                W(e, "qZ1", t_pqwb[7])
                W(e, "cV", t_e3)
                W(e, "cT", t_e3mm_by_ig[7])
                W(e, "cS", t_e3_cS)
            if icg >= 1:
                W(e, "cT", t_e5mm_by_icg[icg - 1])
            for ic in range(8):
                W(e, "qX0", cnt["qX0"])
                inc(e.dma_start(pqt[:, 2048 * ic:2048 * ic + 1024],
                                pqd[0:1024, 128 * (8 * icg + ic):
                                    128 * (8 * icg + ic + 1)],
                                transpose=True), "qX0")
        if icg == 0:
            t_e5mm_by_icg = {}
        on_sync(f)
        t_pqt[icg] = tok("qX0")

        def f(e, icg=icg):
            if icg == 0:
                W(e, "qZ1", t_pq_all)
            for ic in range(8):
                W(e, "qX0", cnt["qX0"])
                inc(e.dma_start(pqt[:, 2048 * ic + 1024:2048 * (ic + 1)],
                                pqd[1024:2048, 128 * (8 * icg + ic):
                                    128 * (8 * icg + ic + 1)],
                                transpose=True), "qX0")
        on_sync(f)
        t_pqt_h1 = {} if icg == 0 else t_pqt_h1
        t_pqt_h1[icg] = tok("qX0")

        def f(e, icg=icg):
            if icg == 0:
                W(e, "cV", t_e3)
                W(e, "cT", t_e3mm_by_ig[7])
            if icg >= 1:
                W(e, "cT", t_e5mm_by_icg[icg - 1])
            W(e, "qW0", cnt["qW0"])
            i = e.dma_start(
                wds[:, :].rearrange("p (c j) -> p c j", j=D),
                bass.AP(wqd["d"], 128 * 8 * icg * D,
                        [[D, 128], [128 * D, 8], [1, D]]))
            inc(i, "qW0")
        on_gp(f)
        t_wdld[icg] = tok("qW0")

        for tt in range(16):
            psel = nf % 2
            ps = psC if psel == 0 else psD

            def f(e, icg=icg, tt=tt, ps=ps, nf=nf):
                if tt == 0:
                    W(e, "qX0", t_pqt[icg])
                    W(e, "qW0", t_wdld[icg])
                if tt == 8:
                    W(e, "qX0", t_pqt_h1[icg])
                if nf >= 2:
                    W(e, "cV", t_fev[nf - 2])
                for ic in range(8):
                    for og in range(2):
                        i = e.matmul(
                            ps[:, 512 * og:512 * (og + 1)],
                            pqt[:, 2048 * ic + 128 * tt:
                                2048 * ic + 128 * (tt + 1)],
                            wds[:, D * ic + 512 * og:D * ic + 512 * (og + 1)],
                            start=(ic == 0), stop=(ic == 7))
                inc(i, "cT")
            on_ten(f)
            t_mm = tok("cT")
            t_e5mm_by_icg[icg] = t_mm

            def f(e, icg=icg, tt=tt, ps=ps, t_mm=t_mm):
                W(e, "cT", t_mm)
                if icg == 0 and tt == 0:
                    W(e, "cS", t_e1_cS)
                    W(e, "qX0", t_e1_qX0)
                    W(e, "qX1", t_e1_qX1)
                if icg == 0 and tt == 4:
                    W(e, "qZ1", t_pq_all)
                dst = fsb[:, D * tt:D * (tt + 1)]
                if icg == 0:
                    i = e.tensor_copy(dst, ps[:, 0:1024])
                else:
                    i = e.tensor_add(dst, dst, ps[:, 0:1024])
                inc(i, "cV")
            on_vec(f)
            t_fev[nf] = tok("cV")
            nf += 1
    t_e5 = tok("cV")

    flush()

    t_x1ld, t_yv, t_ywb = {}, {}, {}
    for tt in range(16):
        sl = tt % 4
        qx = "qX1"
        qy = "qY0" if tt % 2 == 0 else "qY1"

        def f(e, tt=tt, sl=sl, qx=qx):
            if tt == 0:
                W(e, "qZ0", t_x1wb_all[0])
                W(e, "qZ1", t_x1wb_all[1])
                W(e, "cV", t_e5)
                W(e, "cT", t_e5mm_by_icg[3])
                W(e, "qW1", t_wgld[7][1])
                W(e, "qW0", t_wdld[3])
            if tt >= 4:
                W(e, "cV", t_yv[tt - 4])
            W(e, qx, cnt[qx])
            i = e.dma_start(x1r[sl][:], x1d[128 * tt:128 * (tt + 1), :])
            inc(i, qx)
        on_sync(f)
        t_x1ld[tt] = tok(qx)

        def f(e, tt=tt, sl=sl):
            fslice = fsb[:, D * tt:D * (tt + 1)]
            i = e.tensor_scalar(fslice, fslice, psci[:, tt:tt + 1], None,
                                Alu.mult, Alu.add,
                                accum_out=stm[:, sl:sl + 1])
            inc(i, "cV")
        on_vec(f)
        t_fs = tok("cV")

        def f(e, tt=tt, sl=sl, t_fs=t_fs, qy=qy):
            W(e, "cV", t_fs)
            if tt >= 4:
                W(e, qy, t_ywb[tt - 4])
            i = e.activation(otl[sl][:],
                             fsb[:, D * tt:D * (tt + 1)], Act.Square,
                             accum_out=stq[:, sl:sl + 1])
            inc(i, "cS")
        on_scal(f)
        t_sq = tok("cS")

        def f(e, sl=sl, t_sq=t_sq):
            e.tensor_scalar(stm[:, sl:sl + 1], stm[:, sl:sl + 1], 1.0 / D,
                            None, Alu.mult)
            e.drain()
            e.tensor_mul(stq2[:, sl:sl + 1], stm[:, sl:sl + 1],
                         stm[:, sl:sl + 1])
            e.drain()
            W(e, "cS", t_sq)
            i = e.scalar_tensor_tensor(stv[:, sl:sl + 1], stq[:, sl:sl + 1],
                                       1.0 / D, stq2[:, sl:sl + 1],
                                       Alu.mult, Alu.subtract)
            inc(i, "cV")
        on_vec(f)
        t_var = tok("cV")

        def f(e, sl=sl, t_var=t_var):
            W(e, "cV", t_var)
            i = e.activation(stv[:, sl:sl + 1], stv[:, sl:sl + 1],
                             Act.Sqrt, bias=1e-5)
            inc(i, "cS")
        on_scal(f)
        t_r = tok("cS")

        def f(e, tt=tt, sl=sl, t_r=t_r, tl=t_x1ld[tt], qy=qy):
            W(e, "cS", t_r)
            W(e, "qX1", tl)
            if tt >= 4:
                W(e, qy, t_ywb[tt - 4])
            e.reciprocal(stv[:, sl:sl + 1], stv[:, sl:sl + 1])
            e.drain()
            e.tensor_scalar(otl[sl][:], fsb[:, D * tt:D * (tt + 1)],
                            stm[:, sl:sl + 1], stv[:, sl:sl + 1],
                            Alu.subtract, Alu.mult)
            e.drain()
            i = e.tensor_add(otl[sl][:], otl[sl][:], x1r[sl][:])
            inc(i, "cV")
        on_vec(f)
        t_yv[tt] = tok("cV")

        def f(e, tt=tt, sl=sl, tv=t_yv[tt], qy=qy):
            W(e, "cV", tv)
            W(e, qy, cnt[qy])
            i = e.dma_start(y[128 * tt:128 * (tt + 1), :], otl[sl][:])
            inc(i, qy)
        on_sync(f)
        t_ywb[tt] = tok(qy)

    def f(e):
        W(e, "qY0", cnt["qY0"])
        W(e, "qY1", cnt["qY1"])
    on_sync(f)

    flush()
    blk.__exit__(None, None, None)
    return nc


_NC_CACHE = [None]


def kernel(**inputs):
    def g(name):
        return np.ascontiguousarray(np.asarray(inputs[name], dtype=np.float32))

    x = g("L0_attn_x")
    ipT = np.ascontiguousarray(g("L0_attn_in_proj_w").T)
    ipb = g("L0_attn_in_proj_b").reshape(1, 3 * D)
    owT = np.ascontiguousarray(g("L0_attn_out_w").T)
    obr = g("L0_attn_out_b").reshape(1, D)
    w1T = np.ascontiguousarray(g("L0_attn_w1").T)
    b1v = g("L0_attn_b1")
    w2T = np.ascontiguousarray(g("L0_attn_w2").T)
    b2r = g("L0_attn_b2").reshape(1, D)
    wgT = np.ascontiguousarray(g("L0_ffn_w_gate").T)
    wuT = np.ascontiguousarray(g("L0_ffn_w_up").T)
    wdT = np.ascontiguousarray(g("L0_ffn_w_down").T)
    ddv = g("L0_attn_delta_decay")
    ln1v = g("L0_attn_ln1_w")
    ln2v = g("L0_attn_ln2_w")
    gwT8 = np.ascontiguousarray(g("L0_attn_gate_w")[16:24].T)
    gb8 = np.ascontiguousarray(g("L0_attn_gate_b")[16:24].reshape(8, 1))
    phi = g("L0_attn_phi_res")
    id8 = np.eye(8, dtype=np.float32)

    import ml_dtypes
    BF = ml_dtypes.bfloat16
    id128 = np.eye(128, dtype=BF)
    masks = []
    for h in range(2):
        mk = np.zeros((128, 64), dtype=np.float32)
        for r in range(128):
            for q in range(64):
                if r <= 64 * h + q:
                    mk[r, q] = 1.0
        masks.append(mk.astype(BF))

    if _NC_CACHE[0] is None:
        _NC_CACHE[0] = build_nc()
    nc = _NC_CACHE[0]

    in_maps = []
    for c in range(8):
        b, h = c // 2, c % 2
        in_maps.append({
            "x0": np.ascontiguousarray(x[b]),
            "ipT": ipT, "ipb": ipb, "owT": owT, "obr": obr,
            "w1T": w1T, "b1v": b1v, "w2T": w2T, "b2r": b2r,
            "wgT": wgT, "wuT": wuT, "wdT": wdT,
            "ddv": ddv, "ln1v": ln1v, "ln2v": ln2v,
            "gwT8": gwT8, "gb8": gb8, "phi": phi, "id8d": id8,
            "hcold": np.stack([np.full(128, h, np.float32),
                               np.full(128, 1.0 - h, np.float32)],
                              axis=1),
            "x0ownd": np.ascontiguousarray(
                x[b].reshape(32, 2, 64, D)[:, h].reshape(OWN, D)),
            "maskUd": masks[h],
            "id128d": id128,
        })

    res = run_bass_kernel_spmd(nc, in_maps, core_ids=list(range(8)))
    out = np.empty((B, S, D), dtype=np.float32)
    for c in range(8):
        b, h = c // 2, c % 2
        yc = np.asarray(res.results[c]["y"])
        out[b].reshape(32, 2, 64, D)[:, h] = yc.reshape(32, 64, D)
    return out
